# version 03
# baseline (speedup 1.0000x reference)
import math
from contextlib import ExitStack

import numpy as np
import concourse.bass as bass
import concourse.mybir as mybir
from concourse.bass_utils import run_bass_kernel_spmd

F32 = mybir.dt.float32
I32 = mybir.dt.int32
AF = mybir.ActivationFunctionType
ALU = mybir.AluOpType
AX = mybir.AxisListType

S, D, NH, HD = 8192, 2048, 16, 128
TOK = 1024
DC = D // 128
TC = TOK // 128
NSLAB = S // TOK
NE, FF = 8, 2048

PI = float(np.pi)
TWOPI = 2 * math.pi
TWOPI_HI = float(np.float32(TWOPI))
TWOPI_LO = float(TWOPI - np.float64(np.float32(TWOPI)))
ISQ = 1.0 / math.sqrt(HD)
LN1E4_64 = math.log(10000.0) / 64.0
GC = 1.0 / math.e
GL = 0.5 - math.log(4.0 / 3.0)
GU = 0.5
EPS_LN = 1e-5

ENGS = ("sync", "gpsimd", "vector", "scalar", "tensor")
DEBUG = False
PHASES = 9


class B:
    def __init__(self, nc):
        self.nc = nc
        self.prog = {e: [] for e in ENGS}
        self.sems = {}
        self.cnt = {}

    def sem(self, ctx, name):
        self.sems[name] = ctx.enter_context(self.nc.semaphore(name))
        self.cnt[name] = 0

    def emit(self, eng, fn, inc=None, add=1):
        if inc is None:
            self.prog[eng].append(fn)
            return None
        sh = self.sems[inc]
        self.cnt[inc] += add
        self.prog[eng].append(lambda q, fn=fn, sh=sh, add=add: fn(q).then_inc(sh, add))
        return self.cnt[inc]

    def wait(self, eng, sem, val=None):
        if val is None:
            val = self.cnt[sem]
        if val <= 0:
            return
        sh = self.sems[sem]
        self.prog[eng].append(lambda q, sh=sh, val=val: q.wait_ge(sh, val))

    def dma(self, eng, sem, out, in_, **kw):
        self.wait(eng, sem)
        return self.emit(eng, lambda q, o=out, i=in_, kw=kw: q.dma_start(out=o, in_=i, **kw),
                         inc=sem, add=16)


def build_program():
    nc = bass.Bass("TRN2", target_bir_lowering=False)
    b = B(nc)

    x_own = nc.dram_tensor("x_own", [TOK, D], F32, kind="ExternalInput").ap()
    x_full = nc.dram_tensor("x_full", [S, D], F32, kind="ExternalInput").ap()
    Wq = nc.dram_tensor("Wq", [D, D], F32, kind="ExternalInput").ap()
    Wk = nc.dram_tensor("Wk", [D, D], F32, kind="ExternalInput").ap()
    Wv = nc.dram_tensor("Wv", [D, D], F32, kind="ExternalInput").ap()
    Wo = nc.dram_tensor("Wo", [D, D], F32, kind="ExternalInput").ap()
    bq = nc.dram_tensor("bq", [1, D], F32, kind="ExternalInput").ap()
    bk = nc.dram_tensor("bk", [1, D], F32, kind="ExternalInput").ap()
    bv = nc.dram_tensor("bv", [1, D], F32, kind="ExternalInput").ap()
    bo = nc.dram_tensor("bo", [1, D], F32, kind="ExternalInput").ap()
    Wg = nc.dram_tensor("Wg", [NE, D, FF], F32, kind="ExternalInput").ap()
    Wu = nc.dram_tensor("Wu", [NE, D, FF], F32, kind="ExternalInput").ap()
    Wd = nc.dram_tensor("Wd", [NE, FF, D], F32, kind="ExternalInput").ap()
    Wr = nc.dram_tensor("Wr", [D, NE], F32, kind="ExternalInput").ap()
    temp = nc.dram_tensor("temp", [1, 1], F32, kind="ExternalInput").ap()
    core_ofs = nc.dram_tensor("core_ofs", [1, 1], F32, kind="ExternalInput").ap()
    out_own = nc.dram_tensor("out_own", [TOK, D], F32, kind="ExternalOutput").ap()

    skind = dict(kind="ExternalOutput") if DEBUG else {}
    K_dram = nc.dram_tensor("K_scr", [D, S], F32, **skind).ap()
    V_dram = nc.dram_tensor("V_scr", [S, D], F32, **skind).ap()
    Q_dram = nc.dram_tensor("Q_scr", [D, TOK], F32, **skind).ap()
    O_dram = nc.dram_tensor("O_scr", [D, TOK], F32, **skind).ap()
    XP_dram = nc.dram_tensor("XP_scr", [TOK, D], F32, **skind).ap()
    FFN_dram = nc.dram_tensor("FFN_scr", [D, TOK], F32, **skind).ap()
    GW_dram = nc.dram_tensor("GW_scr", [8, TOK], F32, **skind).ap()
    X1_dram = nc.dram_tensor("X1_scr", [128, 16 * 1024], F32, **skind).ap()
    row_dram = nc.dram_tensor("row_scr", [1, TOK], F32).ap()
    wrow_dram = nc.dram_tensor("wrow_scr", [1, TOK], F32).ap()
    sc_dram = nc.dram_tensor("sc_scr", [1, 8], F32).ap()

    with ExitStack() as ctx:
        en = ctx.enter_context
        big_a = en(nc.sbuf_tensor("big_a", [128, 16 * 1024], F32))
        big_b = en(nc.sbuf_tensor("big_b", [128, 16 * 1024], F32))
        kvw = en(nc.sbuf_tensor("kvw", [128, 8192], F32))
        qbuf = en(nc.sbuf_tensor("qbuf", [128, TOK], F32))
        obuf = en(nc.sbuf_tensor("obuf", [128, TOK], F32))
        scr2k = en(nc.sbuf_tensor("scr2k", [128, 2048], F32))
        ktmp = en(nc.sbuf_tensor("ktmp", [128, 1024], F32))
        rtmp = en(nc.sbuf_tensor("rtmp", [128, 1024], F32))
        wrep = en(nc.sbuf_tensor("wrep", [128, 1024], F32))
        qkob = en(nc.sbuf_tensor("qkob", [128, 48], F32))
        ident = en(nc.sbuf_tensor("ident", [128, 128], F32))
        swapm = en(nc.sbuf_tensor("swapm", [128, 128], F32))
        iotaf = en(nc.sbuf_tensor("iotaf", [128, 128], F32))
        Wr_sb = en(nc.sbuf_tensor("Wr_sb", [128, DC * NE], F32))
        cols = en(nc.sbuf_tensor("cols", [128, 16], F32))
        sc16 = en(nc.sbuf_tensor("sc16", [128, 16], F32))
        scols = en(nc.sbuf_tensor("scols", [128, 8], F32))
        qn2c = en(nc.sbuf_tensor("qn2c", [128, 8], F32))
        negm = en(nc.sbuf_tensor("negm", [128, 8], F32))
        negm2 = en(nc.sbuf_tensor("negm2", [128, 8], F32))
        srows = en(nc.sbuf_tensor("srows", [8, 128], F32))
        kmax2 = en(nc.sbuf_tensor("kmax2", [1, 16], F32))
        iotap = en(nc.sbuf_tensor("iotap", [128, 1], F32))
        iotap64 = en(nc.sbuf_tensor("iotap64", [64, 1], F32))
        invf64 = en(nc.sbuf_tensor("invf64", [64, 1], F32))
        invt_col = en(nc.sbuf_tensor("invt_col", [128, 1], F32))
        cofs_col = en(nc.sbuf_tensor("cofs_col", [128, 1], F32))
        km2rep = en(nc.sbuf_tensor("km2rep", [128, 1], F32))
        invf128 = en(nc.sbuf_tensor("invf128", [128, 1], F32))
        sign_col = en(nc.sbuf_tensor("sign_col", [128, 1], F32))
        swapidx = en(nc.sbuf_tensor("swapidx", [128, 1], F32))
        sgn2 = en(nc.sbuf_tensor("sgn2", [1, 2], F32))
        ones_col = en(nc.sbuf_tensor("ones_col", [128, 1], F32))
        eps_col = en(nc.sbuf_tensor("eps_col", [128, 1], F32))
        gc_col = en(nc.sbuf_tensor("gc_col", [128, 1], F32))
        g = {nm: en(nc.sbuf_tensor("g_" + nm, [128, 64], F32))
             for nm in ("inh", "t1", "t2", "dist", "w", "fb", "fb2",
                        "eqi1", "eqi2")}
        gi8 = en(nc.sbuf_tensor("gi8", [128, 8], F32))
        gi8m = en(nc.sbuf_tensor("gi8m", [128, 8], F32))
        gc_ = {nm: en(nc.sbuf_tensor("gc_" + nm, [128, 8], F32))
               for nm in ("a", "b", "c", "d", "e", "f")}
        jv1 = en(nc.sbuf_tensor("jv1", [128, 8], F32))
        jv2 = en(nc.sbuf_tensor("jv2", [128, 8], F32))
        js1 = en(nc.sbuf_tensor("js1", [128, 8], F32))
        js2 = en(nc.sbuf_tensor("js2", [128, 8], F32))
        srows2 = en(nc.sbuf_tensor("srows2", [8, 128], F32))
        PS0 = en(nc.psum_tensor("PS0", [128, 2048], F32))
        PS1 = en(nc.psum_tensor("PS1", [128, 2048], F32))

        for s_ in ("ld", "lg", "sv", "ss", "st", "sg", "out"):
            b.sem(ctx, s_)
        E, Wt = b.emit, b.wait

        def pad_v(n=4):
            for _ in range(n):
                E("vector", lambda q: q.tensor_copy(jv1[:, :], jv2[:, :]))

        def pad_s(n=4):
            for _ in range(n):
                E("scalar", lambda q: q.copy(js1[:, :], js2[:, :]))


        SIN_OFS, COS_OFS = 4096, 5120
        TB1, TB2, TB3, TBI = 0, 1024, 2048, 3072

        def bb64(col, width=1024):
            return bass.AP(big_b, col, [[16 * 1024, 64], [1, width]])

        sin2s = kvw[:, SIN_OFS:SIN_OFS + 1024]
        cos2 = kvw[:, COS_OFS:COS_OFS + 1024]

        E("gpsimd", lambda q: q.iota(iotap[:, :], [[0, 1]], channel_multiplier=1,
                                     allow_small_or_imprecise_dtypes=True))
        E("gpsimd", lambda q: q.iota(iotap64[:, :], [[0, 1]], channel_multiplier=1,
                                     allow_small_or_imprecise_dtypes=True))
        E("gpsimd", lambda q: q.iota(iotaf[:, :], [[1, 128]], channel_multiplier=0,
                                     allow_small_or_imprecise_dtypes=True))
        E("gpsimd", lambda q: q.iota(gi8[:, :], [[1, 8]], channel_multiplier=0,
                                     allow_small_or_imprecise_dtypes=True))
        E("gpsimd", lambda q: q.memset(jv2[:, :], 0.0))
        E("gpsimd", lambda q: q.memset(sgn2[0:1, 0:1], -1.0))
        E("gpsimd", lambda q: q.memset(sgn2[0:1, 1:2], 1.0))
        E("gpsimd", lambda q: q.memset(js2[:, :], 0.0))
        E("gpsimd", lambda q: q.memset(ones_col[:, :], 1.0))
        E("gpsimd", lambda q: q.memset(eps_col[:, :], EPS_LN))
        E("gpsimd", lambda q: q.memset(gc_col[:, :], -GC))
        E("gpsimd", lambda q: q.memset(kmax2[:, :], 0.0), inc="sg")
        sg_c0 = b.cnt["sg"]

        b.dma("sync", "ld", kvw[0:1, 6144:8192], bv)
        b.dma("sync", "ld", Wr_sb[:, :],
              bass.AP(Wr.tensor, 0, [[NE, 128], [128 * NE, DC], [1, NE]]))
        b.dma("sync", "ld", cofs_col[:, :], core_ofs.partition_broadcast(128))
        b.dma("sync", "ld", cols[0:1, 14:15], temp)
        ld_c0 = b.cnt["ld"]

        Wt("vector", "ld", ld_c0)
        E("vector", lambda q: q.reciprocal(cols[0:1, 15:16], cols[0:1, 14:15]), inc="sv")
        Wt("gpsimd", "sv")
        b.dma("gpsimd", "lg", sc_dram[0:1, 0:1], cols[0:1, 15:16])
        Wt("gpsimd", "lg")
        b.dma("gpsimd", "lg", invt_col[:, :], sc_dram[0:1, 0:1].partition_broadcast(128))

        Wt("scalar", "sg", sg_c0)
        E("scalar", lambda q: q.activation(invf64[:, :], iotap64[:, :], AF.Exp,
                                           scale=-LN1E4_64), inc="ss")
        ss_invf = b.cnt["ss"]
        Wt("vector", "sg", sg_c0)
        E("vector", lambda q: q.tensor_scalar(ident[:, :], iotaf[:, :], iotap[:, 0:1],
                                              None, ALU.is_equal))
        E("vector", lambda q: q.tensor_scalar(gi8m[:, :], gi8[:, :], -8.0,
                                              None, ALU.add), inc="sv")
        Wt("gpsimd", "ss")
        b.dma("gpsimd", "lg", row_dram[0:1, 0:64], invf64[:, :])
        b.dma("gpsimd", "lg", wrow_dram[0:1, 0:2], sgn2[0:1, :])
        b.dma("gpsimd", "lg", invf128[:, :],
              bass.AP(row_dram.tensor, 0, [[1, 64], [0, 2]]))
        b.dma("gpsimd", "lg", sign_col[:, :],
              bass.AP(wrow_dram.tensor, 0, [[0, 64], [1, 2]]))
        lg_c0 = b.cnt["lg"]
        Wt("vector", "lg", lg_c0)
        E("vector", lambda q: q.tensor_tensor(swapidx[:, :], iotap[:, :],
                                              sign_col[:, :], ALU.subtract))
        pad_v(4)
        E("vector", lambda q: q.tensor_scalar(swapm[:, :], iotaf[:, :],
                                              swapidx[:, 0:1], None, ALU.is_equal),
          inc="sv")
        for bi, bsrc in enumerate((bq, bk, bo)):
            Wt("sync", "st")
            b.dma("sync", "ld", scr2k[0:16, 0:128],
                  bass.AP(bsrc.tensor, 0, [[128, 16], [1, 128]]))
            Wt("tensor", "ld")
            Wt("tensor", "sv")
            E("tensor", lambda q, bi=bi: q.transpose(
                PS1[:, bi * 16:(bi + 1) * 16],
                scr2k[0:16, 0:128],
                ident[0:16, 0:16]), inc="st")
        Wt("vector", "st")
        E("vector", lambda q: q.tensor_copy(qkob[:, 0:48], PS1[:, 0:48]), inc="sv")
        Wt("gpsimd", "st")
        E("gpsimd", lambda q: q.memset(iotaf[0:1, :], 1.0), inc="sg")
        sg_ones = b.cnt["sg"]

        def bb128(col, width=1024):
            return bass.AP(big_b, col, [[16 * 1024, 128], [1, width]])

        def build_tables(ofs_imm):
            Wt("gpsimd", "st")
            Wt("gpsimd", "sv")
            Wt("gpsimd", "lg")
            E("gpsimd", lambda q: q.iota(bb128(TBI), [[1, 1024]], channel_multiplier=0,
                                         allow_small_or_imprecise_dtypes=True), inc="sg")
            Wt("vector", "sg")
            Wt("vector", "lg")
            Wt("vector", "ss", ss_invf)
            if ofs_imm is None:
                Wt("vector", "ld", ld_c0)
                E("vector", lambda q: q.tensor_scalar(bb128(TB1), bb128(TBI),
                                                      cofs_col[:, 0:1], None, ALU.add))
            else:
                E("vector", lambda q, o=float(ofs_imm): q.tensor_scalar(
                    bb128(TB1), bb128(TBI), o, None, ALU.add))
            E("vector", lambda q: q.tensor_scalar(bb128(TB1), bb128(TB1),
                                                  invf128[:, 0:1], None, ALU.mult))
            E("vector", lambda q: q.tensor_scalar(bb128(TB2), bb128(TB1), 1.0 / TWOPI,
                                                  None, ALU.mult))
            E("vector", lambda q: q.tensor_copy(bb128(TBI).bitcast(I32), bb128(TB2)))
            E("vector", lambda q: q.tensor_copy(bb128(TB2), bb128(TBI).bitcast(I32)))
            E("vector", lambda q: q.scalar_tensor_tensor(bb128(TB1), bb128(TB2),
                                                         -TWOPI_HI, bb128(TB1),
                                                         ALU.mult, ALU.add))
            E("vector", lambda q: q.scalar_tensor_tensor(bb128(TB1), bb128(TB2),
                                                         -TWOPI_LO, bb128(TB1),
                                                         ALU.mult, ALU.add))
            E("vector", lambda q: q.tensor_scalar(bb128(TB3), bb128(TB1), PI, None,
                                                  ALU.is_gt))
            E("vector", lambda q: q.scalar_tensor_tensor(bb128(TB1), bb128(TB3),
                                                         -TWOPI_HI, bb128(TB1),
                                                         ALU.mult, ALU.add))
            E("vector", lambda q: q.tensor_scalar(bb128(TB3), bb128(TB1), -PI, None,
                                                  ALU.is_lt))
            E("vector", lambda q: q.scalar_tensor_tensor(bb128(TB1), bb128(TB3),
                                                         TWOPI_HI, bb128(TB1),
                                                         ALU.mult, ALU.add), inc="sv")
            Wt("scalar", "sv")
            E("scalar", lambda q: q.activation(bb128(TB2), bb128(TB1), AF.Sin))
            E("scalar", lambda q: q.activation(bb128(TB3), bb128(TB1), AF.Sin,
                                               scale=0.5), inc="ss")
            Wt("vector", "ss")
            E("vector", lambda q: q.tensor_scalar(sin2s, bb128(TB2), sign_col[:, 0:1],
                                                  None, ALU.mult))
            E("vector", lambda q: q.tensor_tensor(bb128(TB3), bb128(TB3), bb128(TB3),
                                                  ALU.mult))
            E("vector", lambda q: q.tensor_scalar(cos2, bb128(TB3), -2.0, 1.0,
                                                  ALU.mult, ALU.add), inc="sv")
            return b.cnt["sv"]

        def ln_slab(src_handle, dst_handle):
            Wt("vector", "ld")
            Wt("vector", "lg")
            Wt("scalar", "ld")
            Wt("scalar", "lg")
            Wt("scalar", "st")
            for tc in range(TC):
                E("vector", lambda q, tc=tc: q.reduce_sum(
                    cols[:, tc:tc + 1],
                    src_handle[:, tc * 2048:(tc + 1) * 2048], AX.X), inc="sv")
                E("scalar", lambda q, tc=tc: q.activation(
                    PS0[:, 0:2048], src_handle[:, tc * 2048:(tc + 1) * 2048],
                    AF.Square, accum_out=cols[:, 8 + tc:9 + tc]), inc="ss")
            Wt("vector", "ss")
            pad_v(4)
            E("vector", lambda q: q.tensor_scalar(gc_["a"][:, :], cols[:, 0:8],
                                                  1.0 / D, None, ALU.mult))
            E("vector", lambda q: q.tensor_scalar(gc_["b"][:, :], cols[:, 8:16],
                                                  1.0 / D, None, ALU.mult))
            pad_v(3)
            E("vector", lambda q: q.tensor_tensor(gc_["c"][:, :], gc_["a"][:, :],
                                                  gc_["a"][:, :], ALU.mult))
            pad_v(3)
            E("vector", lambda q: q.tensor_tensor(gc_["d"][:, :], gc_["b"][:, :],
                                                  gc_["c"][:, :], ALU.subtract),
              inc="sv")
            Wt("scalar", "sv")
            E("scalar", lambda q: q.activation(gc_["e"][:, :], gc_["d"][:, :],
                                               AF.Sqrt, bias=eps_col[:, 0:1]),
              inc="ss")
            Wt("vector", "ss")
            E("vector", lambda q: q.reciprocal(gc_["f"][:, :], gc_["e"][:, :]))
            pad_v(3)
            E("vector", lambda q: q.tensor_tensor(gc_["c"][:, :], gc_["a"][:, :],
                                                  gc_["f"][:, :], ALU.mult))
            pad_v(3)
            E("vector", lambda q: q.tensor_scalar(gc_["d"][:, :], gc_["c"][:, :],
                                                  -1.0, None, ALU.mult), inc="sv")
            pad_v(4)
            for tc in range(TC):
                E("vector", lambda q, tc=tc: q.tensor_scalar(
                    scr2k[:, :], src_handle[:, tc * 2048:(tc + 1) * 2048],
                    gc_["f"][:, tc:tc + 1], gc_["d"][:, tc:tc + 1],
                    ALU.mult, ALU.add), inc="sv")
                transpose_chunks(dst_handle, tc * 128, 1024,
                                 lambda c: scr2k[:, c * 128:(c + 1) * 128], DC)

        tpar = {"p": 0}

        def transpose_chunks(dst_handle, dst_base, dst_cstride, src_ap_fn, nch):
            for half in range((nch + 7) // 8):
                n_h = min(8, nch - half * 8)
                po = 1024 * tpar["p"]
                Wt("tensor", "sv")
                Wt("tensor", "ss")
                for j in range(n_h):
                    c = half * 8 + j
                    E("tensor", lambda q, c=c, j=j, po=po: q.transpose(
                        PS1[:, po + j * 128: po + (j + 1) * 128],
                        src_ap_fn(c), ident[:, :]),
                      inc=("st" if j == n_h - 1 else None))
                Wt("vector", "st")
                E("vector", lambda q, po=po, half=half, n_h=n_h: q.tensor_copy(
                    bass.AP(dst_handle, dst_base + half * 8 * dst_cstride,
                            [[16 * 1024, 128], [dst_cstride, n_h], [1, 128]]),
                    bass.AP(PS1, po, [[2048, 128], [128, n_h], [1, 128]])),
                  inc="sv")
                tpar["p"] ^= 1

        def wcol(Wap, co, w, ld_):
            return bass.AP(Wap.tensor, Wap.offset + co * w,
                           [[ld_, 128], [128 * ld_, DC], [1, w]])

        def proj_chunk(dst_ps, ph, wslot, brow_idx, do):
            for dc in range(DC):
                for hf in range(2):
                    E("tensor", lambda q, dc=dc, hf=hf, ph=ph, w=wslot: q.matmul(
                        dst_ps[:, ph + hf * 512: ph + (hf + 1) * 512],
                        kvw[:, w + dc * 128: w + (dc + 1) * 128],
                        big_a[:, dc * 1024 + hf * 512: dc * 1024 + (hf + 1) * 512],
                        start=(dc == 0), stop=(dc == DC - 1)),
                      inc=("st" if dc == DC - 1 and hf == 1 else None))

        def rope_chunk(ph, do, track_k2, dst_ap, bias_base):
            Wt("vector", "st")
            E("vector", lambda q, ph=ph, bc=bias_base + do: q.tensor_scalar(
                ktmp[:, :], PS0[:, ph:ph + 1024], qkob[:, bc:bc + 1], None,
                ALU.add), inc="sv")
            Wt("tensor", "ss")
            Wt("tensor", "sv")
            sph = 1024 * (do % 2)
            for hf in range(2):
                E("tensor", lambda q, hf=hf, sph=sph: q.matmul(
                    PS1[:, sph + hf * 512: sph + (hf + 1) * 512],
                    swapm[:, :], ktmp[:, hf * 512:(hf + 1) * 512],
                    start=True, stop=True), inc=("st" if hf == 1 else None))
            Wt("vector", "st")
            E("vector", lambda q: q.tensor_tensor(rtmp[:, :], ktmp[:, :], cos2,
                                                  ALU.mult))
            E("vector", lambda q, sph=sph, d=dst_ap: q.tensor_tensor(
                d, PS1[:, sph:sph + 1024], sin2s, ALU.mult))
            E("vector", lambda q, d=dst_ap: q.tensor_tensor(d, d, rtmp[:, :], ALU.add),
              inc="sv")
            if track_k2:
                Wt("scalar", "sv")
                E("scalar", lambda q, d=dst_ap: q.activation(rtmp[:, :], d, AF.Square),
                  inc="ss")
                Wt("tensor", "ss")
                for hf in range(2):
                    E("tensor", lambda q, hf=hf, sph=sph: q.matmul(
                        PS1[0:1, sph + hf * 512: sph + (hf + 1) * 512],
                        ones_col[:, 0:1], rtmp[:, hf * 512:(hf + 1) * 512],
                        start=True, stop=True), inc=("st" if hf == 1 else None))
                Wt("vector", "st")
                E("vector", lambda q, sph=sph: q.reduce_max(
                    cols[0:1, 13:14], PS1[0:1, sph:sph + 1024], AX.X))
                pad_v(4)
                E("vector", lambda q, do=do: q.tensor_max(
                    kmax2[0:1, do:do + 1], kmax2[0:1, do:do + 1], cols[0:1, 13:14]),
                  inc="sv")

        if PHASES >= 1:
            for sl in range(NSLAB):
                build_tables(float(sl * TOK))
                Wt("sync", "sv")
                Wt("sync", "lg")
                b.dma("sync", "ld",
                      bass.AP(big_b, 0, [[16 * 1024, 128], [2048, TC], [1, 2048]]),
                      bass.AP(x_full.tensor, sl * TOK * D,
                              [[D, 128], [128 * D, TC], [1, D]]))
                ln_slab(big_b, big_a)
                for do in range(DC):
                    wslot = (do % 2) * 2048
                    Wt("sync", "st")
                    b.dma("sync", "ld", kvw[:, wslot:wslot + 2048], wcol(Wk, do, 128, D))
                    Wt("tensor", "ld")
                    Wt("tensor", "ss")
                    Wt("tensor", "sv")
                    ph = 1024 * (do % 2)
                    proj_chunk(PS0, ph, wslot, 2, do)
                    rope_chunk(ph, do, True, big_b[:, do * 1024:(do + 1) * 1024], 16)
                Wt("gpsimd", "sv")
                Wt("gpsimd", "lg")
                E("gpsimd", lambda q, sl=sl: q.dma_start(
                    out=bass.AP(K_dram.tensor, sl * TOK,
                                [[S, 128], [128 * S, DC], [1, 1024]]),
                    in_=bass.AP(big_b, 0,
                                [[16 * 1024, 128], [1024, DC], [1, 1024]])),
                  inc="lg", add=16)
                lg_ksp = b.cnt["lg"]
                for dv8 in range(8):
                    Wt("sync", "st")
                    Wt("sync", "lg", lg_ksp)
                    b.dma("sync", "ld", kvw[:, 0:4096], wcol(Wv, dv8, 256, D))
                    Wt("tensor", "ld")
                    Wt("tensor", "lg", lg_ksp)
                    for tc in range(TC):
                        ps = PS0 if tc < 4 else PS1
                        pofs = (tc % 4) * 512
                        Wt("tensor", "ss")
                        for dc in range(DC):
                            E("tensor", lambda q, dc=dc, tc=tc, ps=ps, pofs=pofs: q.matmul(
                                ps[:, pofs:pofs + 256],
                                big_a[:, dc * 1024 + tc * 128: dc * 1024 + (tc + 1) * 128],
                                kvw[:, dc * 256:(dc + 1) * 256],
                                start=(dc == 0), stop=False))
                        E("tensor", lambda q, dv8=dv8, ps=ps, pofs=pofs: q.matmul(
                            ps[:, pofs:pofs + 256],
                            iotaf[0:1, 0:128],
                            kvw[0:1, 6144 + dv8 * 256: 6144 + (dv8 + 1) * 256],
                            start=False, stop=True), inc="st")
                        Wt("scalar", "st")
                        E("scalar", lambda q, tc=tc, dv8=dv8, ps=ps, pofs=pofs: q.copy(
                            big_b[:, tc * D + dv8 * 256: tc * D + (dv8 + 1) * 256],
                            ps[:, pofs:pofs + 256]), inc="ss")
                Wt("gpsimd", "ss")
                Wt("gpsimd", "lg")
                E("gpsimd", lambda q, sl=sl: q.dma_start(
                    out=bass.AP(V_dram.tensor, sl * TOK * D,
                                [[D, 128], [128 * D, TC], [1, D]]),
                    in_=bass.AP(big_b, 0,
                                [[16 * 1024, 128], [2048, TC], [1, 2048]])),
                  inc="lg", add=16)


        if PHASES >= 2:
            build_tables(None)
            Wt("sync", "sv")
            Wt("sync", "lg")
            b.dma("sync", "ld",
                  bass.AP(big_b, 0, [[16 * 1024, 128], [2048, TC], [1, 2048]]),
                  bass.AP(x_own.tensor, 0, [[D, 128], [128 * D, TC], [1, D]]))
            ln_slab(big_b, big_a)
            if DEBUG:
                Wt("gpsimd", "sv")
                Wt("gpsimd", "lg")
                E("gpsimd", lambda q: q.dma_start(out=X1_dram, in_=big_a[:, :]),
                  inc="lg", add=16)
            for do in range(DC):
                wslot = (do % 2) * 2048
                Wt("sync", "st")
                b.dma("sync", "ld", kvw[:, wslot:wslot + 2048], wcol(Wq, do, 128, D))
                Wt("tensor", "ld")
                Wt("tensor", "ss")
                Wt("tensor", "sv")
                Wt("tensor", "lg", b.cnt["lg"])
                ph = 1024 * (do % 2)
                proj_chunk(PS0, ph, wslot, 1, do)
                Wt("gpsimd", "sv")
                rope_chunk(ph, do, False, obuf[:, :], 0)
                Wt("gpsimd", "sv")
                Wt("gpsimd", "lg")
                E("gpsimd", lambda q, do=do: q.dma_start(
                    out=bass.AP(Q_dram.tensor, do * 128 * TOK, [[TOK, 128], [1, 1024]]),
                    in_=obuf[:, :]), inc="lg", add=16)
                Wt("vector", "lg")

        if PHASES >= 3:
            Wt("sync", "lg")
            for h in range(NH):
                Wt("sync", "st")
                Wt("sync", "ss")
                b.dma("sync", "ld", big_a[:, 0:S],
                      bass.AP(K_dram.tensor, h * 128 * S, [[S, 128], [1, S]]))
                b.dma("sync", "ld", big_a[:, S:2 * S],
                      bass.AP(V_dram.tensor, h * 128,
                              [[D, 128], [128 * D, 64], [1, 128]]))
                b.dma("sync", "ld", qbuf[:, :],
                      bass.AP(Q_dram.tensor, h * 128 * TOK, [[TOK, 128], [1, 1024]]))
                ld_head = b.cnt["ld"]
                Wt("scalar", "ld", ld_head)
                E("scalar", lambda q: q.activation(rtmp[:, :], qbuf[:, :], AF.Square),
                  inc="ss")
                Wt("tensor", "ss")
                Wt("tensor", "sv")
                for hf in range(2):
                    E("tensor", lambda q, hf=hf: q.matmul(
                        PS1[0:1, 1024 + hf * 512: 1024 + (hf + 1) * 512],
                        ones_col[:, 0:1], rtmp[:, hf * 512:(hf + 1) * 512],
                        start=True, stop=True), inc=("st" if hf == 1 else None))
                Wt("vector", "st")
                E("vector", lambda q: q.reduce_max(srows[0:1, 0:1],
                                                   PS1[0:1, 1024:2048], AX.X))
                pad_v(4)
                E("vector", lambda q, h=h: q.tensor_tensor(
                    srows[0:1, 1:2], srows[0:1, 0:1], kmax2[0:1, h:h + 1], ALU.mult),
                  inc="sv")
                Wt("scalar", "sv")
                E("scalar", lambda q: q.activation(srows[0:1, 2:3], srows[0:1, 1:2],
                                                   AF.Sqrt), inc="ss")
                Wt("vector", "ss")
                E("vector", lambda q: q.tensor_scalar(srows[0:1, 3:4], srows[0:1, 2:3],
                                                      -ISQ, None, ALU.mult), inc="sv")
                Wt("gpsimd", "sv")
                b.dma("gpsimd", "lg", sc_dram[0:1, 2:3], srows[0:1, 3:4])
                Wt("gpsimd", "lg")
                b.dma("gpsimd", "lg", km2rep[:, :],
                      sc_dram[0:1, 2:3].partition_broadcast(128))
                lg_negm = b.cnt["lg"]
                Wt("scalar", "lg", lg_negm)
                sv_negm = b.cnt["sv"]
                for qt in range(TC):
                    Wt("tensor", "ld", ld_head)
                    Wt("scalar", "sv")
                    for kc in range(16):
                        Wt("tensor", "ss")
                        E("tensor", lambda q, kc=kc, qt=qt: q.matmul(
                            PS0[:, (kc % 4) * 512:(kc % 4) * 512 + 512],
                            qbuf[:, qt * 128:(qt + 1) * 128],
                            big_a[:, kc * 512:(kc + 1) * 512],
                            start=True, stop=True), inc="st")
                        Wt("scalar", "st")
                        if kc == 0:
                            Wt("scalar", "sv", sv_negm)
                        E("scalar", lambda q, kc=kc, qt=qt: q.activation(
                            big_b[:, kc * 512:(kc + 1) * 512],
                            PS0[:, (kc % 4) * 512:(kc % 4) * 512 + 512], AF.Exp,
                            bias=km2rep[:, 0:1], scale=ISQ,
                            accum_out=sc16[:, kc:kc + 1]), inc="ss")
                    Wt("vector", "ss")
                    Wt("vector", "st")
                    E("vector", lambda q, qt=qt: q.reduce_sum(
                        scols[:, qt:qt + 1], sc16[:, 0:16], AX.X), inc="sv")
                    for batch in range(16):
                        po = 512 * (batch % 2)
                        Wt("tensor", "sv")
                        Wt("tensor", "ss")
                        for j in range(4):
                            ch = batch * 4 + j
                            E("tensor", lambda q, ch=ch, j=j, po=po: q.transpose(
                                PS1[:, po + j * 128: po + (j + 1) * 128],
                                big_b[:, ch * 128:(ch + 1) * 128],
                                ident[:, :]), inc=("st" if j == 3 else None))
                        Wt("vector", "st")
                        E("vector", lambda q, batch=batch, po=po: q.tensor_copy(
                            scr2k[:, (batch % 2) * 512:(batch % 2) * 512 + 512],
                            PS1[:, po:po + 512]), inc="sv")
                        Wt("tensor", "sv")
                        for j in range(4):
                            ch = batch * 4 + j
                            E("tensor", lambda q, ch=ch, j=j, batch=batch: q.matmul(
                                PS1[:, 1536:1664],
                                big_a[:, S + ch * 128: S + (ch + 1) * 128],
                                scr2k[:, (batch % 2) * 512 + j * 128:
                                      (batch % 2) * 512 + (j + 1) * 128],
                                start=(ch == 0), stop=(ch == 63)),
                              inc=("st" if ch == 63 else None))
                    Wt("vector", "st")
                    E("vector", lambda q, qt=qt: q.tensor_copy(
                        obuf[:, qt * 128:(qt + 1) * 128], PS1[:, 1536:1664]), inc="sv")
                Wt("tensor", "sv")
                E("tensor", lambda q: q.transpose(PS1[0:8, 1664:1792], scols[:, 0:8],
                                                  ident[:, :]), inc="st")
                Wt("vector", "st")
                E("vector", lambda q: q.tensor_copy(srows[0:8, :], PS1[0:8, 1664:1792]))
                pad_v(4)
                E("vector", lambda q: q.reciprocal(srows2[0:8, :], srows[0:8, :]),
                  inc="sv")
                Wt("gpsimd", "sv")
                b.dma("gpsimd", "lg", bass.AP(row_dram.tensor, 0, [[128, 8], [1, 128]]),
                      srows2[0:8, :])
                Wt("gpsimd", "lg")
                b.dma("gpsimd", "lg", wrep[:, :], row_dram.partition_broadcast(128))
                Wt("vector", "lg")
                E("vector", lambda q: q.tensor_tensor(obuf[:, :], obuf[:, :],
                                                      wrep[:, :], ALU.mult), inc="sv")
                Wt("gpsimd", "sv")
                Wt("gpsimd", "lg")
                E("gpsimd", lambda q, h=h: q.dma_start(
                    out=bass.AP(O_dram.tensor, h * 128 * TOK, [[TOK, 128], [1, 1024]]),
                    in_=obuf[:, :]), inc="lg", add=16)
                Wt("vector", "lg")

        if PHASES >= 4:
            Wt("sync", "lg")
            Wt("sync", "st")
            Wt("sync", "ss")
            b.dma("sync", "ld", big_b[:, :],
                  bass.AP(O_dram.tensor, 0, [[TOK, 128], [128 * TOK, DC], [1, 1024]]))
            for do in range(DC):
                wslot = (do % 2) * 2048
                Wt("sync", "st")
                b.dma("sync", "ld", kvw[:, wslot:wslot + 2048], wcol(Wo, do, 128, D))
                Wt("tensor", "ld")
                Wt("tensor", "ss")
                ph = 1024 * (do % 2)
                for dc in range(DC):
                    for hf in range(2):
                        E("tensor", lambda q, dc=dc, hf=hf, ph=ph, w=wslot: q.matmul(
                            PS0[:, ph + hf * 512: ph + (hf + 1) * 512],
                            kvw[:, w + dc * 128: w + (dc + 1) * 128],
                            big_b[:, dc * 1024 + hf * 512: dc * 1024 + (hf + 1) * 512],
                            start=(dc == 0), stop=(dc == DC - 1)),
                          inc=("st" if dc == DC - 1 and hf == 1 else None))
                Wt("vector", "st")
                E("vector", lambda q, ph=ph, do=do: q.tensor_scalar(
                    ktmp[:, :], PS0[:, ph:ph + 1024], qkob[:, 32 + do:33 + do], None,
                    ALU.add), inc="sv")
                transpose_chunks(big_a, do * 128, 2048,
                                 lambda c: ktmp[:, c * 128:(c + 1) * 128], TC)
            for tc in range(TC):
                Wt("sync", "sv")
                b.dma("sync", "ld", kvw[:, 0:2048], x_own[tc * 128:(tc + 1) * 128, :])
                Wt("vector", "ld")
                E("vector", lambda q, tc=tc: q.tensor_tensor(
                    big_a[:, tc * 2048:(tc + 1) * 2048],
                    big_a[:, tc * 2048:(tc + 1) * 2048], kvw[:, 0:2048], ALU.add),
                  inc="sv")
            Wt("gpsimd", "sv")
            Wt("gpsimd", "lg")
            E("gpsimd", lambda q: q.dma_start(
                out=bass.AP(XP_dram.tensor, 0, [[D, 128], [128 * D, TC], [1, D]]),
                in_=bass.AP(big_a, 0, [[16 * 1024, 128], [2048, TC], [1, 2048]])),
              inc="lg", add=16)
            ln_slab(big_a, big_b)

        if PHASES >= 5:
            def g3(t):
                return bass.AP(t, 0, [[64, 128], [8, 8], [1, 8]])

            def gcE(t):
                return bass.AP(t, 0, [[8, 128], [1, 8], [0, 8]])

            def gcT(t):
                return bass.AP(t, 0, [[8, 128], [0, 8], [1, 8]])

            def gro(t):
                return bass.AP(t, 0, [[8, 128], [1, 8], [1, 1]])

            for tc in range(TC):
                po = 512 * (tc % 2)
                Wt("tensor", "sv")
                Wt("tensor", "ss")
                for dc in range(DC):
                    E("tensor", lambda q, dc=dc, tc=tc, po=po: q.matmul(
                        PS1[:, po:po + 8],
                        big_b[:, dc * 1024 + tc * 128: dc * 1024 + (tc + 1) * 128],
                        Wr_sb[:, dc * 8:(dc + 1) * 8],
                        start=(dc == 0), stop=(dc == DC - 1)),
                      inc=("st" if dc == DC - 1 else None))
                Wt("scalar", "st")
                Wt("scalar", "lg")
                E("scalar", lambda q, po=po, tc=tc: q.activation(
                    g["inh"][:, tc * 8:(tc + 1) * 8], PS1[:, po:po + 8], AF.Sigmoid,
                    scale=invt_col[:, 0:1]), inc="ss")
            Wt("vector", "ss")
            E("vector", lambda q: q.tensor_scalar(g["t1"][:, :], g["inh"][:, :],
                                                  GL, None, ALU.is_ge))
            E("vector", lambda q: q.tensor_scalar(g["t2"][:, :], g["inh"][:, :],
                                                  GU, None, ALU.is_le))
            pad_v(2)
            E("vector", lambda q: q.tensor_tensor(g["t2"][:, :], g["t1"][:, :],
                                                  g["t2"][:, :], ALU.mult), inc="sv")
            pad_s(2)
            E("scalar", lambda q: q.activation(g["dist"][:, :], g["inh"][:, :],
                                               AF.Abs, bias=gc_col[:, 0:1]))
            pad_s(4)
            Wt("scalar", "sv")
            E("scalar", lambda q: q.activation(g["w"][:, :], g["dist"][:, :],
                                               AF.Exp, scale=-10.0))
            E("scalar", lambda q: q.activation(g["fb"][:, :], g["dist"][:, :],
                                               AF.Exp, scale=-10.0 / 3.0), inc="ss")
            Wt("vector", "ss")
            pad_v(3)
            E("vector", lambda q: q.tensor_tensor(g["w"][:, :], g["w"][:, :],
                                                  g["t2"][:, :], ALU.mult))
            pad_v(3)
            E("vector", lambda q: q.tensor_reduce(gro(gc_["a"]), g3(g["w"]),
                                                  AX.X, ALU.add))
            E("vector", lambda q: q.reduce_max(gro(gc_["c"]), g3(g["fb"]), AX.X))
            pad_v(3)
            E("vector", lambda q: q.tensor_scalar(gc_["b"][:, :], gc_["a"][:, :],
                                                  1e-8, None, ALU.is_lt))
            E("vector", lambda q: q.tensor_tensor(g3(g["t1"]), g3(g["fb"]),
                                                  gcE(gc_["c"]), ALU.is_equal))
            pad_v(3)
            E("vector", lambda q: q.tensor_tensor(g3(g["t1"]), g3(g["t1"]),
                                                  gcT(gi8m), ALU.mult))
            pad_v(4)
            E("vector", lambda q: q.tensor_reduce(gro(gc_["d"]), g3(g["t1"]),
                                                  AX.X, ALU.min))
            pad_v(4)
            E("vector", lambda q: q.tensor_tensor(g3(g["eqi1"]), gcT(gi8m),
                                                  gcE(gc_["d"]), ALU.is_equal))
            pad_v(4)
            E("vector", lambda q: q.scalar_tensor_tensor(g["fb2"][:, :],
                                                         g["eqi1"][:, :], -1e30,
                                                         g["fb"][:, :], ALU.mult,
                                                         ALU.add))
            pad_v(4)
            E("vector", lambda q: q.reduce_max(gro(gc_["c"]), g3(g["fb2"]), AX.X))
            pad_v(4)
            E("vector", lambda q: q.tensor_tensor(g3(g["t1"]), g3(g["fb2"]),
                                                  gcE(gc_["c"]), ALU.is_equal))
            pad_v(3)
            E("vector", lambda q: q.tensor_tensor(g3(g["t1"]), g3(g["t1"]),
                                                  gcT(gi8m), ALU.mult))
            pad_v(4)
            E("vector", lambda q: q.tensor_reduce(gro(gc_["d"]), g3(g["t1"]),
                                                  AX.X, ALU.min))
            pad_v(4)
            E("vector", lambda q: q.tensor_tensor(g3(g["eqi2"]), gcT(gi8m),
                                                  gcE(gc_["d"]), ALU.is_equal))
            pad_v(4)
            E("vector", lambda q: q.tensor_tensor(g["t1"][:, :], g["eqi1"][:, :],
                                                  g["eqi2"][:, :], ALU.add))
            pad_v(4)
            E("vector", lambda q: q.tensor_tensor(g["t1"][:, :], g["fb"][:, :],
                                                  g["t1"][:, :], ALU.mult))
            pad_v(4)
            E("vector", lambda q: q.tensor_reduce(gro(gc_["e"]), g3(g["t1"]),
                                                  AX.X, ALU.add))
            pad_v(4)
            E("vector", lambda q: q.tensor_scalar(gc_["c"][:, :], gc_["e"][:, :],
                                                  1e-8, None, ALU.max))
            pad_v(4)
            E("vector", lambda q: q.reciprocal(gc_["d"][:, :], gc_["c"][:, :]))
            pad_v(4)
            E("vector", lambda q: q.tensor_tensor(g3(g["t1"]), g3(g["t1"]),
                                                  gcE(gc_["d"]), ALU.mult))
            pad_v(4)
            E("vector", lambda q: q.tensor_tensor(g["t2"][:, :], g["t1"][:, :],
                                                  g["w"][:, :], ALU.subtract))
            pad_v(3)
            E("vector", lambda q: q.tensor_tensor(g3(g["t2"]), g3(g["t2"]),
                                                  gcE(gc_["b"]), ALU.mult))
            pad_v(4)
            E("vector", lambda q: q.tensor_tensor(g["w"][:, :], g["w"][:, :],
                                                  g["t2"][:, :], ALU.add))
            pad_v(4)
            E("vector", lambda q: q.tensor_reduce(gro(gc_["a"]), g3(g["w"]),
                                                  AX.X, ALU.add))
            pad_v(4)
            E("vector", lambda q: q.tensor_scalar(gc_["c"][:, :], gc_["a"][:, :],
                                                  1e-8, None, ALU.max))
            pad_v(4)
            E("vector", lambda q: q.reciprocal(gc_["d"][:, :], gc_["c"][:, :]))
            pad_v(4)
            E("vector", lambda q: q.tensor_tensor(g3(g["w"]), g3(g["w"]),
                                                  gcE(gc_["d"]), ALU.mult), inc="sv")
            for tc in range(TC):
                Wt("tensor", "sv")
                E("tensor", lambda q, tc=tc: q.transpose(
                    PS1[0:8, 1536:1664], g["w"][:, tc * 8:(tc + 1) * 8],
                    ident[:, :]), inc="st")
                Wt("vector", "st")
                E("vector", lambda q, tc=tc: q.tensor_copy(
                    qbuf[0:8, tc * 128:(tc + 1) * 128], PS1[0:8, 1536:1664]), inc="sv")
            if DEBUG:
                Wt("gpsimd", "sv")
                Wt("gpsimd", "lg")
                E("gpsimd", lambda q: q.dma_start(out=GW_dram, in_=qbuf[0:8, :]),
                  inc="lg", add=16)
            for e in range(NE):
                Wt("gpsimd", "sv")
                Wt("gpsimd", "ss")
                b.dma("gpsimd", "lg", wrow_dram, qbuf[e:e + 1, :])
                Wt("gpsimd", "lg")
                b.dma("gpsimd", "lg", wrep[:, :], row_dram.partition_broadcast(128)
                      if False else wrow_dram.partition_broadcast(128))
                lg_w = b.cnt["lg"]
                for fc in range(DC):
                    Wt("sync", "st")
                    b.dma("sync", "ld", kvw[:, 0:2048],
                          wcol(bass.AP(Wg.tensor, e * D * FF, [[1, 1]]), fc, 128, FF))
                    b.dma("sync", "ld", kvw[:, 2048:4096],
                          wcol(bass.AP(Wu.tensor, e * D * FF, [[1, 1]]), fc, 128, FF))
                    Wt("tensor", "ld")
                    Wt("tensor", "ss")
                    Wt("tensor", "sv")
                    for dc in range(DC):
                        for hf in range(2):
                            E("tensor", lambda q, dc=dc, hf=hf: q.matmul(
                                PS0[:, hf * 512:(hf + 1) * 512],
                                kvw[:, dc * 128:(dc + 1) * 128],
                                big_b[:, dc * 1024 + hf * 512: dc * 1024 + (hf + 1) * 512],
                                start=(dc == 0), stop=(dc == DC - 1)))
                    for dc in range(DC):
                        for hf in range(2):
                            E("tensor", lambda q, dc=dc, hf=hf: q.matmul(
                                PS0[:, 1024 + hf * 512: 1024 + (hf + 1) * 512],
                                kvw[:, 2048 + dc * 128: 2048 + (dc + 1) * 128],
                                big_b[:, dc * 1024 + hf * 512: dc * 1024 + (hf + 1) * 512],
                                start=(dc == 0), stop=(dc == DC - 1)),
                              inc=("st" if dc == DC - 1 and hf == 1 else None))
                    Wt("scalar", "st")
                    E("scalar", lambda q: q.activation(ktmp[:, :], PS0[:, 0:1024],
                                                       AF.Silu), inc="ss")
                    Wt("vector", "ss")
                    Wt("vector", "lg", lg_w)
                    E("vector", lambda q: q.tensor_tensor(rtmp[:, :], ktmp[:, :],
                                                          PS0[:, 1024:2048], ALU.mult))
                    E("vector", lambda q, fc=fc: q.tensor_tensor(
                        big_a[:, fc * 1024:(fc + 1) * 1024], rtmp[:, :], wrep[:, :],
                        ALU.mult), inc="sv")
                for do in range(DC):
                    Wt("sync", "st")
                    b.dma("sync", "ld", kvw[:, 0:2048],
                          wcol(bass.AP(Wd.tensor, e * FF * D, [[1, 1]]), do, 128, D))
                    Wt("tensor", "ld")
                    Wt("tensor", "sv")
                    ph = 1024 * (do % 2)
                    Wt("tensor", "ss")
                    for fc in range(DC):
                        for hf in range(2):
                            E("tensor", lambda q, fc=fc, hf=hf, ph=ph: q.matmul(
                                PS1[:, ph + hf * 512: ph + (hf + 1) * 512],
                                kvw[:, fc * 128:(fc + 1) * 128],
                                big_a[:, fc * 1024 + hf * 512: fc * 1024 + (hf + 1) * 512],
                                start=(fc == 0), stop=(fc == DC - 1)),
                              inc=("st" if fc == DC - 1 and hf == 1 else None))
                    Wt("scalar", "st")
                    Wt("scalar", "lg")
                    bo_ = 1024 * (do % 2)
                    E("scalar", lambda q, ph=ph, bo_=bo_: q.copy(
                        scr2k[:, bo_:bo_ + 1024], PS1[:, ph:ph + 1024]), inc="ss")
                    Wt("gpsimd", "ss")
                    kwacc = {} if e == 0 else dict(accum_op=ALU.add)
                    Wt("gpsimd", "lg")
                    E("gpsimd", lambda q, do=do, bo_=bo_, kw=kwacc: q.dma_start(
                        out=bass.AP(FFN_dram.tensor, do * 128 * TOK,
                                    [[TOK, 128], [1, 1024]]),
                        in_=scr2k[:, bo_:bo_ + 1024], **kw), inc="lg", add=16)

        if PHASES >= 6:
            Wt("sync", "lg")
            for tc in range(TC):
                Wt("sync", "st")
                Wt("sync", "sv")
                b.dma("sync", "ld", kvw[:, 0:2048],
                      bass.AP(FFN_dram.tensor, tc * 128,
                              [[TOK, 128], [128 * TOK, DC], [1, 128]]))
                b.dma("sync", "ld", kvw[:, 2048:4096],
                      bass.AP(XP_dram.tensor, tc * 128 * D, [[D, 128], [1, D]]))
                Wt("tensor", "ld")
                Wt("tensor", "sv")
                ps = PS0 if tc % 2 == 0 else PS1
                for dc in range(DC):
                    E("tensor", lambda q, dc=dc, ps=ps: q.transpose(
                        ps[:, dc * 128:(dc + 1) * 128],
                        kvw[:, dc * 128:(dc + 1) * 128], ident[:, :]),
                      inc=("st" if dc == DC - 1 else None))
                Wt("vector", "st")
                Wt("vector", "out")
                ao = 2048 * (tc % 2)
                E("vector", lambda q, ps=ps, ao=ao: q.tensor_tensor(
                    big_a[:, ao:ao + 2048], kvw[:, 2048:4096], ps[:, 0:2048], ALU.add),
                  inc="sv")
                Wt("sync", "sv")
                b.dma("sync", "out", out_own[tc * 128:(tc + 1) * 128, :],
                      big_a[:, ao:ao + 2048])

        Wt("sync", "out")
        Wt("sync", "lg")
        Wt("gpsimd", "lg")
        Wt("gpsimd", "sg")

        with nc.Block() as block:
            for eng in ENGS:
                def mk_run(eng):
                    def run(q):
                        for fn in b.prog[eng]:
                            fn(q)
                    return run
                getattr(block, eng)(mk_run(eng))

    return nc


_cached = {}


def kernel(L0_attn_hidden_state, L0_attn_Wq, L0_attn_bq, L0_attn_Wk, L0_attn_bk,
           L0_attn_Wv, L0_attn_bv, L0_attn_Wo, L0_attn_bo, L0_ffn_Wg, L0_ffn_Wu,
           L0_ffn_Wd, L0_ffn_Wr, L0_ffn_temperature):
    x = np.ascontiguousarray(np.asarray(L0_attn_hidden_state, dtype=np.float32)
                             .reshape(S, D))
    if "nc" not in _cached:
        _cached["nc"] = build_program()
    nc = _cached["nc"]
    common = {
        "x_full": x,
        "Wq": np.ascontiguousarray(L0_attn_Wq, dtype=np.float32),
        "Wk": np.ascontiguousarray(L0_attn_Wk, dtype=np.float32),
        "Wv": np.ascontiguousarray(L0_attn_Wv, dtype=np.float32),
        "Wo": np.ascontiguousarray(L0_attn_Wo, dtype=np.float32),
        "bq": np.ascontiguousarray(L0_attn_bq, dtype=np.float32).reshape(1, D),
        "bk": np.ascontiguousarray(L0_attn_bk, dtype=np.float32).reshape(1, D),
        "bv": np.ascontiguousarray(L0_attn_bv, dtype=np.float32).reshape(1, D),
        "bo": np.ascontiguousarray(L0_attn_bo, dtype=np.float32).reshape(1, D),
        "Wg": np.ascontiguousarray(L0_ffn_Wg, dtype=np.float32),
        "Wu": np.ascontiguousarray(L0_ffn_Wu, dtype=np.float32),
        "Wd": np.ascontiguousarray(L0_ffn_Wd, dtype=np.float32),
        "Wr": np.ascontiguousarray(L0_ffn_Wr, dtype=np.float32),
        "temp": np.ascontiguousarray(L0_ffn_temperature,
                                     dtype=np.float32).reshape(1, 1),
    }
    in_maps = []
    for c in range(8):
        m = dict(common)
        m["x_own"] = np.ascontiguousarray(x[c * TOK:(c + 1) * TOK])
        m["core_ofs"] = np.asarray([[float(c * TOK)]], dtype=np.float32)
        in_maps.append(m)
    res = run_bass_kernel_spmd(nc, in_maps, core_ids=list(range(8)))
    out = np.concatenate([r["out_own"] for r in res.results], axis=0)
    return out.reshape(1, S, D).astype(np.float32)


# version 04
# speedup vs baseline: 1.0080x; 1.0080x over previous
import math
from contextlib import ExitStack

import numpy as np
import concourse.bass as bass
import concourse.mybir as mybir
from concourse.bass_utils import run_bass_kernel_spmd

F32 = mybir.dt.float32
I32 = mybir.dt.int32
AF = mybir.ActivationFunctionType
ALU = mybir.AluOpType
AX = mybir.AxisListType

S, D, NH, HD = 8192, 2048, 16, 128
TOK = 1024
DC = D // 128
TC = TOK // 128
NSLAB = S // TOK
NE, FF = 8, 2048

PI = float(np.pi)
TWOPI = 2 * math.pi
TWOPI_HI = float(np.float32(TWOPI))
TWOPI_LO = float(TWOPI - np.float64(np.float32(TWOPI)))
ISQ = 1.0 / math.sqrt(HD)
LN1E4_64 = math.log(10000.0) / 64.0
GC = 1.0 / math.e
GL = 0.5 - math.log(4.0 / 3.0)
GU = 0.5
EPS_LN = 1e-5

ENGS = ("sync", "gpsimd", "vector", "scalar", "tensor")
DEBUG = False
PHASES = 9


class B:
    def __init__(self, nc):
        self.nc = nc
        self.prog = {e: [] for e in ENGS}
        self.sems = {}
        self.cnt = {}

    def sem(self, ctx, name):
        self.sems[name] = ctx.enter_context(self.nc.semaphore(name))
        self.cnt[name] = 0

    def emit(self, eng, fn, inc=None, add=1):
        if inc is None:
            self.prog[eng].append(fn)
            return None
        sh = self.sems[inc]
        self.cnt[inc] += add
        self.prog[eng].append(lambda q, fn=fn, sh=sh, add=add: fn(q).then_inc(sh, add))
        return self.cnt[inc]

    def wait(self, eng, sem, val=None):
        if val is None:
            val = self.cnt[sem]
        if val <= 0:
            return
        sh = self.sems[sem]
        self.prog[eng].append(lambda q, sh=sh, val=val: q.wait_ge(sh, val))

    def dma(self, eng, sem, out, in_, **kw):
        self.wait(eng, sem)
        return self.emit(eng, lambda q, o=out, i=in_, kw=kw: q.dma_start(out=o, in_=i, **kw),
                         inc=sem, add=16)


def build_program():
    nc = bass.Bass("TRN2", target_bir_lowering=False)
    b = B(nc)

    x_own = nc.dram_tensor("x_own", [TOK, D], F32, kind="ExternalInput").ap()
    x_full = nc.dram_tensor("x_full", [S, D], F32, kind="ExternalInput").ap()
    Wq = nc.dram_tensor("Wq", [D, D], F32, kind="ExternalInput").ap()
    Wk = nc.dram_tensor("Wk", [D, D], F32, kind="ExternalInput").ap()
    Wv = nc.dram_tensor("Wv", [D, D], F32, kind="ExternalInput").ap()
    Wo = nc.dram_tensor("Wo", [D, D], F32, kind="ExternalInput").ap()
    bq = nc.dram_tensor("bq", [1, D], F32, kind="ExternalInput").ap()
    bk = nc.dram_tensor("bk", [1, D], F32, kind="ExternalInput").ap()
    bv = nc.dram_tensor("bv", [1, D], F32, kind="ExternalInput").ap()
    bo = nc.dram_tensor("bo", [1, D], F32, kind="ExternalInput").ap()
    Wg = nc.dram_tensor("Wg", [NE, D, FF], F32, kind="ExternalInput").ap()
    Wu = nc.dram_tensor("Wu", [NE, D, FF], F32, kind="ExternalInput").ap()
    Wd = nc.dram_tensor("Wd", [NE, FF, D], F32, kind="ExternalInput").ap()
    Wr = nc.dram_tensor("Wr", [D, NE], F32, kind="ExternalInput").ap()
    temp = nc.dram_tensor("temp", [1, 1], F32, kind="ExternalInput").ap()
    core_ofs = nc.dram_tensor("core_ofs", [1, 1], F32, kind="ExternalInput").ap()
    out_own = nc.dram_tensor("out_own", [TOK, D], F32, kind="ExternalOutput").ap()

    skind = dict(kind="ExternalOutput") if DEBUG else {}
    K_dram = nc.dram_tensor("K_scr", [D, S], F32, **skind).ap()
    V_dram = nc.dram_tensor("V_scr", [S, D], F32, **skind).ap()
    Q_dram = nc.dram_tensor("Q_scr", [D, TOK], F32, **skind).ap()
    O_dram = nc.dram_tensor("O_scr", [D, TOK], F32, **skind).ap()
    XP_dram = nc.dram_tensor("XP_scr", [TOK, D], F32, **skind).ap()
    FFN_dram = nc.dram_tensor("FFN_scr", [D, TOK], F32, **skind).ap()
    GW_dram = nc.dram_tensor("GW_scr", [8, TOK], F32, **skind).ap()
    X1_dram = nc.dram_tensor("X1_scr", [128, 16 * 1024], F32, **skind).ap()
    row_dram = nc.dram_tensor("row_scr", [1, TOK], F32).ap()
    wrow_dram = nc.dram_tensor("wrow_scr", [1, TOK], F32).ap()
    sc_dram = nc.dram_tensor("sc_scr", [1, 8], F32).ap()

    with ExitStack() as ctx:
        en = ctx.enter_context
        big_a = en(nc.sbuf_tensor("big_a", [128, 16 * 1024], F32))
        big_b = en(nc.sbuf_tensor("big_b", [128, 16 * 1024], F32))
        kvw = en(nc.sbuf_tensor("kvw", [128, 8192], F32))
        qbuf = en(nc.sbuf_tensor("qbuf", [128, TOK], F32))
        obuf = en(nc.sbuf_tensor("obuf", [128, TOK], F32))
        scr2k = en(nc.sbuf_tensor("scr2k", [128, 2048], F32))
        ktmp = en(nc.sbuf_tensor("ktmp", [128, 1024], F32))
        rtmp = en(nc.sbuf_tensor("rtmp", [128, 1024], F32))
        wrep = en(nc.sbuf_tensor("wrep", [128, 1024], F32))
        qkob = en(nc.sbuf_tensor("qkob", [128, 48], F32))
        ident = en(nc.sbuf_tensor("ident", [128, 128], F32))
        swapm = en(nc.sbuf_tensor("swapm", [128, 128], F32))
        iotaf = en(nc.sbuf_tensor("iotaf", [128, 128], F32))
        Wr_sb = en(nc.sbuf_tensor("Wr_sb", [128, DC * NE], F32))
        cols = en(nc.sbuf_tensor("cols", [128, 16], F32))
        sc16 = en(nc.sbuf_tensor("sc16", [128, 16], F32))
        scols = en(nc.sbuf_tensor("scols", [128, 8], F32))
        qn2c = en(nc.sbuf_tensor("qn2c", [128, 8], F32))
        negm = en(nc.sbuf_tensor("negm", [128, 8], F32))
        negm2 = en(nc.sbuf_tensor("negm2", [128, 8], F32))
        srows = en(nc.sbuf_tensor("srows", [8, 128], F32))
        kmax2 = en(nc.sbuf_tensor("kmax2", [1, 16], F32))
        iotap = en(nc.sbuf_tensor("iotap", [128, 1], F32))
        iotap64 = en(nc.sbuf_tensor("iotap64", [64, 1], F32))
        invf64 = en(nc.sbuf_tensor("invf64", [64, 1], F32))
        invt_col = en(nc.sbuf_tensor("invt_col", [128, 1], F32))
        cofs_col = en(nc.sbuf_tensor("cofs_col", [128, 1], F32))
        km2rep = en(nc.sbuf_tensor("km2rep", [128, 1], F32))
        invf128 = en(nc.sbuf_tensor("invf128", [128, 1], F32))
        sign_col = en(nc.sbuf_tensor("sign_col", [128, 1], F32))
        swapidx = en(nc.sbuf_tensor("swapidx", [128, 1], F32))
        sgn2 = en(nc.sbuf_tensor("sgn2", [1, 2], F32))
        ones_col = en(nc.sbuf_tensor("ones_col", [128, 1], F32))
        eps_col = en(nc.sbuf_tensor("eps_col", [128, 1], F32))
        gc_col = en(nc.sbuf_tensor("gc_col", [128, 1], F32))
        g = {nm: en(nc.sbuf_tensor("g_" + nm, [128, 64], F32))
             for nm in ("inh", "t1", "t2", "dist", "w", "fb", "fb2",
                        "eqi1", "eqi2")}
        gi8 = en(nc.sbuf_tensor("gi8", [128, 8], F32))
        gi8m = en(nc.sbuf_tensor("gi8m", [128, 8], F32))
        gc_ = {nm: en(nc.sbuf_tensor("gc_" + nm, [128, 8], F32))
               for nm in ("a", "b", "c", "d", "e", "f")}
        jv1 = en(nc.sbuf_tensor("jv1", [128, 8], F32))
        jv2 = en(nc.sbuf_tensor("jv2", [128, 8], F32))
        js1 = en(nc.sbuf_tensor("js1", [128, 8], F32))
        js2 = en(nc.sbuf_tensor("js2", [128, 8], F32))
        srows2 = en(nc.sbuf_tensor("srows2", [8, 128], F32))
        PS0 = en(nc.psum_tensor("PS0", [128, 2048], F32))
        PS1 = en(nc.psum_tensor("PS1", [128, 2048], F32))

        for s_ in ("ld", "lg", "sv", "ss", "st", "sg", "out"):
            b.sem(ctx, s_)
        E, Wt = b.emit, b.wait

        def pad_v(n=4):
            for _ in range(n):
                E("vector", lambda q: q.tensor_copy(jv1[:, :], jv2[:, :]))

        def pad_s(n=4):
            for _ in range(n):
                E("scalar", lambda q: q.copy(js1[:, :], js2[:, :]))


        SIN_OFS, COS_OFS = 4096, 5120
        TB1, TB2, TB3, TBI = 0, 1024, 2048, 3072

        def bb64(col, width=1024):
            return bass.AP(big_b, col, [[16 * 1024, 64], [1, width]])

        sin2s = kvw[:, SIN_OFS:SIN_OFS + 1024]
        cos2 = kvw[:, COS_OFS:COS_OFS + 1024]

        E("gpsimd", lambda q: q.iota(iotap[:, :], [[0, 1]], channel_multiplier=1,
                                     allow_small_or_imprecise_dtypes=True))
        E("gpsimd", lambda q: q.iota(iotap64[:, :], [[0, 1]], channel_multiplier=1,
                                     allow_small_or_imprecise_dtypes=True))
        E("gpsimd", lambda q: q.iota(iotaf[:, :], [[1, 128]], channel_multiplier=0,
                                     allow_small_or_imprecise_dtypes=True))
        E("gpsimd", lambda q: q.iota(gi8[:, :], [[1, 8]], channel_multiplier=0,
                                     allow_small_or_imprecise_dtypes=True))
        E("gpsimd", lambda q: q.memset(jv2[:, :], 0.0))
        E("gpsimd", lambda q: q.memset(sgn2[0:1, 0:1], -1.0))
        E("gpsimd", lambda q: q.memset(sgn2[0:1, 1:2], 1.0))
        E("gpsimd", lambda q: q.memset(js2[:, :], 0.0))
        E("gpsimd", lambda q: q.memset(ones_col[:, :], 1.0))
        E("gpsimd", lambda q: q.memset(eps_col[:, :], EPS_LN))
        E("gpsimd", lambda q: q.memset(gc_col[:, :], -GC))
        E("gpsimd", lambda q: q.memset(kmax2[:, :], 0.0), inc="sg")
        sg_c0 = b.cnt["sg"]

        b.dma("sync", "ld", kvw[0:1, 6144:8192], bv)
        b.dma("sync", "ld", Wr_sb[:, :],
              bass.AP(Wr.tensor, 0, [[NE, 128], [128 * NE, DC], [1, NE]]))
        b.dma("sync", "ld", cofs_col[:, :], core_ofs.partition_broadcast(128))
        b.dma("sync", "ld", cols[0:1, 14:15], temp)
        ld_c0 = b.cnt["ld"]

        Wt("vector", "ld", ld_c0)
        E("vector", lambda q: q.reciprocal(cols[0:1, 15:16], cols[0:1, 14:15]), inc="sv")
        Wt("gpsimd", "sv")
        b.dma("gpsimd", "lg", sc_dram[0:1, 0:1], cols[0:1, 15:16])
        Wt("gpsimd", "lg")
        b.dma("gpsimd", "lg", invt_col[:, :], sc_dram[0:1, 0:1].partition_broadcast(128))

        Wt("scalar", "sg", sg_c0)
        E("scalar", lambda q: q.activation(invf64[:, :], iotap64[:, :], AF.Exp,
                                           scale=-LN1E4_64), inc="ss")
        ss_invf = b.cnt["ss"]
        Wt("vector", "sg", sg_c0)
        E("vector", lambda q: q.tensor_scalar(ident[:, :], iotaf[:, :], iotap[:, 0:1],
                                              None, ALU.is_equal))
        E("vector", lambda q: q.tensor_scalar(gi8m[:, :], gi8[:, :], -8.0,
                                              None, ALU.add), inc="sv")
        Wt("gpsimd", "ss")
        b.dma("gpsimd", "lg", row_dram[0:1, 0:64], invf64[:, :])
        b.dma("gpsimd", "lg", wrow_dram[0:1, 0:2], sgn2[0:1, :])
        b.dma("gpsimd", "lg", invf128[:, :],
              bass.AP(row_dram.tensor, 0, [[1, 64], [0, 2]]))
        b.dma("gpsimd", "lg", sign_col[:, :],
              bass.AP(wrow_dram.tensor, 0, [[0, 64], [1, 2]]))
        lg_c0 = b.cnt["lg"]
        Wt("vector", "sg", sg_c0)
        Wt("vector", "lg", lg_c0)
        E("vector", lambda q: q.tensor_tensor(swapidx[:, :], iotap[:, :],
                                              sign_col[:, :], ALU.subtract))
        pad_v(4)
        E("vector", lambda q: q.tensor_scalar(swapm[:, :], iotaf[:, :],
                                              swapidx[:, 0:1], None, ALU.is_equal),
          inc="sv")
        for bi, bsrc in enumerate((bq, bk, bo)):
            Wt("sync", "st")
            b.dma("sync", "ld", scr2k[0:16, 0:128],
                  bass.AP(bsrc.tensor, 0, [[128, 16], [1, 128]]))
            Wt("tensor", "ld")
            Wt("tensor", "sv")
            E("tensor", lambda q, bi=bi: q.transpose(
                PS1[:, bi * 16:(bi + 1) * 16],
                scr2k[0:16, 0:128],
                ident[0:16, 0:16]), inc="st")
        Wt("vector", "st")
        E("vector", lambda q: q.tensor_copy(qkob[:, 0:48], PS1[:, 0:48]), inc="sv")
        Wt("gpsimd", "st")
        E("gpsimd", lambda q: q.memset(iotaf[0:1, :], 1.0), inc="sg")
        sg_ones = b.cnt["sg"]

        def bb128(col, width=1024):
            return bass.AP(big_b, col, [[16 * 1024, 128], [1, width]])

        def build_tables(ofs_imm):
            Wt("gpsimd", "st")
            Wt("gpsimd", "sv")
            Wt("gpsimd", "lg")
            E("gpsimd", lambda q: q.iota(bb128(TBI), [[1, 1024]], channel_multiplier=0,
                                         allow_small_or_imprecise_dtypes=True), inc="sg")
            Wt("vector", "sg")
            Wt("vector", "lg")
            Wt("vector", "ss", ss_invf)
            if ofs_imm is None:
                Wt("vector", "ld", ld_c0)
                E("vector", lambda q: q.tensor_scalar(bb128(TB1), bb128(TBI),
                                                      cofs_col[:, 0:1], None, ALU.add))
            else:
                E("vector", lambda q, o=float(ofs_imm): q.tensor_scalar(
                    bb128(TB1), bb128(TBI), o, None, ALU.add))
            E("vector", lambda q: q.tensor_scalar(bb128(TB1), bb128(TB1),
                                                  invf128[:, 0:1], None, ALU.mult))
            E("vector", lambda q: q.tensor_scalar(bb128(TB2), bb128(TB1), 1.0 / TWOPI,
                                                  None, ALU.mult))
            E("vector", lambda q: q.tensor_copy(bb128(TBI).bitcast(I32), bb128(TB2)))
            E("vector", lambda q: q.tensor_copy(bb128(TB2), bb128(TBI).bitcast(I32)))
            E("vector", lambda q: q.scalar_tensor_tensor(bb128(TB1), bb128(TB2),
                                                         -TWOPI_HI, bb128(TB1),
                                                         ALU.mult, ALU.add))
            E("vector", lambda q: q.scalar_tensor_tensor(bb128(TB1), bb128(TB2),
                                                         -TWOPI_LO, bb128(TB1),
                                                         ALU.mult, ALU.add))
            E("vector", lambda q: q.tensor_scalar(bb128(TB3), bb128(TB1), PI, None,
                                                  ALU.is_gt))
            E("vector", lambda q: q.scalar_tensor_tensor(bb128(TB1), bb128(TB3),
                                                         -TWOPI_HI, bb128(TB1),
                                                         ALU.mult, ALU.add))
            E("vector", lambda q: q.tensor_scalar(bb128(TB3), bb128(TB1), -PI, None,
                                                  ALU.is_lt))
            E("vector", lambda q: q.scalar_tensor_tensor(bb128(TB1), bb128(TB3),
                                                         TWOPI_HI, bb128(TB1),
                                                         ALU.mult, ALU.add), inc="sv")
            Wt("scalar", "sv")
            E("scalar", lambda q: q.activation(bb128(TB2), bb128(TB1), AF.Sin))
            E("scalar", lambda q: q.activation(bb128(TB3), bb128(TB1), AF.Sin,
                                               scale=0.5), inc="ss")
            Wt("vector", "ss")
            E("vector", lambda q: q.tensor_scalar(sin2s, bb128(TB2), sign_col[:, 0:1],
                                                  None, ALU.mult))
            E("vector", lambda q: q.tensor_tensor(bb128(TB3), bb128(TB3), bb128(TB3),
                                                  ALU.mult))
            E("vector", lambda q: q.tensor_scalar(cos2, bb128(TB3), -2.0, 1.0,
                                                  ALU.mult, ALU.add), inc="sv")
            return b.cnt["sv"]

        def ln_slab(src_handle, dst_handle):
            Wt("vector", "ld")
            Wt("vector", "lg")
            Wt("scalar", "ld")
            Wt("scalar", "lg")
            Wt("scalar", "st")
            for tc in range(TC):
                E("vector", lambda q, tc=tc: q.reduce_sum(
                    cols[:, tc:tc + 1],
                    src_handle[:, tc * 2048:(tc + 1) * 2048], AX.X), inc="sv")
                E("scalar", lambda q, tc=tc: q.activation(
                    PS0[:, 0:2048], src_handle[:, tc * 2048:(tc + 1) * 2048],
                    AF.Square, accum_out=cols[:, 8 + tc:9 + tc]), inc="ss")
            Wt("vector", "ss")
            pad_v(4)
            E("vector", lambda q: q.tensor_scalar(gc_["a"][:, :], cols[:, 0:8],
                                                  1.0 / D, None, ALU.mult))
            E("vector", lambda q: q.tensor_scalar(gc_["b"][:, :], cols[:, 8:16],
                                                  1.0 / D, None, ALU.mult))
            pad_v(3)
            E("vector", lambda q: q.tensor_tensor(gc_["c"][:, :], gc_["a"][:, :],
                                                  gc_["a"][:, :], ALU.mult))
            pad_v(3)
            E("vector", lambda q: q.tensor_tensor(gc_["d"][:, :], gc_["b"][:, :],
                                                  gc_["c"][:, :], ALU.subtract),
              inc="sv")
            Wt("scalar", "sv")
            E("scalar", lambda q: q.activation(gc_["e"][:, :], gc_["d"][:, :],
                                               AF.Sqrt, bias=eps_col[:, 0:1]),
              inc="ss")
            Wt("vector", "ss")
            E("vector", lambda q: q.reciprocal(gc_["f"][:, :], gc_["e"][:, :]))
            pad_v(3)
            E("vector", lambda q: q.tensor_tensor(gc_["c"][:, :], gc_["a"][:, :],
                                                  gc_["f"][:, :], ALU.mult))
            pad_v(3)
            E("vector", lambda q: q.tensor_scalar(gc_["d"][:, :], gc_["c"][:, :],
                                                  -1.0, None, ALU.mult), inc="sv")
            pad_v(4)
            for tc in range(TC):
                E("vector", lambda q, tc=tc: q.tensor_scalar(
                    scr2k[:, :], src_handle[:, tc * 2048:(tc + 1) * 2048],
                    gc_["f"][:, tc:tc + 1], gc_["d"][:, tc:tc + 1],
                    ALU.mult, ALU.add), inc="sv")
                transpose_chunks(dst_handle, tc * 128, 1024,
                                 lambda c: scr2k[:, c * 128:(c + 1) * 128], DC)

        tpar = {"p": 0}

        def transpose_chunks(dst_handle, dst_base, dst_cstride, src_ap_fn, nch):
            for half in range((nch + 7) // 8):
                n_h = min(8, nch - half * 8)
                po = 1024 * tpar["p"]
                Wt("tensor", "sv")
                Wt("tensor", "ss")
                for j in range(n_h):
                    c = half * 8 + j
                    E("tensor", lambda q, c=c, j=j, po=po: q.transpose(
                        PS1[:, po + j * 128: po + (j + 1) * 128],
                        src_ap_fn(c), ident[:, :]),
                      inc=("st" if j == n_h - 1 else None))
                Wt("vector", "st")
                E("vector", lambda q, po=po, half=half, n_h=n_h: q.tensor_copy(
                    bass.AP(dst_handle, dst_base + half * 8 * dst_cstride,
                            [[16 * 1024, 128], [dst_cstride, n_h], [1, 128]]),
                    bass.AP(PS1, po, [[2048, 128], [128, n_h], [1, 128]])),
                  inc="sv")
                tpar["p"] ^= 1

        def wcol(Wap, co, w, ld_):
            return bass.AP(Wap.tensor, Wap.offset + co * w,
                           [[ld_, 128], [128 * ld_, DC], [1, w]])

        def proj_chunk(dst_ps, ph, wslot, brow_idx, do):
            for dc in range(DC):
                for hf in range(2):
                    E("tensor", lambda q, dc=dc, hf=hf, ph=ph, w=wslot: q.matmul(
                        dst_ps[:, ph + hf * 512: ph + (hf + 1) * 512],
                        kvw[:, w + dc * 128: w + (dc + 1) * 128],
                        big_a[:, dc * 1024 + hf * 512: dc * 1024 + (hf + 1) * 512],
                        start=(dc == 0), stop=(dc == DC - 1)),
                      inc=("st" if dc == DC - 1 and hf == 1 else None))

        def rope_chunk(ph, do, track_k2, dst_ap, bias_base):
            Wt("vector", "st")
            E("vector", lambda q, ph=ph, bc=bias_base + do: q.tensor_scalar(
                ktmp[:, :], PS0[:, ph:ph + 1024], qkob[:, bc:bc + 1], None,
                ALU.add), inc="sv")
            Wt("tensor", "ss")
            Wt("tensor", "sv")
            sph = 1024 * (do % 2)
            for hf in range(2):
                E("tensor", lambda q, hf=hf, sph=sph: q.matmul(
                    PS1[:, sph + hf * 512: sph + (hf + 1) * 512],
                    swapm[:, :], ktmp[:, hf * 512:(hf + 1) * 512],
                    start=True, stop=True), inc=("st" if hf == 1 else None))
            Wt("vector", "st")
            E("vector", lambda q: q.tensor_tensor(rtmp[:, :], ktmp[:, :], cos2,
                                                  ALU.mult))
            E("vector", lambda q, sph=sph, d=dst_ap: q.tensor_tensor(
                d, PS1[:, sph:sph + 1024], sin2s, ALU.mult))
            E("vector", lambda q, d=dst_ap: q.tensor_tensor(d, d, rtmp[:, :], ALU.add),
              inc="sv")
            if track_k2:
                Wt("scalar", "sv")
                E("scalar", lambda q, d=dst_ap: q.activation(rtmp[:, :], d, AF.Square),
                  inc="ss")
                Wt("tensor", "ss")
                for hf in range(2):
                    E("tensor", lambda q, hf=hf, sph=sph: q.matmul(
                        PS1[0:1, sph + hf * 512: sph + (hf + 1) * 512],
                        ones_col[:, 0:1], rtmp[:, hf * 512:(hf + 1) * 512],
                        start=True, stop=True), inc=("st" if hf == 1 else None))
                Wt("vector", "st")
                E("vector", lambda q, sph=sph: q.reduce_max(
                    cols[0:1, 13:14], PS1[0:1, sph:sph + 1024], AX.X))
                pad_v(4)
                E("vector", lambda q, do=do: q.tensor_max(
                    kmax2[0:1, do:do + 1], kmax2[0:1, do:do + 1], cols[0:1, 13:14]),
                  inc="sv")

        if PHASES >= 1:
            for sl in range(NSLAB):
                build_tables(float(sl * TOK))
                Wt("sync", "sv")
                Wt("sync", "lg")
                b.dma("sync", "ld",
                      bass.AP(big_b, 0, [[16 * 1024, 128], [2048, TC], [1, 2048]]),
                      bass.AP(x_full.tensor, sl * TOK * D,
                              [[D, 128], [128 * D, TC], [1, D]]))
                ln_slab(big_b, big_a)
                for do in range(DC):
                    wslot = (do % 2) * 2048
                    Wt("sync", "st")
                    b.dma("sync", "ld", kvw[:, wslot:wslot + 2048], wcol(Wk, do, 128, D))
                    Wt("tensor", "ld")
                    Wt("tensor", "ss")
                    Wt("tensor", "sv")
                    ph = 1024 * (do % 2)
                    proj_chunk(PS0, ph, wslot, 2, do)
                    rope_chunk(ph, do, True, big_b[:, do * 1024:(do + 1) * 1024], 16)
                Wt("gpsimd", "sv")
                Wt("gpsimd", "lg")
                E("gpsimd", lambda q, sl=sl: q.dma_start(
                    out=bass.AP(K_dram.tensor, sl * TOK,
                                [[S, 128], [128 * S, DC], [1, 1024]]),
                    in_=bass.AP(big_b, 0,
                                [[16 * 1024, 128], [1024, DC], [1, 1024]])),
                  inc="lg", add=16)
                lg_ksp = b.cnt["lg"]
                for dv8 in range(8):
                    Wt("sync", "st")
                    Wt("sync", "lg", lg_ksp)
                    b.dma("sync", "ld", kvw[:, 0:4096], wcol(Wv, dv8, 256, D))
                    Wt("tensor", "ld")
                    Wt("tensor", "lg", lg_ksp)
                    for tc in range(TC):
                        ps = PS0 if tc < 4 else PS1
                        pofs = (tc % 4) * 512
                        Wt("tensor", "ss")
                        for dc in range(DC):
                            E("tensor", lambda q, dc=dc, tc=tc, ps=ps, pofs=pofs: q.matmul(
                                ps[:, pofs:pofs + 256],
                                big_a[:, dc * 1024 + tc * 128: dc * 1024 + (tc + 1) * 128],
                                kvw[:, dc * 256:(dc + 1) * 256],
                                start=(dc == 0), stop=False))
                        E("tensor", lambda q, dv8=dv8, ps=ps, pofs=pofs: q.matmul(
                            ps[:, pofs:pofs + 256],
                            iotaf[0:1, 0:128],
                            kvw[0:1, 6144 + dv8 * 256: 6144 + (dv8 + 1) * 256],
                            start=False, stop=True), inc="st")
                        Wt("scalar", "st")
                        E("scalar", lambda q, tc=tc, dv8=dv8, ps=ps, pofs=pofs: q.copy(
                            big_b[:, tc * D + dv8 * 256: tc * D + (dv8 + 1) * 256],
                            ps[:, pofs:pofs + 256]), inc="ss")
                Wt("gpsimd", "ss")
                Wt("gpsimd", "lg")
                E("gpsimd", lambda q, sl=sl: q.dma_start(
                    out=bass.AP(V_dram.tensor, sl * TOK * D,
                                [[D, 128], [128 * D, TC], [1, D]]),
                    in_=bass.AP(big_b, 0,
                                [[16 * 1024, 128], [2048, TC], [1, 2048]])),
                  inc="lg", add=16)


        if PHASES >= 2:
            build_tables(None)
            Wt("sync", "sv")
            Wt("sync", "lg")
            b.dma("sync", "ld",
                  bass.AP(big_b, 0, [[16 * 1024, 128], [2048, TC], [1, 2048]]),
                  bass.AP(x_own.tensor, 0, [[D, 128], [128 * D, TC], [1, D]]))
            ln_slab(big_b, big_a)
            if DEBUG:
                Wt("gpsimd", "sv")
                Wt("gpsimd", "lg")
                E("gpsimd", lambda q: q.dma_start(out=X1_dram, in_=big_a[:, :]),
                  inc="lg", add=16)
            for do in range(DC):
                wslot = (do % 2) * 2048
                Wt("sync", "st")
                b.dma("sync", "ld", kvw[:, wslot:wslot + 2048], wcol(Wq, do, 128, D))
                Wt("tensor", "ld")
                Wt("tensor", "ss")
                Wt("tensor", "sv")
                Wt("tensor", "lg", b.cnt["lg"])
                ph = 1024 * (do % 2)
                proj_chunk(PS0, ph, wslot, 1, do)
                Wt("gpsimd", "sv")
                rope_chunk(ph, do, False, obuf[:, :], 0)
                Wt("gpsimd", "sv")
                Wt("gpsimd", "lg")
                E("gpsimd", lambda q, do=do: q.dma_start(
                    out=bass.AP(Q_dram.tensor, do * 128 * TOK, [[TOK, 128], [1, 1024]]),
                    in_=obuf[:, :]), inc="lg", add=16)
                Wt("vector", "lg")

        if PHASES >= 3:
            Wt("sync", "lg")
            for h in range(NH):
                Wt("sync", "st")
                Wt("sync", "ss")
                b.dma("sync", "ld", big_a[:, 0:S],
                      bass.AP(K_dram.tensor, h * 128 * S, [[S, 128], [1, S]]))
                b.dma("sync", "ld", big_a[:, S:2 * S],
                      bass.AP(V_dram.tensor, h * 128,
                              [[D, 128], [128 * D, 64], [1, 128]]))
                b.dma("sync", "ld", qbuf[:, :],
                      bass.AP(Q_dram.tensor, h * 128 * TOK, [[TOK, 128], [1, 1024]]))
                ld_head = b.cnt["ld"]
                Wt("scalar", "ld", ld_head)
                E("scalar", lambda q: q.activation(rtmp[:, :], qbuf[:, :], AF.Square),
                  inc="ss")
                Wt("tensor", "ss")
                Wt("tensor", "sv")
                for hf in range(2):
                    E("tensor", lambda q, hf=hf: q.matmul(
                        PS1[0:1, 1024 + hf * 512: 1024 + (hf + 1) * 512],
                        ones_col[:, 0:1], rtmp[:, hf * 512:(hf + 1) * 512],
                        start=True, stop=True), inc=("st" if hf == 1 else None))
                Wt("vector", "st")
                E("vector", lambda q: q.reduce_max(srows[0:1, 0:1],
                                                   PS1[0:1, 1024:2048], AX.X))
                pad_v(4)
                E("vector", lambda q, h=h: q.tensor_tensor(
                    srows[0:1, 1:2], srows[0:1, 0:1], kmax2[0:1, h:h + 1], ALU.mult),
                  inc="sv")
                Wt("scalar", "sv")
                E("scalar", lambda q: q.activation(srows[0:1, 2:3], srows[0:1, 1:2],
                                                   AF.Sqrt), inc="ss")
                Wt("vector", "ss")
                E("vector", lambda q: q.tensor_scalar(srows[0:1, 3:4], srows[0:1, 2:3],
                                                      -ISQ, None, ALU.mult), inc="sv")
                Wt("gpsimd", "sv")
                b.dma("gpsimd", "lg", sc_dram[0:1, 2:3], srows[0:1, 3:4])
                Wt("gpsimd", "lg")
                b.dma("gpsimd", "lg", km2rep[:, :],
                      sc_dram[0:1, 2:3].partition_broadcast(128))
                lg_negm = b.cnt["lg"]
                Wt("scalar", "lg", lg_negm)
                sv_negm = b.cnt["sv"]
                for qt in range(TC):
                    Wt("tensor", "ld", ld_head)
                    Wt("scalar", "sv")
                    for kc in range(16):
                        Wt("tensor", "ss")
                        E("tensor", lambda q, kc=kc, qt=qt: q.matmul(
                            PS0[:, (kc % 4) * 512:(kc % 4) * 512 + 512],
                            qbuf[:, qt * 128:(qt + 1) * 128],
                            big_a[:, kc * 512:(kc + 1) * 512],
                            start=True, stop=True), inc="st")
                        Wt("scalar", "st")
                        if kc == 0:
                            Wt("scalar", "sv", sv_negm)
                        E("scalar", lambda q, kc=kc, qt=qt: q.activation(
                            big_b[:, kc * 512:(kc + 1) * 512],
                            PS0[:, (kc % 4) * 512:(kc % 4) * 512 + 512], AF.Exp,
                            bias=km2rep[:, 0:1], scale=ISQ,
                            accum_out=sc16[:, kc:kc + 1]), inc="ss")
                    Wt("vector", "ss")
                    Wt("vector", "st")
                    E("vector", lambda q, qt=qt: q.reduce_sum(
                        scols[:, qt:qt + 1], sc16[:, 0:16], AX.X), inc="sv")
                    for batch in range(16):
                        po = 512 * (batch % 2)
                        Wt("tensor", "sv")
                        Wt("tensor", "ss")
                        for j in range(4):
                            ch = batch * 4 + j
                            E("tensor", lambda q, ch=ch, j=j, po=po: q.transpose(
                                PS1[:, po + j * 128: po + (j + 1) * 128],
                                big_b[:, ch * 128:(ch + 1) * 128],
                                ident[:, :]), inc=("st" if j == 3 else None))
                        Wt("vector", "st")
                        E("vector", lambda q, batch=batch, po=po: q.tensor_copy(
                            scr2k[:, (batch % 2) * 512:(batch % 2) * 512 + 512],
                            PS1[:, po:po + 512]), inc="sv")
                        Wt("tensor", "sv")
                        for j in range(4):
                            ch = batch * 4 + j
                            E("tensor", lambda q, ch=ch, j=j, batch=batch: q.matmul(
                                PS1[:, 1536:1664],
                                big_a[:, S + ch * 128: S + (ch + 1) * 128],
                                scr2k[:, (batch % 2) * 512 + j * 128:
                                      (batch % 2) * 512 + (j + 1) * 128],
                                start=(ch == 0), stop=(ch == 63)),
                              inc=("st" if ch == 63 else None))
                    Wt("vector", "st")
                    E("vector", lambda q, qt=qt: q.tensor_copy(
                        obuf[:, qt * 128:(qt + 1) * 128], PS1[:, 1536:1664]), inc="sv")
                Wt("tensor", "sv")
                E("tensor", lambda q: q.transpose(PS1[0:8, 1664:1792], scols[:, 0:8],
                                                  ident[:, :]), inc="st")
                Wt("vector", "st")
                E("vector", lambda q: q.tensor_copy(srows[0:8, :], PS1[0:8, 1664:1792]))
                pad_v(4)
                E("vector", lambda q: q.reciprocal(srows2[0:8, :], srows[0:8, :]),
                  inc="sv")
                Wt("gpsimd", "sv")
                b.dma("gpsimd", "lg", bass.AP(row_dram.tensor, 0, [[128, 8], [1, 128]]),
                      srows2[0:8, :])
                Wt("gpsimd", "lg")
                b.dma("gpsimd", "lg", wrep[:, :], row_dram.partition_broadcast(128))
                Wt("vector", "lg")
                E("vector", lambda q: q.tensor_tensor(obuf[:, :], obuf[:, :],
                                                      wrep[:, :], ALU.mult), inc="sv")
                Wt("gpsimd", "sv")
                Wt("gpsimd", "lg")
                E("gpsimd", lambda q, h=h: q.dma_start(
                    out=bass.AP(O_dram.tensor, h * 128 * TOK, [[TOK, 128], [1, 1024]]),
                    in_=obuf[:, :]), inc="lg", add=16)
                Wt("vector", "lg")

        if PHASES >= 4:
            Wt("sync", "lg")
            Wt("sync", "st")
            Wt("sync", "ss")
            b.dma("sync", "ld", big_b[:, :],
                  bass.AP(O_dram.tensor, 0, [[TOK, 128], [128 * TOK, DC], [1, 1024]]))
            for do in range(DC):
                wslot = (do % 2) * 2048
                Wt("sync", "st")
                b.dma("sync", "ld", kvw[:, wslot:wslot + 2048], wcol(Wo, do, 128, D))
                Wt("tensor", "ld")
                Wt("tensor", "ss")
                ph = 1024 * (do % 2)
                for dc in range(DC):
                    for hf in range(2):
                        E("tensor", lambda q, dc=dc, hf=hf, ph=ph, w=wslot: q.matmul(
                            PS0[:, ph + hf * 512: ph + (hf + 1) * 512],
                            kvw[:, w + dc * 128: w + (dc + 1) * 128],
                            big_b[:, dc * 1024 + hf * 512: dc * 1024 + (hf + 1) * 512],
                            start=(dc == 0), stop=(dc == DC - 1)),
                          inc=("st" if dc == DC - 1 and hf == 1 else None))
                Wt("vector", "st")
                E("vector", lambda q, ph=ph, do=do: q.tensor_scalar(
                    ktmp[:, :], PS0[:, ph:ph + 1024], qkob[:, 32 + do:33 + do], None,
                    ALU.add), inc="sv")
                transpose_chunks(big_a, do * 128, 2048,
                                 lambda c: ktmp[:, c * 128:(c + 1) * 128], TC)
            for tc in range(TC):
                Wt("sync", "sv")
                b.dma("sync", "ld", kvw[:, 0:2048], x_own[tc * 128:(tc + 1) * 128, :])
                Wt("vector", "ld")
                E("vector", lambda q, tc=tc: q.tensor_tensor(
                    big_a[:, tc * 2048:(tc + 1) * 2048],
                    big_a[:, tc * 2048:(tc + 1) * 2048], kvw[:, 0:2048], ALU.add),
                  inc="sv")
            Wt("gpsimd", "sv")
            Wt("gpsimd", "lg")
            E("gpsimd", lambda q: q.dma_start(
                out=bass.AP(XP_dram.tensor, 0, [[D, 128], [128 * D, TC], [1, D]]),
                in_=bass.AP(big_a, 0, [[16 * 1024, 128], [2048, TC], [1, 2048]])),
              inc="lg", add=16)
            ln_slab(big_a, big_b)

        if PHASES >= 5:
            def g3(t):
                return bass.AP(t, 0, [[64, 128], [8, 8], [1, 8]])

            def gcE(t):
                return bass.AP(t, 0, [[8, 128], [1, 8], [0, 8]])

            def gcT(t):
                return bass.AP(t, 0, [[8, 128], [0, 8], [1, 8]])

            def gro(t):
                return bass.AP(t, 0, [[8, 128], [1, 8], [1, 1]])

            for tc in range(TC):
                po = 512 * (tc % 2)
                Wt("tensor", "sv")
                Wt("tensor", "ss")
                for dc in range(DC):
                    E("tensor", lambda q, dc=dc, tc=tc, po=po: q.matmul(
                        PS1[:, po:po + 8],
                        big_b[:, dc * 1024 + tc * 128: dc * 1024 + (tc + 1) * 128],
                        Wr_sb[:, dc * 8:(dc + 1) * 8],
                        start=(dc == 0), stop=(dc == DC - 1)),
                      inc=("st" if dc == DC - 1 else None))
                Wt("scalar", "st")
                Wt("scalar", "lg")
                E("scalar", lambda q, po=po, tc=tc: q.activation(
                    g["inh"][:, tc * 8:(tc + 1) * 8], PS1[:, po:po + 8], AF.Sigmoid,
                    scale=invt_col[:, 0:1]), inc="ss")
            Wt("vector", "ss")
            E("vector", lambda q: q.tensor_scalar(g["t1"][:, :], g["inh"][:, :],
                                                  GL, None, ALU.is_ge))
            E("vector", lambda q: q.tensor_scalar(g["t2"][:, :], g["inh"][:, :],
                                                  GU, None, ALU.is_le))
            pad_v(2)
            E("vector", lambda q: q.tensor_tensor(g["t2"][:, :], g["t1"][:, :],
                                                  g["t2"][:, :], ALU.mult), inc="sv")
            pad_s(2)
            E("scalar", lambda q: q.activation(g["dist"][:, :], g["inh"][:, :],
                                               AF.Abs, bias=gc_col[:, 0:1]))
            pad_s(4)
            Wt("scalar", "sv")
            E("scalar", lambda q: q.activation(g["w"][:, :], g["dist"][:, :],
                                               AF.Exp, scale=-10.0))
            E("scalar", lambda q: q.activation(g["fb"][:, :], g["dist"][:, :],
                                               AF.Exp, scale=-10.0 / 3.0), inc="ss")
            Wt("vector", "ss")
            pad_v(3)
            E("vector", lambda q: q.tensor_tensor(g["w"][:, :], g["w"][:, :],
                                                  g["t2"][:, :], ALU.mult))
            pad_v(3)
            E("vector", lambda q: q.tensor_reduce(gro(gc_["a"]), g3(g["w"]),
                                                  AX.X, ALU.add))
            E("vector", lambda q: q.reduce_max(gro(gc_["c"]), g3(g["fb"]), AX.X))
            pad_v(3)
            E("vector", lambda q: q.tensor_scalar(gc_["b"][:, :], gc_["a"][:, :],
                                                  1e-8, None, ALU.is_lt))
            E("vector", lambda q: q.tensor_tensor(g3(g["t1"]), g3(g["fb"]),
                                                  gcE(gc_["c"]), ALU.is_equal))
            pad_v(3)
            E("vector", lambda q: q.tensor_tensor(g3(g["t1"]), g3(g["t1"]),
                                                  gcT(gi8m), ALU.mult))
            pad_v(4)
            E("vector", lambda q: q.tensor_reduce(gro(gc_["d"]), g3(g["t1"]),
                                                  AX.X, ALU.min))
            pad_v(4)
            E("vector", lambda q: q.tensor_tensor(g3(g["eqi1"]), gcT(gi8m),
                                                  gcE(gc_["d"]), ALU.is_equal))
            pad_v(4)
            E("vector", lambda q: q.scalar_tensor_tensor(g["fb2"][:, :],
                                                         g["eqi1"][:, :], -1e30,
                                                         g["fb"][:, :], ALU.mult,
                                                         ALU.add))
            pad_v(4)
            E("vector", lambda q: q.reduce_max(gro(gc_["c"]), g3(g["fb2"]), AX.X))
            pad_v(4)
            E("vector", lambda q: q.tensor_tensor(g3(g["t1"]), g3(g["fb2"]),
                                                  gcE(gc_["c"]), ALU.is_equal))
            pad_v(3)
            E("vector", lambda q: q.tensor_tensor(g3(g["t1"]), g3(g["t1"]),
                                                  gcT(gi8m), ALU.mult))
            pad_v(4)
            E("vector", lambda q: q.tensor_reduce(gro(gc_["d"]), g3(g["t1"]),
                                                  AX.X, ALU.min))
            pad_v(4)
            E("vector", lambda q: q.tensor_tensor(g3(g["eqi2"]), gcT(gi8m),
                                                  gcE(gc_["d"]), ALU.is_equal))
            pad_v(4)
            E("vector", lambda q: q.tensor_tensor(g["t1"][:, :], g["eqi1"][:, :],
                                                  g["eqi2"][:, :], ALU.add))
            pad_v(4)
            E("vector", lambda q: q.tensor_tensor(g["t1"][:, :], g["fb"][:, :],
                                                  g["t1"][:, :], ALU.mult))
            pad_v(4)
            E("vector", lambda q: q.tensor_reduce(gro(gc_["e"]), g3(g["t1"]),
                                                  AX.X, ALU.add))
            pad_v(4)
            E("vector", lambda q: q.tensor_scalar(gc_["c"][:, :], gc_["e"][:, :],
                                                  1e-8, None, ALU.max))
            pad_v(4)
            E("vector", lambda q: q.reciprocal(gc_["d"][:, :], gc_["c"][:, :]))
            pad_v(4)
            E("vector", lambda q: q.tensor_tensor(g3(g["t1"]), g3(g["t1"]),
                                                  gcE(gc_["d"]), ALU.mult))
            pad_v(4)
            E("vector", lambda q: q.tensor_tensor(g["t2"][:, :], g["t1"][:, :],
                                                  g["w"][:, :], ALU.subtract))
            pad_v(3)
            E("vector", lambda q: q.tensor_tensor(g3(g["t2"]), g3(g["t2"]),
                                                  gcE(gc_["b"]), ALU.mult))
            pad_v(4)
            E("vector", lambda q: q.tensor_tensor(g["w"][:, :], g["w"][:, :],
                                                  g["t2"][:, :], ALU.add))
            pad_v(4)
            E("vector", lambda q: q.tensor_reduce(gro(gc_["a"]), g3(g["w"]),
                                                  AX.X, ALU.add))
            pad_v(4)
            E("vector", lambda q: q.tensor_scalar(gc_["c"][:, :], gc_["a"][:, :],
                                                  1e-8, None, ALU.max))
            pad_v(4)
            E("vector", lambda q: q.reciprocal(gc_["d"][:, :], gc_["c"][:, :]))
            pad_v(4)
            E("vector", lambda q: q.tensor_tensor(g3(g["w"]), g3(g["w"]),
                                                  gcE(gc_["d"]), ALU.mult), inc="sv")
            for tc in range(TC):
                Wt("tensor", "sv")
                E("tensor", lambda q, tc=tc: q.transpose(
                    PS1[0:8, 1536:1664], g["w"][:, tc * 8:(tc + 1) * 8],
                    ident[:, :]), inc="st")
                Wt("vector", "st")
                E("vector", lambda q, tc=tc: q.tensor_copy(
                    qbuf[0:8, tc * 128:(tc + 1) * 128], PS1[0:8, 1536:1664]), inc="sv")
            if DEBUG:
                Wt("gpsimd", "sv")
                Wt("gpsimd", "lg")
                E("gpsimd", lambda q: q.dma_start(out=GW_dram, in_=qbuf[0:8, :]),
                  inc="lg", add=16)
            for e in range(NE):
                Wt("gpsimd", "sv")
                Wt("gpsimd", "ss")
                b.dma("gpsimd", "lg", wrow_dram, qbuf[e:e + 1, :])
                Wt("gpsimd", "lg")
                b.dma("gpsimd", "lg", wrep[:, :], row_dram.partition_broadcast(128)
                      if False else wrow_dram.partition_broadcast(128))
                lg_w = b.cnt["lg"]
                for fc in range(DC):
                    Wt("sync", "st")
                    b.dma("sync", "ld", kvw[:, 0:2048],
                          wcol(bass.AP(Wg.tensor, e * D * FF, [[1, 1]]), fc, 128, FF))
                    b.dma("sync", "ld", kvw[:, 2048:4096],
                          wcol(bass.AP(Wu.tensor, e * D * FF, [[1, 1]]), fc, 128, FF))
                    Wt("tensor", "ld")
                    Wt("tensor", "ss")
                    Wt("tensor", "sv")
                    for dc in range(DC):
                        for hf in range(2):
                            E("tensor", lambda q, dc=dc, hf=hf: q.matmul(
                                PS0[:, hf * 512:(hf + 1) * 512],
                                kvw[:, dc * 128:(dc + 1) * 128],
                                big_b[:, dc * 1024 + hf * 512: dc * 1024 + (hf + 1) * 512],
                                start=(dc == 0), stop=(dc == DC - 1)))
                    for dc in range(DC):
                        for hf in range(2):
                            E("tensor", lambda q, dc=dc, hf=hf: q.matmul(
                                PS0[:, 1024 + hf * 512: 1024 + (hf + 1) * 512],
                                kvw[:, 2048 + dc * 128: 2048 + (dc + 1) * 128],
                                big_b[:, dc * 1024 + hf * 512: dc * 1024 + (hf + 1) * 512],
                                start=(dc == 0), stop=(dc == DC - 1)),
                              inc=("st" if dc == DC - 1 and hf == 1 else None))
                    Wt("scalar", "st")
                    E("scalar", lambda q: q.activation(ktmp[:, :], PS0[:, 0:1024],
                                                       AF.Silu), inc="ss")
                    Wt("vector", "ss")
                    Wt("vector", "lg", lg_w)
                    E("vector", lambda q: q.tensor_tensor(rtmp[:, :], ktmp[:, :],
                                                          PS0[:, 1024:2048], ALU.mult))
                    E("vector", lambda q, fc=fc: q.tensor_tensor(
                        big_a[:, fc * 1024:(fc + 1) * 1024], rtmp[:, :], wrep[:, :],
                        ALU.mult), inc="sv")
                for do in range(DC):
                    Wt("sync", "st")
                    b.dma("sync", "ld", kvw[:, 0:2048],
                          wcol(bass.AP(Wd.tensor, e * FF * D, [[1, 1]]), do, 128, D))
                    Wt("tensor", "ld")
                    Wt("tensor", "sv")
                    ph = 1024 * (do % 2)
                    Wt("tensor", "ss")
                    for fc in range(DC):
                        for hf in range(2):
                            E("tensor", lambda q, fc=fc, hf=hf, ph=ph: q.matmul(
                                PS1[:, ph + hf * 512: ph + (hf + 1) * 512],
                                kvw[:, fc * 128:(fc + 1) * 128],
                                big_a[:, fc * 1024 + hf * 512: fc * 1024 + (hf + 1) * 512],
                                start=(fc == 0), stop=(fc == DC - 1)),
                              inc=("st" if fc == DC - 1 and hf == 1 else None))
                    Wt("scalar", "st")
                    Wt("scalar", "lg")
                    bo_ = 1024 * (do % 2)
                    E("scalar", lambda q, ph=ph, bo_=bo_: q.copy(
                        scr2k[:, bo_:bo_ + 1024], PS1[:, ph:ph + 1024]), inc="ss")
                    Wt("gpsimd", "ss")
                    kwacc = {} if e == 0 else dict(accum_op=ALU.add)
                    Wt("gpsimd", "lg")
                    E("gpsimd", lambda q, do=do, bo_=bo_, kw=kwacc: q.dma_start(
                        out=bass.AP(FFN_dram.tensor, do * 128 * TOK,
                                    [[TOK, 128], [1, 1024]]),
                        in_=scr2k[:, bo_:bo_ + 1024], **kw), inc="lg", add=16)

        if PHASES >= 6:
            Wt("sync", "lg")
            for tc in range(TC):
                Wt("sync", "st")
                Wt("sync", "sv")
                b.dma("sync", "ld", kvw[:, 0:2048],
                      bass.AP(FFN_dram.tensor, tc * 128,
                              [[TOK, 128], [128 * TOK, DC], [1, 128]]))
                b.dma("sync", "ld", kvw[:, 2048:4096],
                      bass.AP(XP_dram.tensor, tc * 128 * D, [[D, 128], [1, D]]))
                Wt("tensor", "ld")
                Wt("tensor", "sv")
                ps = PS0 if tc % 2 == 0 else PS1
                for dc in range(DC):
                    E("tensor", lambda q, dc=dc, ps=ps: q.transpose(
                        ps[:, dc * 128:(dc + 1) * 128],
                        kvw[:, dc * 128:(dc + 1) * 128], ident[:, :]),
                      inc=("st" if dc == DC - 1 else None))
                Wt("vector", "st")
                Wt("vector", "out")
                ao = 2048 * (tc % 2)
                E("vector", lambda q, ps=ps, ao=ao: q.tensor_tensor(
                    big_a[:, ao:ao + 2048], kvw[:, 2048:4096], ps[:, 0:2048], ALU.add),
                  inc="sv")
                Wt("sync", "sv")
                b.dma("sync", "out", out_own[tc * 128:(tc + 1) * 128, :],
                      big_a[:, ao:ao + 2048])

        Wt("sync", "out")
        Wt("sync", "lg")
        Wt("gpsimd", "lg")
        Wt("gpsimd", "sg")

        with nc.Block() as block:
            for eng in ENGS:
                def mk_run(eng):
                    def run(q):
                        for fn in b.prog[eng]:
                            fn(q)
                    return run
                getattr(block, eng)(mk_run(eng))

    return nc


_cached = {}


def kernel(L0_attn_hidden_state, L0_attn_Wq, L0_attn_bq, L0_attn_Wk, L0_attn_bk,
           L0_attn_Wv, L0_attn_bv, L0_attn_Wo, L0_attn_bo, L0_ffn_Wg, L0_ffn_Wu,
           L0_ffn_Wd, L0_ffn_Wr, L0_ffn_temperature):
    x = np.ascontiguousarray(np.asarray(L0_attn_hidden_state, dtype=np.float32)
                             .reshape(S, D))
    if "nc" not in _cached:
        _cached["nc"] = build_program()
    nc = _cached["nc"]
    common = {
        "x_full": x,
        "Wq": np.ascontiguousarray(L0_attn_Wq, dtype=np.float32),
        "Wk": np.ascontiguousarray(L0_attn_Wk, dtype=np.float32),
        "Wv": np.ascontiguousarray(L0_attn_Wv, dtype=np.float32),
        "Wo": np.ascontiguousarray(L0_attn_Wo, dtype=np.float32),
        "bq": np.ascontiguousarray(L0_attn_bq, dtype=np.float32).reshape(1, D),
        "bk": np.ascontiguousarray(L0_attn_bk, dtype=np.float32).reshape(1, D),
        "bv": np.ascontiguousarray(L0_attn_bv, dtype=np.float32).reshape(1, D),
        "bo": np.ascontiguousarray(L0_attn_bo, dtype=np.float32).reshape(1, D),
        "Wg": np.ascontiguousarray(L0_ffn_Wg, dtype=np.float32),
        "Wu": np.ascontiguousarray(L0_ffn_Wu, dtype=np.float32),
        "Wd": np.ascontiguousarray(L0_ffn_Wd, dtype=np.float32),
        "Wr": np.ascontiguousarray(L0_ffn_Wr, dtype=np.float32),
        "temp": np.ascontiguousarray(L0_ffn_temperature,
                                     dtype=np.float32).reshape(1, 1),
    }
    in_maps = []
    for c in range(8):
        m = dict(common)
        m["x_own"] = np.ascontiguousarray(x[c * TOK:(c + 1) * TOK])
        m["core_ofs"] = np.asarray([[float(c * TOK)]], dtype=np.float32)
        in_maps.append(m)
    res = run_bass_kernel_spmd(nc, in_maps, core_ids=list(range(8)))
    out = np.concatenate([r["out_own"] for r in res.results], axis=0)
    return out.reshape(1, S, D).astype(np.float32)


# version 05
# speedup vs baseline: 1.0093x; 1.0012x over previous
import math
from contextlib import ExitStack

import numpy as np
import concourse.bass as bass
import concourse.mybir as mybir
from concourse.bass_utils import run_bass_kernel_spmd

F32 = mybir.dt.float32
I32 = mybir.dt.int32
AF = mybir.ActivationFunctionType
ALU = mybir.AluOpType
AX = mybir.AxisListType

S, D, NH, HD = 8192, 2048, 16, 128
TOK = 1024
DC = D // 128
TC = TOK // 128
NSLAB = S // TOK
NE, FF = 8, 2048

PI = float(np.pi)
TWOPI = 2 * math.pi
TWOPI_HI = float(np.float32(TWOPI))
TWOPI_LO = float(TWOPI - np.float64(np.float32(TWOPI)))
ISQ = 1.0 / math.sqrt(HD)
LN1E4_64 = math.log(10000.0) / 64.0
GC = 1.0 / math.e
GL = 0.5 - math.log(4.0 / 3.0)
GU = 0.5
EPS_LN = 1e-5

ENGS = ("sync", "gpsimd", "vector", "scalar", "tensor")
DEBUG = False
PHASES = 9


class B:
    def __init__(self, nc):
        self.nc = nc
        self.prog = {e: [] for e in ENGS}
        self.sems = {}
        self.cnt = {}

    def sem(self, ctx, name):
        self.sems[name] = ctx.enter_context(self.nc.semaphore(name))
        self.cnt[name] = 0

    def emit(self, eng, fn, inc=None, add=1):
        if eng == "vector":
            inc = inc or "sv"
            self.wait(eng, "sv")
        elif eng == "scalar":
            inc = inc or "ss"
            self.wait(eng, "ss")
        if inc is None:
            self.prog[eng].append(fn)
            return None
        sh = self.sems[inc]
        self.cnt[inc] += add
        self.prog[eng].append(lambda q, fn=fn, sh=sh, add=add: fn(q).then_inc(sh, add))
        return self.cnt[inc]

    def wait(self, eng, sem, val=None):
        if val is None:
            val = self.cnt[sem]
        if val <= 0:
            return
        sh = self.sems[sem]
        self.prog[eng].append(lambda q, sh=sh, val=val: q.wait_ge(sh, val))

    def dma(self, eng, sem, out, in_, **kw):
        self.wait(eng, sem)
        return self.emit(eng, lambda q, o=out, i=in_, kw=kw: q.dma_start(out=o, in_=i, **kw),
                         inc=sem, add=16)


def build_program():
    nc = bass.Bass("TRN2", target_bir_lowering=False)
    b = B(nc)

    x_own = nc.dram_tensor("x_own", [TOK, D], F32, kind="ExternalInput").ap()
    x_full = nc.dram_tensor("x_full", [S, D], F32, kind="ExternalInput").ap()
    Wq = nc.dram_tensor("Wq", [D, D], F32, kind="ExternalInput").ap()
    Wk = nc.dram_tensor("Wk", [D, D], F32, kind="ExternalInput").ap()
    Wv = nc.dram_tensor("Wv", [D, D], F32, kind="ExternalInput").ap()
    Wo = nc.dram_tensor("Wo", [D, D], F32, kind="ExternalInput").ap()
    bq = nc.dram_tensor("bq", [1, D], F32, kind="ExternalInput").ap()
    bk = nc.dram_tensor("bk", [1, D], F32, kind="ExternalInput").ap()
    bv = nc.dram_tensor("bv", [1, D], F32, kind="ExternalInput").ap()
    bo = nc.dram_tensor("bo", [1, D], F32, kind="ExternalInput").ap()
    Wg = nc.dram_tensor("Wg", [NE, D, FF], F32, kind="ExternalInput").ap()
    Wu = nc.dram_tensor("Wu", [NE, D, FF], F32, kind="ExternalInput").ap()
    Wd = nc.dram_tensor("Wd", [NE, FF, D], F32, kind="ExternalInput").ap()
    Wr = nc.dram_tensor("Wr", [D, NE], F32, kind="ExternalInput").ap()
    temp = nc.dram_tensor("temp", [1, 1], F32, kind="ExternalInput").ap()
    core_ofs = nc.dram_tensor("core_ofs", [1, 1], F32, kind="ExternalInput").ap()
    out_own = nc.dram_tensor("out_own", [TOK, D], F32, kind="ExternalOutput").ap()

    skind = dict(kind="ExternalOutput") if DEBUG else {}
    K_dram = nc.dram_tensor("K_scr", [D, S], F32, **skind).ap()
    V_dram = nc.dram_tensor("V_scr", [S, D], F32, **skind).ap()
    Q_dram = nc.dram_tensor("Q_scr", [D, TOK], F32, **skind).ap()
    O_dram = nc.dram_tensor("O_scr", [D, TOK], F32, **skind).ap()
    XP_dram = nc.dram_tensor("XP_scr", [TOK, D], F32, **skind).ap()
    FFN_dram = nc.dram_tensor("FFN_scr", [D, TOK], F32, **skind).ap()
    GW_dram = nc.dram_tensor("GW_scr", [8, TOK], F32, **skind).ap()
    X1_dram = nc.dram_tensor("X1_scr", [128, 16 * 1024], F32, **skind).ap()
    row_dram = nc.dram_tensor("row_scr", [1, TOK], F32).ap()
    wrow_dram = nc.dram_tensor("wrow_scr", [1, TOK], F32).ap()
    sc_dram = nc.dram_tensor("sc_scr", [1, 8], F32).ap()

    with ExitStack() as ctx:
        en = ctx.enter_context
        big_a = en(nc.sbuf_tensor("big_a", [128, 16 * 1024], F32))
        big_b = en(nc.sbuf_tensor("big_b", [128, 16 * 1024], F32))
        kvw = en(nc.sbuf_tensor("kvw", [128, 8192], F32))
        qbuf = en(nc.sbuf_tensor("qbuf", [128, TOK], F32))
        obuf = en(nc.sbuf_tensor("obuf", [128, TOK], F32))
        scr2k = en(nc.sbuf_tensor("scr2k", [128, 2048], F32))
        ktmp = en(nc.sbuf_tensor("ktmp", [128, 1024], F32))
        rtmp = en(nc.sbuf_tensor("rtmp", [128, 1024], F32))
        wrep = en(nc.sbuf_tensor("wrep", [128, 1024], F32))
        qkob = en(nc.sbuf_tensor("qkob", [128, 48], F32))
        ident = en(nc.sbuf_tensor("ident", [128, 128], F32))
        swapm = en(nc.sbuf_tensor("swapm", [128, 128], F32))
        iotaf = en(nc.sbuf_tensor("iotaf", [128, 128], F32))
        Wr_sb = en(nc.sbuf_tensor("Wr_sb", [128, DC * NE], F32))
        cols = en(nc.sbuf_tensor("cols", [128, 16], F32))
        sc16 = en(nc.sbuf_tensor("sc16", [128, 16], F32))
        scols = en(nc.sbuf_tensor("scols", [128, 8], F32))
        qn2c = en(nc.sbuf_tensor("qn2c", [128, 8], F32))
        negm = en(nc.sbuf_tensor("negm", [128, 8], F32))
        negm2 = en(nc.sbuf_tensor("negm2", [128, 8], F32))
        srows = en(nc.sbuf_tensor("srows", [8, 128], F32))
        kmax2 = en(nc.sbuf_tensor("kmax2", [1, 16], F32))
        iotap = en(nc.sbuf_tensor("iotap", [128, 1], F32))
        iotap64 = en(nc.sbuf_tensor("iotap64", [64, 1], F32))
        invf64 = en(nc.sbuf_tensor("invf64", [64, 1], F32))
        invt_col = en(nc.sbuf_tensor("invt_col", [128, 1], F32))
        cofs_col = en(nc.sbuf_tensor("cofs_col", [128, 1], F32))
        km2rep = en(nc.sbuf_tensor("km2rep", [128, 1], F32))
        invf128 = en(nc.sbuf_tensor("invf128", [128, 1], F32))
        sign_col = en(nc.sbuf_tensor("sign_col", [128, 1], F32))
        swapidx = en(nc.sbuf_tensor("swapidx", [128, 1], F32))
        sgn2 = en(nc.sbuf_tensor("sgn2", [1, 2], F32))
        ones_col = en(nc.sbuf_tensor("ones_col", [128, 1], F32))
        eps_col = en(nc.sbuf_tensor("eps_col", [128, 1], F32))
        gc_col = en(nc.sbuf_tensor("gc_col", [128, 1], F32))
        g = {nm: en(nc.sbuf_tensor("g_" + nm, [128, 64], F32))
             for nm in ("inh", "t1", "t2", "dist", "w", "fb", "fb2",
                        "eqi1", "eqi2")}
        gi8 = en(nc.sbuf_tensor("gi8", [128, 8], F32))
        gi8m = en(nc.sbuf_tensor("gi8m", [128, 8], F32))
        gc_ = {nm: en(nc.sbuf_tensor("gc_" + nm, [128, 8], F32))
               for nm in ("a", "b", "c", "d", "e", "f")}
        jv1 = en(nc.sbuf_tensor("jv1", [128, 32], F32))
        jv2 = en(nc.sbuf_tensor("jv2", [128, 8], F32))
        js1 = en(nc.sbuf_tensor("js1", [128, 32], F32))
        js2 = en(nc.sbuf_tensor("js2", [128, 8], F32))
        srows2 = en(nc.sbuf_tensor("srows2", [8, 128], F32))
        PS0 = en(nc.psum_tensor("PS0", [128, 2048], F32))
        PS1 = en(nc.psum_tensor("PS1", [128, 2048], F32))

        for s_ in ("ld", "lg", "sv", "ss", "st", "sg", "out"):
            b.sem(ctx, s_)
        E, Wt = b.emit, b.wait

        def pad_v(n=4):
            pass

        def pad_s(n=4):
            pass


        SIN_OFS, COS_OFS = 4096, 5120
        TB1, TB2, TB3, TBI = 0, 1024, 2048, 3072

        def bb64(col, width=1024):
            return bass.AP(big_b, col, [[16 * 1024, 64], [1, width]])

        sin2s = kvw[:, SIN_OFS:SIN_OFS + 1024]
        cos2 = kvw[:, COS_OFS:COS_OFS + 1024]

        E("gpsimd", lambda q: q.iota(iotap[:, :], [[0, 1]], channel_multiplier=1,
                                     allow_small_or_imprecise_dtypes=True))
        E("gpsimd", lambda q: q.iota(iotap64[:, :], [[0, 1]], channel_multiplier=1,
                                     allow_small_or_imprecise_dtypes=True))
        E("gpsimd", lambda q: q.iota(iotaf[:, :], [[1, 128]], channel_multiplier=0,
                                     allow_small_or_imprecise_dtypes=True))
        E("gpsimd", lambda q: q.iota(gi8[:, :], [[1, 8]], channel_multiplier=0,
                                     allow_small_or_imprecise_dtypes=True))
        E("gpsimd", lambda q: q.memset(jv2[:, :], 0.0))
        E("gpsimd", lambda q: q.memset(sgn2[0:1, 0:1], -1.0))
        E("gpsimd", lambda q: q.memset(sgn2[0:1, 1:2], 1.0))
        E("gpsimd", lambda q: q.memset(js2[:, :], 0.0))
        E("gpsimd", lambda q: q.memset(ones_col[:, :], 1.0))
        E("gpsimd", lambda q: q.memset(eps_col[:, :], EPS_LN))
        E("gpsimd", lambda q: q.memset(gc_col[:, :], -GC))
        E("gpsimd", lambda q: q.memset(kmax2[:, :], 0.0), inc="sg")
        sg_c0 = b.cnt["sg"]

        b.dma("sync", "ld", kvw[0:1, 6144:8192], bv)
        b.dma("sync", "ld", Wr_sb[:, :],
              bass.AP(Wr.tensor, 0, [[NE, 128], [128 * NE, DC], [1, NE]]))
        b.dma("sync", "ld", cofs_col[:, :], core_ofs.partition_broadcast(128))
        b.dma("sync", "ld", cols[0:1, 14:15], temp)
        ld_c0 = b.cnt["ld"]

        Wt("vector", "ld", ld_c0)
        E("vector", lambda q: q.reciprocal(cols[0:1, 15:16], cols[0:1, 14:15]), inc="sv")
        Wt("gpsimd", "sv")
        b.dma("gpsimd", "lg", sc_dram[0:1, 0:1], cols[0:1, 15:16])
        Wt("gpsimd", "lg")
        b.dma("gpsimd", "lg", invt_col[:, :], sc_dram[0:1, 0:1].partition_broadcast(128))

        Wt("scalar", "sg", sg_c0)
        E("scalar", lambda q: q.activation(invf64[:, :], iotap64[:, :], AF.Exp,
                                           scale=-LN1E4_64), inc="ss")
        ss_invf = b.cnt["ss"]
        Wt("vector", "sg", sg_c0)
        E("vector", lambda q: q.tensor_scalar(ident[:, :], iotaf[:, :], iotap[:, 0:1],
                                              None, ALU.is_equal))
        E("vector", lambda q: q.tensor_scalar(gi8m[:, :], gi8[:, :], -8.0,
                                              None, ALU.add), inc="sv")
        Wt("gpsimd", "ss")
        b.dma("gpsimd", "lg", row_dram[0:1, 0:64], invf64[:, :])
        b.dma("gpsimd", "lg", wrow_dram[0:1, 0:2], sgn2[0:1, :])
        b.dma("gpsimd", "lg", invf128[:, :],
              bass.AP(row_dram.tensor, 0, [[1, 64], [0, 2]]))
        b.dma("gpsimd", "lg", sign_col[:, :],
              bass.AP(wrow_dram.tensor, 0, [[0, 64], [1, 2]]))
        lg_c0 = b.cnt["lg"]
        Wt("vector", "sg", sg_c0)
        Wt("vector", "lg", lg_c0)
        E("vector", lambda q: q.tensor_tensor(swapidx[:, :], iotap[:, :],
                                              sign_col[:, :], ALU.subtract))
        pad_v(4)
        E("vector", lambda q: q.tensor_scalar(swapm[:, :], iotaf[:, :],
                                              swapidx[:, 0:1], None, ALU.is_equal),
          inc="sv")
        for bi, bsrc in enumerate((bq, bk, bo)):
            Wt("sync", "st")
            b.dma("sync", "ld", scr2k[0:16, 0:128],
                  bass.AP(bsrc.tensor, 0, [[128, 16], [1, 128]]))
            Wt("tensor", "ld")
            Wt("tensor", "sv")
            E("tensor", lambda q, bi=bi: q.transpose(
                PS1[:, bi * 16:(bi + 1) * 16],
                scr2k[0:16, 0:128],
                ident[0:16, 0:16]), inc="st")
        Wt("vector", "st")
        E("vector", lambda q: q.tensor_copy(qkob[:, 0:48], PS1[:, 0:48]), inc="sv")
        Wt("gpsimd", "st")
        E("gpsimd", lambda q: q.memset(iotaf[0:1, :], 1.0), inc="sg")
        sg_ones = b.cnt["sg"]

        def bb128(col, width=1024):
            return bass.AP(big_b, col, [[16 * 1024, 128], [1, width]])

        def build_tables(ofs_imm):
            Wt("gpsimd", "st")
            Wt("gpsimd", "sv")
            Wt("gpsimd", "lg")
            E("gpsimd", lambda q: q.iota(bb128(TBI), [[1, 1024]], channel_multiplier=0,
                                         allow_small_or_imprecise_dtypes=True), inc="sg")
            Wt("vector", "sg")
            Wt("vector", "lg")
            Wt("vector", "ss", ss_invf)
            if ofs_imm is None:
                Wt("vector", "ld", ld_c0)
                E("vector", lambda q: q.tensor_scalar(bb128(TB1), bb128(TBI),
                                                      cofs_col[:, 0:1], None, ALU.add))
            else:
                E("vector", lambda q, o=float(ofs_imm): q.tensor_scalar(
                    bb128(TB1), bb128(TBI), o, None, ALU.add))
            E("vector", lambda q: q.tensor_scalar(bb128(TB1), bb128(TB1),
                                                  invf128[:, 0:1], None, ALU.mult))
            E("vector", lambda q: q.tensor_scalar(bb128(TB2), bb128(TB1), 1.0 / TWOPI,
                                                  None, ALU.mult))
            E("vector", lambda q: q.tensor_copy(bb128(TBI).bitcast(I32), bb128(TB2)))
            E("vector", lambda q: q.tensor_copy(bb128(TB2), bb128(TBI).bitcast(I32)))
            E("vector", lambda q: q.scalar_tensor_tensor(bb128(TB1), bb128(TB2),
                                                         -TWOPI_HI, bb128(TB1),
                                                         ALU.mult, ALU.add))
            E("vector", lambda q: q.scalar_tensor_tensor(bb128(TB1), bb128(TB2),
                                                         -TWOPI_LO, bb128(TB1),
                                                         ALU.mult, ALU.add))
            E("vector", lambda q: q.tensor_scalar(bb128(TB3), bb128(TB1), PI, None,
                                                  ALU.is_gt))
            E("vector", lambda q: q.scalar_tensor_tensor(bb128(TB1), bb128(TB3),
                                                         -TWOPI_HI, bb128(TB1),
                                                         ALU.mult, ALU.add))
            E("vector", lambda q: q.tensor_scalar(bb128(TB3), bb128(TB1), -PI, None,
                                                  ALU.is_lt))
            E("vector", lambda q: q.scalar_tensor_tensor(bb128(TB1), bb128(TB3),
                                                         TWOPI_HI, bb128(TB1),
                                                         ALU.mult, ALU.add), inc="sv")
            Wt("scalar", "sv")
            E("scalar", lambda q: q.activation(bb128(TB2), bb128(TB1), AF.Sin))
            E("scalar", lambda q: q.activation(bb128(TB3), bb128(TB1), AF.Sin,
                                               scale=0.5), inc="ss")
            Wt("vector", "ss")
            E("vector", lambda q: q.tensor_scalar(sin2s, bb128(TB2), sign_col[:, 0:1],
                                                  None, ALU.mult))
            E("vector", lambda q: q.tensor_tensor(bb128(TB3), bb128(TB3), bb128(TB3),
                                                  ALU.mult))
            E("vector", lambda q: q.tensor_scalar(cos2, bb128(TB3), -2.0, 1.0,
                                                  ALU.mult, ALU.add), inc="sv")
            return b.cnt["sv"]

        def ln_slab(src_handle, dst_handle):
            Wt("vector", "ld")
            Wt("vector", "lg")
            Wt("scalar", "ld")
            Wt("scalar", "lg")
            Wt("scalar", "st")
            for tc in range(TC):
                E("vector", lambda q, tc=tc: q.reduce_sum(
                    cols[:, tc:tc + 1],
                    src_handle[:, tc * 2048:(tc + 1) * 2048], AX.X), inc="sv")
                E("scalar", lambda q, tc=tc: q.activation(
                    PS0[:, 0:2048], src_handle[:, tc * 2048:(tc + 1) * 2048],
                    AF.Square, accum_out=cols[:, 8 + tc:9 + tc]), inc="ss")
            Wt("vector", "ss")
            pad_v(4)
            E("vector", lambda q: q.tensor_scalar(gc_["a"][:, :], cols[:, 0:8],
                                                  1.0 / D, None, ALU.mult))
            E("vector", lambda q: q.tensor_scalar(gc_["b"][:, :], cols[:, 8:16],
                                                  1.0 / D, None, ALU.mult))
            pad_v(3)
            E("vector", lambda q: q.tensor_tensor(gc_["c"][:, :], gc_["a"][:, :],
                                                  gc_["a"][:, :], ALU.mult))
            pad_v(3)
            E("vector", lambda q: q.tensor_tensor(gc_["d"][:, :], gc_["b"][:, :],
                                                  gc_["c"][:, :], ALU.subtract),
              inc="sv")
            Wt("scalar", "sv")
            E("scalar", lambda q: q.activation(gc_["e"][:, :], gc_["d"][:, :],
                                               AF.Sqrt, bias=eps_col[:, 0:1]),
              inc="ss")
            Wt("vector", "ss")
            E("vector", lambda q: q.reciprocal(gc_["f"][:, :], gc_["e"][:, :]))
            pad_v(3)
            E("vector", lambda q: q.tensor_tensor(gc_["c"][:, :], gc_["a"][:, :],
                                                  gc_["f"][:, :], ALU.mult))
            pad_v(3)
            E("vector", lambda q: q.tensor_scalar(gc_["d"][:, :], gc_["c"][:, :],
                                                  -1.0, None, ALU.mult), inc="sv")
            pad_v(4)
            for tc in range(TC):
                E("vector", lambda q, tc=tc: q.tensor_scalar(
                    scr2k[:, :], src_handle[:, tc * 2048:(tc + 1) * 2048],
                    gc_["f"][:, tc:tc + 1], gc_["d"][:, tc:tc + 1],
                    ALU.mult, ALU.add), inc="sv")
                transpose_chunks(dst_handle, tc * 128, 1024,
                                 lambda c: scr2k[:, c * 128:(c + 1) * 128], DC)

        tpar = {"p": 0}

        def transpose_chunks(dst_handle, dst_base, dst_cstride, src_ap_fn, nch):
            for half in range((nch + 7) // 8):
                n_h = min(8, nch - half * 8)
                po = 1024 * tpar["p"]
                Wt("tensor", "sv")
                Wt("tensor", "ss")
                for j in range(n_h):
                    c = half * 8 + j
                    E("tensor", lambda q, c=c, j=j, po=po: q.transpose(
                        PS1[:, po + j * 128: po + (j + 1) * 128],
                        src_ap_fn(c), ident[:, :]),
                      inc=("st" if j == n_h - 1 else None))
                Wt("vector", "st")
                E("vector", lambda q, po=po, half=half, n_h=n_h: q.tensor_copy(
                    bass.AP(dst_handle, dst_base + half * 8 * dst_cstride,
                            [[16 * 1024, 128], [dst_cstride, n_h], [1, 128]]),
                    bass.AP(PS1, po, [[2048, 128], [128, n_h], [1, 128]])),
                  inc="sv")
                tpar["p"] ^= 1

        def wcol(Wap, co, w, ld_):
            return bass.AP(Wap.tensor, Wap.offset + co * w,
                           [[ld_, 128], [128 * ld_, DC], [1, w]])

        def proj_chunk(dst_ps, ph, wslot, brow_idx, do):
            for dc in range(DC):
                for hf in range(2):
                    E("tensor", lambda q, dc=dc, hf=hf, ph=ph, w=wslot: q.matmul(
                        dst_ps[:, ph + hf * 512: ph + (hf + 1) * 512],
                        kvw[:, w + dc * 128: w + (dc + 1) * 128],
                        big_a[:, dc * 1024 + hf * 512: dc * 1024 + (hf + 1) * 512],
                        start=(dc == 0), stop=(dc == DC - 1)),
                      inc=("st" if dc == DC - 1 and hf == 1 else None))

        def rope_chunk(ph, do, track_k2, dst_ap, bias_base):
            Wt("vector", "st")
            E("vector", lambda q, ph=ph, bc=bias_base + do: q.tensor_scalar(
                ktmp[:, :], PS0[:, ph:ph + 1024], qkob[:, bc:bc + 1], None,
                ALU.add), inc="sv")
            Wt("tensor", "ss")
            Wt("tensor", "sv")
            sph = 1024 * (do % 2)
            for hf in range(2):
                E("tensor", lambda q, hf=hf, sph=sph: q.matmul(
                    PS1[:, sph + hf * 512: sph + (hf + 1) * 512],
                    swapm[:, :], ktmp[:, hf * 512:(hf + 1) * 512],
                    start=True, stop=True), inc=("st" if hf == 1 else None))
            Wt("vector", "st")
            E("vector", lambda q: q.tensor_tensor(rtmp[:, :], ktmp[:, :], cos2,
                                                  ALU.mult))
            E("vector", lambda q, sph=sph, d=dst_ap: q.tensor_tensor(
                d, PS1[:, sph:sph + 1024], sin2s, ALU.mult))
            E("vector", lambda q, d=dst_ap: q.tensor_tensor(d, d, rtmp[:, :], ALU.add),
              inc="sv")
            if track_k2:
                Wt("scalar", "sv")
                E("scalar", lambda q, d=dst_ap: q.activation(rtmp[:, :], d, AF.Square),
                  inc="ss")
                Wt("tensor", "ss")
                for hf in range(2):
                    E("tensor", lambda q, hf=hf, sph=sph: q.matmul(
                        PS1[0:1, sph + hf * 512: sph + (hf + 1) * 512],
                        ones_col[:, 0:1], rtmp[:, hf * 512:(hf + 1) * 512],
                        start=True, stop=True), inc=("st" if hf == 1 else None))
                Wt("vector", "st")
                E("vector", lambda q, sph=sph: q.reduce_max(
                    cols[0:1, 13:14], PS1[0:1, sph:sph + 1024], AX.X))
                pad_v(4)
                E("vector", lambda q, do=do: q.tensor_max(
                    kmax2[0:1, do:do + 1], kmax2[0:1, do:do + 1], cols[0:1, 13:14]),
                  inc="sv")

        if PHASES >= 1:
            for sl in range(NSLAB):
                build_tables(float(sl * TOK))
                Wt("sync", "sv")
                Wt("sync", "lg")
                b.dma("sync", "ld",
                      bass.AP(big_b, 0, [[16 * 1024, 128], [2048, TC], [1, 2048]]),
                      bass.AP(x_full.tensor, sl * TOK * D,
                              [[D, 128], [128 * D, TC], [1, D]]))
                ln_slab(big_b, big_a)
                for do in range(DC):
                    wslot = (do % 2) * 2048
                    Wt("sync", "st")
                    b.dma("sync", "ld", kvw[:, wslot:wslot + 2048], wcol(Wk, do, 128, D))
                    Wt("tensor", "ld")
                    Wt("tensor", "ss")
                    Wt("tensor", "sv")
                    ph = 1024 * (do % 2)
                    proj_chunk(PS0, ph, wslot, 2, do)
                    rope_chunk(ph, do, True, big_b[:, do * 1024:(do + 1) * 1024], 16)
                Wt("gpsimd", "sv")
                Wt("gpsimd", "lg")
                E("gpsimd", lambda q, sl=sl: q.dma_start(
                    out=bass.AP(K_dram.tensor, sl * TOK,
                                [[S, 128], [128 * S, DC], [1, 1024]]),
                    in_=bass.AP(big_b, 0,
                                [[16 * 1024, 128], [1024, DC], [1, 1024]])),
                  inc="lg", add=16)
                lg_ksp = b.cnt["lg"]
                for dv8 in range(8):
                    Wt("sync", "st")
                    Wt("sync", "lg", lg_ksp)
                    b.dma("sync", "ld", kvw[:, 0:4096], wcol(Wv, dv8, 256, D))
                    Wt("tensor", "ld")
                    Wt("tensor", "lg", lg_ksp)
                    for tc in range(TC):
                        ps = PS0 if tc < 4 else PS1
                        pofs = (tc % 4) * 512
                        Wt("tensor", "ss")
                        for dc in range(DC):
                            E("tensor", lambda q, dc=dc, tc=tc, ps=ps, pofs=pofs: q.matmul(
                                ps[:, pofs:pofs + 256],
                                big_a[:, dc * 1024 + tc * 128: dc * 1024 + (tc + 1) * 128],
                                kvw[:, dc * 256:(dc + 1) * 256],
                                start=(dc == 0), stop=False))
                        E("tensor", lambda q, dv8=dv8, ps=ps, pofs=pofs: q.matmul(
                            ps[:, pofs:pofs + 256],
                            iotaf[0:1, 0:128],
                            kvw[0:1, 6144 + dv8 * 256: 6144 + (dv8 + 1) * 256],
                            start=False, stop=True), inc="st")
                        Wt("scalar", "st")
                        E("scalar", lambda q, tc=tc, dv8=dv8, ps=ps, pofs=pofs: q.copy(
                            big_b[:, tc * D + dv8 * 256: tc * D + (dv8 + 1) * 256],
                            ps[:, pofs:pofs + 256]), inc="ss")
                Wt("gpsimd", "ss")
                Wt("gpsimd", "lg")
                E("gpsimd", lambda q, sl=sl: q.dma_start(
                    out=bass.AP(V_dram.tensor, sl * TOK * D,
                                [[D, 128], [128 * D, TC], [1, D]]),
                    in_=bass.AP(big_b, 0,
                                [[16 * 1024, 128], [2048, TC], [1, 2048]])),
                  inc="lg", add=16)


        if PHASES >= 2:
            build_tables(None)
            Wt("sync", "sv")
            Wt("sync", "lg")
            b.dma("sync", "ld",
                  bass.AP(big_b, 0, [[16 * 1024, 128], [2048, TC], [1, 2048]]),
                  bass.AP(x_own.tensor, 0, [[D, 128], [128 * D, TC], [1, D]]))
            ln_slab(big_b, big_a)
            if DEBUG:
                Wt("gpsimd", "sv")
                Wt("gpsimd", "lg")
                E("gpsimd", lambda q: q.dma_start(out=X1_dram, in_=big_a[:, :]),
                  inc="lg", add=16)
            for do in range(DC):
                wslot = (do % 2) * 2048
                Wt("sync", "st")
                b.dma("sync", "ld", kvw[:, wslot:wslot + 2048], wcol(Wq, do, 128, D))
                Wt("tensor", "ld")
                Wt("tensor", "ss")
                Wt("tensor", "sv")
                Wt("tensor", "lg", b.cnt["lg"])
                ph = 1024 * (do % 2)
                proj_chunk(PS0, ph, wslot, 1, do)
                Wt("gpsimd", "sv")
                rope_chunk(ph, do, False, obuf[:, :], 0)
                Wt("gpsimd", "sv")
                Wt("gpsimd", "lg")
                E("gpsimd", lambda q, do=do: q.dma_start(
                    out=bass.AP(Q_dram.tensor, do * 128 * TOK, [[TOK, 128], [1, 1024]]),
                    in_=obuf[:, :]), inc="lg", add=16)
                Wt("vector", "lg")

        if PHASES >= 3:
            Wt("sync", "lg")
            for h in range(NH):
                Wt("sync", "st")
                Wt("sync", "ss")
                b.dma("sync", "ld", big_a[:, 0:S],
                      bass.AP(K_dram.tensor, h * 128 * S, [[S, 128], [1, S]]))
                b.dma("sync", "ld", big_a[:, S:2 * S],
                      bass.AP(V_dram.tensor, h * 128,
                              [[D, 128], [128 * D, 64], [1, 128]]))
                b.dma("sync", "ld", qbuf[:, :],
                      bass.AP(Q_dram.tensor, h * 128 * TOK, [[TOK, 128], [1, 1024]]))
                ld_head = b.cnt["ld"]
                Wt("scalar", "ld", ld_head)
                E("scalar", lambda q: q.activation(rtmp[:, :], qbuf[:, :], AF.Square),
                  inc="ss")
                Wt("tensor", "ss")
                Wt("tensor", "sv")
                for hf in range(2):
                    E("tensor", lambda q, hf=hf: q.matmul(
                        PS1[0:1, 1024 + hf * 512: 1024 + (hf + 1) * 512],
                        ones_col[:, 0:1], rtmp[:, hf * 512:(hf + 1) * 512],
                        start=True, stop=True), inc=("st" if hf == 1 else None))
                Wt("vector", "st")
                E("vector", lambda q: q.reduce_max(srows[0:1, 0:1],
                                                   PS1[0:1, 1024:2048], AX.X))
                pad_v(4)
                E("vector", lambda q, h=h: q.tensor_tensor(
                    srows[0:1, 1:2], srows[0:1, 0:1], kmax2[0:1, h:h + 1], ALU.mult),
                  inc="sv")
                Wt("scalar", "sv")
                E("scalar", lambda q: q.activation(srows[0:1, 2:3], srows[0:1, 1:2],
                                                   AF.Sqrt), inc="ss")
                Wt("vector", "ss")
                E("vector", lambda q: q.tensor_scalar(srows[0:1, 3:4], srows[0:1, 2:3],
                                                      -ISQ, None, ALU.mult), inc="sv")
                Wt("gpsimd", "sv")
                b.dma("gpsimd", "lg", sc_dram[0:1, 2:3], srows[0:1, 3:4])
                Wt("gpsimd", "lg")
                b.dma("gpsimd", "lg", km2rep[:, :],
                      sc_dram[0:1, 2:3].partition_broadcast(128))
                lg_negm = b.cnt["lg"]
                Wt("scalar", "lg", lg_negm)
                sv_negm = b.cnt["sv"]
                for qt in range(TC):
                    Wt("tensor", "ld", ld_head)
                    Wt("scalar", "sv")
                    for kc in range(16):
                        Wt("tensor", "ss")
                        E("tensor", lambda q, kc=kc, qt=qt: q.matmul(
                            PS0[:, (kc % 4) * 512:(kc % 4) * 512 + 512],
                            qbuf[:, qt * 128:(qt + 1) * 128],
                            big_a[:, kc * 512:(kc + 1) * 512],
                            start=True, stop=True), inc="st")
                        Wt("scalar", "st")
                        if kc == 0:
                            Wt("scalar", "sv", sv_negm)
                        E("scalar", lambda q, kc=kc, qt=qt: q.activation(
                            big_b[:, kc * 512:(kc + 1) * 512],
                            PS0[:, (kc % 4) * 512:(kc % 4) * 512 + 512], AF.Exp,
                            bias=km2rep[:, 0:1], scale=ISQ,
                            accum_out=sc16[:, kc:kc + 1]), inc="ss")
                    Wt("vector", "ss")
                    Wt("vector", "st")
                    E("vector", lambda q, qt=qt: q.reduce_sum(
                        scols[:, qt:qt + 1], sc16[:, 0:16], AX.X), inc="sv")
                    for batch in range(16):
                        po = 512 * (batch % 2)
                        Wt("tensor", "sv")
                        Wt("tensor", "ss")
                        for j in range(4):
                            ch = batch * 4 + j
                            E("tensor", lambda q, ch=ch, j=j, po=po: q.transpose(
                                PS1[:, po + j * 128: po + (j + 1) * 128],
                                big_b[:, ch * 128:(ch + 1) * 128],
                                ident[:, :]), inc=("st" if j == 3 else None))
                        Wt("vector", "st")
                        E("vector", lambda q, batch=batch, po=po: q.tensor_copy(
                            scr2k[:, (batch % 2) * 512:(batch % 2) * 512 + 512],
                            PS1[:, po:po + 512]), inc="sv")
                        Wt("tensor", "sv")
                        for j in range(4):
                            ch = batch * 4 + j
                            E("tensor", lambda q, ch=ch, j=j, batch=batch: q.matmul(
                                PS1[:, 1536:1664],
                                big_a[:, S + ch * 128: S + (ch + 1) * 128],
                                scr2k[:, (batch % 2) * 512 + j * 128:
                                      (batch % 2) * 512 + (j + 1) * 128],
                                start=(ch == 0), stop=(ch == 63)),
                              inc=("st" if ch == 63 else None))
                    Wt("vector", "st")
                    E("vector", lambda q, qt=qt: q.tensor_copy(
                        obuf[:, qt * 128:(qt + 1) * 128], PS1[:, 1536:1664]), inc="sv")
                Wt("tensor", "sv")
                E("tensor", lambda q: q.transpose(PS1[0:8, 1664:1792], scols[:, 0:8],
                                                  ident[:, :]), inc="st")
                Wt("vector", "st")
                E("vector", lambda q: q.tensor_copy(srows[0:8, :], PS1[0:8, 1664:1792]))
                pad_v(4)
                E("vector", lambda q: q.reciprocal(srows2[0:8, :], srows[0:8, :]),
                  inc="sv")
                Wt("gpsimd", "sv")
                b.dma("gpsimd", "lg", bass.AP(row_dram.tensor, 0, [[128, 8], [1, 128]]),
                      srows2[0:8, :])
                Wt("gpsimd", "lg")
                b.dma("gpsimd", "lg", wrep[:, :], row_dram.partition_broadcast(128))
                Wt("vector", "lg")
                E("vector", lambda q: q.tensor_tensor(obuf[:, :], obuf[:, :],
                                                      wrep[:, :], ALU.mult), inc="sv")
                Wt("gpsimd", "sv")
                Wt("gpsimd", "lg")
                E("gpsimd", lambda q, h=h: q.dma_start(
                    out=bass.AP(O_dram.tensor, h * 128 * TOK, [[TOK, 128], [1, 1024]]),
                    in_=obuf[:, :]), inc="lg", add=16)
                Wt("vector", "lg")

        if PHASES >= 4:
            Wt("sync", "lg")
            Wt("sync", "st")
            Wt("sync", "ss")
            b.dma("sync", "ld", big_b[:, :],
                  bass.AP(O_dram.tensor, 0, [[TOK, 128], [128 * TOK, DC], [1, 1024]]))
            for do in range(DC):
                wslot = (do % 2) * 2048
                Wt("sync", "st")
                b.dma("sync", "ld", kvw[:, wslot:wslot + 2048], wcol(Wo, do, 128, D))
                Wt("tensor", "ld")
                Wt("tensor", "ss")
                ph = 1024 * (do % 2)
                for dc in range(DC):
                    for hf in range(2):
                        E("tensor", lambda q, dc=dc, hf=hf, ph=ph, w=wslot: q.matmul(
                            PS0[:, ph + hf * 512: ph + (hf + 1) * 512],
                            kvw[:, w + dc * 128: w + (dc + 1) * 128],
                            big_b[:, dc * 1024 + hf * 512: dc * 1024 + (hf + 1) * 512],
                            start=(dc == 0), stop=(dc == DC - 1)),
                          inc=("st" if dc == DC - 1 and hf == 1 else None))
                Wt("vector", "st")
                E("vector", lambda q, ph=ph, do=do: q.tensor_scalar(
                    ktmp[:, :], PS0[:, ph:ph + 1024], qkob[:, 32 + do:33 + do], None,
                    ALU.add), inc="sv")
                transpose_chunks(big_a, do * 128, 2048,
                                 lambda c: ktmp[:, c * 128:(c + 1) * 128], TC)
            for tc in range(TC):
                Wt("sync", "sv")
                b.dma("sync", "ld", kvw[:, 0:2048], x_own[tc * 128:(tc + 1) * 128, :])
                Wt("vector", "ld")
                E("vector", lambda q, tc=tc: q.tensor_tensor(
                    big_a[:, tc * 2048:(tc + 1) * 2048],
                    big_a[:, tc * 2048:(tc + 1) * 2048], kvw[:, 0:2048], ALU.add),
                  inc="sv")
            Wt("gpsimd", "sv")
            Wt("gpsimd", "lg")
            E("gpsimd", lambda q: q.dma_start(
                out=bass.AP(XP_dram.tensor, 0, [[D, 128], [128 * D, TC], [1, D]]),
                in_=bass.AP(big_a, 0, [[16 * 1024, 128], [2048, TC], [1, 2048]])),
              inc="lg", add=16)
            ln_slab(big_a, big_b)

        if PHASES >= 5:
            def g3(t):
                return bass.AP(t, 0, [[64, 128], [8, 8], [1, 8]])

            def gcE(t):
                return bass.AP(t, 0, [[8, 128], [1, 8], [0, 8]])

            def gcT(t):
                return bass.AP(t, 0, [[8, 128], [0, 8], [1, 8]])

            def gro(t):
                return bass.AP(t, 0, [[8, 128], [1, 8], [1, 1]])

            for tc in range(TC):
                po = 512 * (tc % 2)
                Wt("tensor", "sv")
                Wt("tensor", "ss")
                for dc in range(DC):
                    E("tensor", lambda q, dc=dc, tc=tc, po=po: q.matmul(
                        PS1[:, po:po + 8],
                        big_b[:, dc * 1024 + tc * 128: dc * 1024 + (tc + 1) * 128],
                        Wr_sb[:, dc * 8:(dc + 1) * 8],
                        start=(dc == 0), stop=(dc == DC - 1)),
                      inc=("st" if dc == DC - 1 else None))
                Wt("scalar", "st")
                Wt("scalar", "lg")
                E("scalar", lambda q, po=po, tc=tc: q.activation(
                    g["inh"][:, tc * 8:(tc + 1) * 8], PS1[:, po:po + 8], AF.Sigmoid,
                    scale=invt_col[:, 0:1]), inc="ss")
            Wt("vector", "ss")
            E("vector", lambda q: q.tensor_scalar(g["t1"][:, :], g["inh"][:, :],
                                                  GL, None, ALU.is_ge))
            E("vector", lambda q: q.tensor_scalar(g["t2"][:, :], g["inh"][:, :],
                                                  GU, None, ALU.is_le))
            pad_v(2)
            E("vector", lambda q: q.tensor_tensor(g["t2"][:, :], g["t1"][:, :],
                                                  g["t2"][:, :], ALU.mult), inc="sv")
            pad_s(2)
            E("scalar", lambda q: q.activation(g["dist"][:, :], g["inh"][:, :],
                                               AF.Abs, bias=gc_col[:, 0:1]))
            pad_s(4)
            Wt("scalar", "sv")
            E("scalar", lambda q: q.activation(g["w"][:, :], g["dist"][:, :],
                                               AF.Exp, scale=-10.0))
            E("scalar", lambda q: q.activation(g["fb"][:, :], g["dist"][:, :],
                                               AF.Exp, scale=-10.0 / 3.0), inc="ss")
            Wt("vector", "ss")
            pad_v(3)
            E("vector", lambda q: q.tensor_tensor(g["w"][:, :], g["w"][:, :],
                                                  g["t2"][:, :], ALU.mult))
            pad_v(3)
            E("vector", lambda q: q.tensor_reduce(gro(gc_["a"]), g3(g["w"]),
                                                  AX.X, ALU.add))
            E("vector", lambda q: q.reduce_max(gro(gc_["c"]), g3(g["fb"]), AX.X))
            pad_v(3)
            E("vector", lambda q: q.tensor_scalar(gc_["b"][:, :], gc_["a"][:, :],
                                                  1e-8, None, ALU.is_lt))
            E("vector", lambda q: q.tensor_tensor(g3(g["t1"]), g3(g["fb"]),
                                                  gcE(gc_["c"]), ALU.is_equal))
            pad_v(3)
            E("vector", lambda q: q.tensor_tensor(g3(g["t1"]), g3(g["t1"]),
                                                  gcT(gi8m), ALU.mult))
            pad_v(4)
            E("vector", lambda q: q.tensor_reduce(gro(gc_["d"]), g3(g["t1"]),
                                                  AX.X, ALU.min))
            pad_v(4)
            E("vector", lambda q: q.tensor_tensor(g3(g["eqi1"]), gcT(gi8m),
                                                  gcE(gc_["d"]), ALU.is_equal))
            pad_v(4)
            E("vector", lambda q: q.scalar_tensor_tensor(g["fb2"][:, :],
                                                         g["eqi1"][:, :], -1e30,
                                                         g["fb"][:, :], ALU.mult,
                                                         ALU.add))
            pad_v(4)
            E("vector", lambda q: q.reduce_max(gro(gc_["c"]), g3(g["fb2"]), AX.X))
            pad_v(4)
            E("vector", lambda q: q.tensor_tensor(g3(g["t1"]), g3(g["fb2"]),
                                                  gcE(gc_["c"]), ALU.is_equal))
            pad_v(3)
            E("vector", lambda q: q.tensor_tensor(g3(g["t1"]), g3(g["t1"]),
                                                  gcT(gi8m), ALU.mult))
            pad_v(4)
            E("vector", lambda q: q.tensor_reduce(gro(gc_["d"]), g3(g["t1"]),
                                                  AX.X, ALU.min))
            pad_v(4)
            E("vector", lambda q: q.tensor_tensor(g3(g["eqi2"]), gcT(gi8m),
                                                  gcE(gc_["d"]), ALU.is_equal))
            pad_v(4)
            E("vector", lambda q: q.tensor_tensor(g["t1"][:, :], g["eqi1"][:, :],
                                                  g["eqi2"][:, :], ALU.add))
            pad_v(4)
            E("vector", lambda q: q.tensor_tensor(g["t1"][:, :], g["fb"][:, :],
                                                  g["t1"][:, :], ALU.mult))
            pad_v(4)
            E("vector", lambda q: q.tensor_reduce(gro(gc_["e"]), g3(g["t1"]),
                                                  AX.X, ALU.add))
            pad_v(4)
            E("vector", lambda q: q.tensor_scalar(gc_["c"][:, :], gc_["e"][:, :],
                                                  1e-8, None, ALU.max))
            pad_v(4)
            E("vector", lambda q: q.reciprocal(gc_["d"][:, :], gc_["c"][:, :]))
            pad_v(4)
            E("vector", lambda q: q.tensor_tensor(g3(g["t1"]), g3(g["t1"]),
                                                  gcE(gc_["d"]), ALU.mult))
            pad_v(4)
            E("vector", lambda q: q.tensor_tensor(g["t2"][:, :], g["t1"][:, :],
                                                  g["w"][:, :], ALU.subtract))
            pad_v(3)
            E("vector", lambda q: q.tensor_tensor(g3(g["t2"]), g3(g["t2"]),
                                                  gcE(gc_["b"]), ALU.mult))
            pad_v(4)
            E("vector", lambda q: q.tensor_tensor(g["w"][:, :], g["w"][:, :],
                                                  g["t2"][:, :], ALU.add))
            pad_v(4)
            E("vector", lambda q: q.tensor_reduce(gro(gc_["a"]), g3(g["w"]),
                                                  AX.X, ALU.add))
            pad_v(4)
            E("vector", lambda q: q.tensor_scalar(gc_["c"][:, :], gc_["a"][:, :],
                                                  1e-8, None, ALU.max))
            pad_v(4)
            E("vector", lambda q: q.reciprocal(gc_["d"][:, :], gc_["c"][:, :]))
            pad_v(4)
            E("vector", lambda q: q.tensor_tensor(g3(g["w"]), g3(g["w"]),
                                                  gcE(gc_["d"]), ALU.mult), inc="sv")
            for tc in range(TC):
                Wt("tensor", "sv")
                E("tensor", lambda q, tc=tc: q.transpose(
                    PS1[0:8, 1536:1664], g["w"][:, tc * 8:(tc + 1) * 8],
                    ident[:, :]), inc="st")
                Wt("vector", "st")
                E("vector", lambda q, tc=tc: q.tensor_copy(
                    qbuf[0:8, tc * 128:(tc + 1) * 128], PS1[0:8, 1536:1664]), inc="sv")
            if DEBUG:
                Wt("gpsimd", "sv")
                Wt("gpsimd", "lg")
                E("gpsimd", lambda q: q.dma_start(out=GW_dram, in_=qbuf[0:8, :]),
                  inc="lg", add=16)
            for e in range(NE):
                Wt("gpsimd", "sv")
                Wt("gpsimd", "ss")
                b.dma("gpsimd", "lg", wrow_dram, qbuf[e:e + 1, :])
                Wt("gpsimd", "lg")
                b.dma("gpsimd", "lg", wrep[:, :], row_dram.partition_broadcast(128)
                      if False else wrow_dram.partition_broadcast(128))
                lg_w = b.cnt["lg"]
                for fc in range(DC):
                    Wt("sync", "st")
                    b.dma("sync", "ld", kvw[:, 0:2048],
                          wcol(bass.AP(Wg.tensor, e * D * FF, [[1, 1]]), fc, 128, FF))
                    b.dma("sync", "ld", kvw[:, 2048:4096],
                          wcol(bass.AP(Wu.tensor, e * D * FF, [[1, 1]]), fc, 128, FF))
                    Wt("tensor", "ld")
                    Wt("tensor", "ss")
                    Wt("tensor", "sv")
                    for dc in range(DC):
                        for hf in range(2):
                            E("tensor", lambda q, dc=dc, hf=hf: q.matmul(
                                PS0[:, hf * 512:(hf + 1) * 512],
                                kvw[:, dc * 128:(dc + 1) * 128],
                                big_b[:, dc * 1024 + hf * 512: dc * 1024 + (hf + 1) * 512],
                                start=(dc == 0), stop=(dc == DC - 1)))
                    for dc in range(DC):
                        for hf in range(2):
                            E("tensor", lambda q, dc=dc, hf=hf: q.matmul(
                                PS0[:, 1024 + hf * 512: 1024 + (hf + 1) * 512],
                                kvw[:, 2048 + dc * 128: 2048 + (dc + 1) * 128],
                                big_b[:, dc * 1024 + hf * 512: dc * 1024 + (hf + 1) * 512],
                                start=(dc == 0), stop=(dc == DC - 1)),
                              inc=("st" if dc == DC - 1 and hf == 1 else None))
                    Wt("scalar", "st")
                    E("scalar", lambda q: q.activation(ktmp[:, :], PS0[:, 0:1024],
                                                       AF.Silu), inc="ss")
                    Wt("vector", "ss")
                    Wt("vector", "lg", lg_w)
                    E("vector", lambda q: q.tensor_tensor(rtmp[:, :], ktmp[:, :],
                                                          PS0[:, 1024:2048], ALU.mult))
                    E("vector", lambda q, fc=fc: q.tensor_tensor(
                        big_a[:, fc * 1024:(fc + 1) * 1024], rtmp[:, :], wrep[:, :],
                        ALU.mult), inc="sv")
                for do in range(DC):
                    Wt("sync", "st")
                    b.dma("sync", "ld", kvw[:, 0:2048],
                          wcol(bass.AP(Wd.tensor, e * FF * D, [[1, 1]]), do, 128, D))
                    Wt("tensor", "ld")
                    Wt("tensor", "sv")
                    ph = 1024 * (do % 2)
                    Wt("tensor", "ss")
                    for fc in range(DC):
                        for hf in range(2):
                            E("tensor", lambda q, fc=fc, hf=hf, ph=ph: q.matmul(
                                PS1[:, ph + hf * 512: ph + (hf + 1) * 512],
                                kvw[:, fc * 128:(fc + 1) * 128],
                                big_a[:, fc * 1024 + hf * 512: fc * 1024 + (hf + 1) * 512],
                                start=(fc == 0), stop=(fc == DC - 1)),
                              inc=("st" if fc == DC - 1 and hf == 1 else None))
                    Wt("scalar", "st")
                    Wt("scalar", "lg")
                    bo_ = 1024 * (do % 2)
                    E("scalar", lambda q, ph=ph, bo_=bo_: q.copy(
                        scr2k[:, bo_:bo_ + 1024], PS1[:, ph:ph + 1024]), inc="ss")
                    Wt("gpsimd", "ss")
                    kwacc = {} if e == 0 else dict(accum_op=ALU.add)
                    Wt("gpsimd", "lg")
                    E("gpsimd", lambda q, do=do, bo_=bo_, kw=kwacc: q.dma_start(
                        out=bass.AP(FFN_dram.tensor, do * 128 * TOK,
                                    [[TOK, 128], [1, 1024]]),
                        in_=scr2k[:, bo_:bo_ + 1024], **kw), inc="lg", add=16)

        if PHASES >= 6:
            Wt("sync", "lg")
            for tc in range(TC):
                Wt("sync", "st")
                Wt("sync", "sv")
                b.dma("sync", "ld", kvw[:, 0:2048],
                      bass.AP(FFN_dram.tensor, tc * 128,
                              [[TOK, 128], [128 * TOK, DC], [1, 128]]))
                b.dma("sync", "ld", kvw[:, 2048:4096],
                      bass.AP(XP_dram.tensor, tc * 128 * D, [[D, 128], [1, D]]))
                Wt("tensor", "ld")
                Wt("tensor", "sv")
                ps = PS0 if tc % 2 == 0 else PS1
                for dc in range(DC):
                    E("tensor", lambda q, dc=dc, ps=ps: q.transpose(
                        ps[:, dc * 128:(dc + 1) * 128],
                        kvw[:, dc * 128:(dc + 1) * 128], ident[:, :]),
                      inc=("st" if dc == DC - 1 else None))
                Wt("vector", "st")
                Wt("vector", "out")
                ao = 2048 * (tc % 2)
                E("vector", lambda q, ps=ps, ao=ao: q.tensor_tensor(
                    big_a[:, ao:ao + 2048], kvw[:, 2048:4096], ps[:, 0:2048], ALU.add),
                  inc="sv")
                Wt("sync", "sv")
                b.dma("sync", "out", out_own[tc * 128:(tc + 1) * 128, :],
                      big_a[:, ao:ao + 2048])

        Wt("sync", "out")
        Wt("sync", "lg")
        Wt("gpsimd", "lg")
        Wt("gpsimd", "sg")

        with nc.Block() as block:
            for eng in ENGS:
                def mk_run(eng):
                    def run(q):
                        for fn in b.prog[eng]:
                            fn(q)
                    return run
                getattr(block, eng)(mk_run(eng))

    return nc


_cached = {}


def kernel(L0_attn_hidden_state, L0_attn_Wq, L0_attn_bq, L0_attn_Wk, L0_attn_bk,
           L0_attn_Wv, L0_attn_bv, L0_attn_Wo, L0_attn_bo, L0_ffn_Wg, L0_ffn_Wu,
           L0_ffn_Wd, L0_ffn_Wr, L0_ffn_temperature):
    x = np.ascontiguousarray(np.asarray(L0_attn_hidden_state, dtype=np.float32)
                             .reshape(S, D))
    if "nc" not in _cached:
        _cached["nc"] = build_program()
    nc = _cached["nc"]
    common = {
        "x_full": x,
        "Wq": np.ascontiguousarray(L0_attn_Wq, dtype=np.float32),
        "Wk": np.ascontiguousarray(L0_attn_Wk, dtype=np.float32),
        "Wv": np.ascontiguousarray(L0_attn_Wv, dtype=np.float32),
        "Wo": np.ascontiguousarray(L0_attn_Wo, dtype=np.float32),
        "bq": np.ascontiguousarray(L0_attn_bq, dtype=np.float32).reshape(1, D),
        "bk": np.ascontiguousarray(L0_attn_bk, dtype=np.float32).reshape(1, D),
        "bv": np.ascontiguousarray(L0_attn_bv, dtype=np.float32).reshape(1, D),
        "bo": np.ascontiguousarray(L0_attn_bo, dtype=np.float32).reshape(1, D),
        "Wg": np.ascontiguousarray(L0_ffn_Wg, dtype=np.float32),
        "Wu": np.ascontiguousarray(L0_ffn_Wu, dtype=np.float32),
        "Wd": np.ascontiguousarray(L0_ffn_Wd, dtype=np.float32),
        "Wr": np.ascontiguousarray(L0_ffn_Wr, dtype=np.float32),
        "temp": np.ascontiguousarray(L0_ffn_temperature,
                                     dtype=np.float32).reshape(1, 1),
    }
    in_maps = []
    for c in range(8):
        m = dict(common)
        m["x_own"] = np.ascontiguousarray(x[c * TOK:(c + 1) * TOK])
        m["core_ofs"] = np.asarray([[float(c * TOK)]], dtype=np.float32)
        in_maps.append(m)
    res = run_bass_kernel_spmd(nc, in_maps, core_ids=list(range(8)))
    out = np.concatenate([r["out_own"] for r in res.results], axis=0)
    return out.reshape(1, S, D).astype(np.float32)


# version 06 (final)
# speedup vs baseline: 1.0194x; 1.0101x over previous
import math
from contextlib import ExitStack

import numpy as np
import concourse.bass as bass
import concourse.mybir as mybir
from concourse.bass_utils import run_bass_kernel_spmd

F32 = mybir.dt.float32
I32 = mybir.dt.int32
AF = mybir.ActivationFunctionType
ALU = mybir.AluOpType
AX = mybir.AxisListType

S, D, NH, HD = 8192, 2048, 16, 128
TOK = 1024
DC = D // 128
TC = TOK // 128
NSLAB = S // TOK
NE, FF = 8, 2048

PI = float(np.pi)
TWOPI = 2 * math.pi
TWOPI_HI = float(np.float32(TWOPI))
TWOPI_LO = float(TWOPI - np.float64(np.float32(TWOPI)))
ISQ = 1.0 / math.sqrt(HD)
LN1E4_64 = math.log(10000.0) / 64.0
GC = 1.0 / math.e
GL = 0.5 - math.log(4.0 / 3.0)
GU = 0.5
EPS_LN = 1e-5

ENGS = ("sync", "gpsimd", "vector", "scalar", "tensor")
DEBUG = False
PHASES = 9


class B:
    def __init__(self, nc):
        self.nc = nc
        self.prog = {e: [] for e in ENGS}
        self.sems = {}
        self.cnt = {}

    def sem(self, ctx, name):
        self.sems[name] = ctx.enter_context(self.nc.semaphore(name))
        self.cnt[name] = 0

    def emit(self, eng, fn, inc=None, add=1):
        if eng == "vector":
            inc = inc or "sv"
            self.wait(eng, "sv")
        elif eng == "scalar":
            inc = inc or "ss"
            self.wait(eng, "ss")
        if inc is None:
            self.prog[eng].append(fn)
            return None
        sh = self.sems[inc]
        self.cnt[inc] += add
        self.prog[eng].append(lambda q, fn=fn, sh=sh, add=add: fn(q).then_inc(sh, add))
        return self.cnt[inc]

    def wait(self, eng, sem, val=None):
        if val is None:
            val = self.cnt[sem]
        if val <= 0:
            return
        sh = self.sems[sem]
        self.prog[eng].append(lambda q, sh=sh, val=val: q.wait_ge(sh, val))

    def dma(self, eng, sem, out, in_, **kw):
        self.wait(eng, sem)
        return self.emit(eng, lambda q, o=out, i=in_, kw=kw: q.dma_start(out=o, in_=i, **kw),
                         inc=sem, add=16)


def build_program():
    nc = bass.Bass("TRN2", target_bir_lowering=False)
    b = B(nc)

    x_own = nc.dram_tensor("x_own", [TOK, D], F32, kind="ExternalInput").ap()
    x_full = nc.dram_tensor("x_full", [S, D], F32, kind="ExternalInput").ap()
    Wq = nc.dram_tensor("Wq", [D, D], F32, kind="ExternalInput").ap()
    Wk = nc.dram_tensor("Wk", [D, D], F32, kind="ExternalInput").ap()
    Wv = nc.dram_tensor("Wv", [D, D], F32, kind="ExternalInput").ap()
    Wo = nc.dram_tensor("Wo", [D, D], F32, kind="ExternalInput").ap()
    bq = nc.dram_tensor("bq", [1, D], F32, kind="ExternalInput").ap()
    bk = nc.dram_tensor("bk", [1, D], F32, kind="ExternalInput").ap()
    bv = nc.dram_tensor("bv", [1, D], F32, kind="ExternalInput").ap()
    bo = nc.dram_tensor("bo", [1, D], F32, kind="ExternalInput").ap()
    Wg = nc.dram_tensor("Wg", [NE, D, FF], F32, kind="ExternalInput").ap()
    Wu = nc.dram_tensor("Wu", [NE, D, FF], F32, kind="ExternalInput").ap()
    Wd = nc.dram_tensor("Wd", [NE, FF, D], F32, kind="ExternalInput").ap()
    Wr = nc.dram_tensor("Wr", [D, NE], F32, kind="ExternalInput").ap()
    temp = nc.dram_tensor("temp", [1, 1], F32, kind="ExternalInput").ap()
    core_ofs = nc.dram_tensor("core_ofs", [1, 1], F32, kind="ExternalInput").ap()
    sin_t = nc.dram_tensor("sin_t", [128, S], F32, kind="ExternalInput").ap()
    cos_t = nc.dram_tensor("cos_t", [128, S], F32, kind="ExternalInput").ap()
    sin_o = nc.dram_tensor("sin_o", [128, TOK], F32, kind="ExternalInput").ap()
    cos_o = nc.dram_tensor("cos_o", [128, TOK], F32, kind="ExternalInput").ap()
    out_own = nc.dram_tensor("out_own", [TOK, D], F32, kind="ExternalOutput").ap()

    skind = dict(kind="ExternalOutput") if DEBUG else {}
    K_dram = nc.dram_tensor("K_scr", [D, S], F32, **skind).ap()
    V_dram = nc.dram_tensor("V_scr", [S, D], F32, **skind).ap()
    Q_dram = nc.dram_tensor("Q_scr", [D, TOK], F32, **skind).ap()
    O_dram = nc.dram_tensor("O_scr", [D, TOK], F32, **skind).ap()
    XP_dram = nc.dram_tensor("XP_scr", [TOK, D], F32, **skind).ap()
    FFN_dram = nc.dram_tensor("FFN_scr", [D, TOK], F32, **skind).ap()
    GW_dram = nc.dram_tensor("GW_scr", [8, TOK], F32, **skind).ap()
    X1_dram = nc.dram_tensor("X1_scr", [128, 16 * 1024], F32, **skind).ap()
    row_dram = nc.dram_tensor("row_scr", [1, TOK], F32).ap()
    wrow_dram = nc.dram_tensor("wrow_scr", [1, TOK], F32).ap()
    sc_dram = nc.dram_tensor("sc_scr", [1, 8], F32).ap()

    with ExitStack() as ctx:
        en = ctx.enter_context
        big_a = en(nc.sbuf_tensor("big_a", [128, 16 * 1024], F32))
        big_b = en(nc.sbuf_tensor("big_b", [128, 16 * 1024], F32))
        kvw = en(nc.sbuf_tensor("kvw", [128, 8192], F32))
        qbuf = en(nc.sbuf_tensor("qbuf", [128, TOK], F32))
        obuf = en(nc.sbuf_tensor("obuf", [128, TOK], F32))
        scr2k = en(nc.sbuf_tensor("scr2k", [128, 2048], F32))
        ktmp = en(nc.sbuf_tensor("ktmp", [128, 1024], F32))
        rtmp = en(nc.sbuf_tensor("rtmp", [128, 1024], F32))
        wrep = en(nc.sbuf_tensor("wrep", [128, 1024], F32))
        qkob = en(nc.sbuf_tensor("qkob", [128, 48], F32))
        ident = en(nc.sbuf_tensor("ident", [128, 128], F32))
        swapm = en(nc.sbuf_tensor("swapm", [128, 128], F32))
        iotaf = en(nc.sbuf_tensor("iotaf", [128, 128], F32))
        Wr_sb = en(nc.sbuf_tensor("Wr_sb", [128, DC * NE], F32))
        cols = en(nc.sbuf_tensor("cols", [128, 16], F32))
        sc16 = en(nc.sbuf_tensor("sc16", [128, 16], F32))
        scols = en(nc.sbuf_tensor("scols", [128, 8], F32))
        qn2c = en(nc.sbuf_tensor("qn2c", [128, 8], F32))
        negm = en(nc.sbuf_tensor("negm", [128, 8], F32))
        negm2 = en(nc.sbuf_tensor("negm2", [128, 8], F32))
        srows = en(nc.sbuf_tensor("srows", [8, 128], F32))
        kmax2 = en(nc.sbuf_tensor("kmax2", [1, 16], F32))
        iotap = en(nc.sbuf_tensor("iotap", [128, 1], F32))
        invt_col = en(nc.sbuf_tensor("invt_col", [128, 1], F32))
        cofs_col = en(nc.sbuf_tensor("cofs_col", [128, 1], F32))
        km2rep = en(nc.sbuf_tensor("km2rep", [128, 1], F32))
        sign_col = en(nc.sbuf_tensor("sign_col", [128, 1], F32))
        swapidx = en(nc.sbuf_tensor("swapidx", [128, 1], F32))
        sgn2 = en(nc.sbuf_tensor("sgn2", [1, 2], F32))
        isqn_col = en(nc.sbuf_tensor("isqn_col", [128, 1], F32))
        isq_col = en(nc.sbuf_tensor("isq_col", [128, 1], F32))
        ones_col = en(nc.sbuf_tensor("ones_col", [128, 1], F32))
        eps_col = en(nc.sbuf_tensor("eps_col", [128, 1], F32))
        gc_col = en(nc.sbuf_tensor("gc_col", [128, 1], F32))
        g = {nm: en(nc.sbuf_tensor("g_" + nm, [128, 64], F32))
             for nm in ("inh", "t1", "t2", "dist", "w", "fb", "fb2",
                        "eqi1", "eqi2")}
        gi8 = en(nc.sbuf_tensor("gi8", [128, 8], F32))
        gi8m = en(nc.sbuf_tensor("gi8m", [128, 8], F32))
        gc_ = {nm: en(nc.sbuf_tensor("gc_" + nm, [128, 8], F32))
               for nm in ("a", "b", "c", "d", "e", "f")}
        jv1 = en(nc.sbuf_tensor("jv1", [128, 32], F32))
        jv2 = en(nc.sbuf_tensor("jv2", [128, 8], F32))
        js1 = en(nc.sbuf_tensor("js1", [128, 32], F32))
        js2 = en(nc.sbuf_tensor("js2", [128, 8], F32))
        srows2 = en(nc.sbuf_tensor("srows2", [8, 128], F32))
        PS0 = en(nc.psum_tensor("PS0", [128, 2048], F32))
        PS1 = en(nc.psum_tensor("PS1", [128, 2048], F32))

        for s_ in ("ld", "lg", "sv", "ss", "st", "sg", "out"):
            b.sem(ctx, s_)
        E, Wt = b.emit, b.wait

        def pad_v(n=4):
            pass

        def pad_s(n=4):
            pass


        SIN_OFS, COS_OFS = 4096, 5120
        TB1, TB2, TB3, TBI = 0, 1024, 2048, 3072

        def bb64(col, width=1024):
            return bass.AP(big_b, col, [[16 * 1024, 64], [1, width]])

        sin2s = kvw[:, SIN_OFS:SIN_OFS + 1024]
        cos2 = kvw[:, COS_OFS:COS_OFS + 1024]

        E("gpsimd", lambda q: q.iota(iotap[:, :], [[0, 1]], channel_multiplier=1,
                                     allow_small_or_imprecise_dtypes=True))
        E("gpsimd", lambda q: q.iota(iotaf[:, :], [[1, 128]], channel_multiplier=0,
                                     allow_small_or_imprecise_dtypes=True))
        E("gpsimd", lambda q: q.iota(gi8[:, :], [[1, 8]], channel_multiplier=0,
                                     allow_small_or_imprecise_dtypes=True))
        E("gpsimd", lambda q: q.memset(jv2[:, :], 0.0))
        E("gpsimd", lambda q: q.memset(sgn2[0:1, 0:1], -1.0))
        E("gpsimd", lambda q: q.memset(sgn2[0:1, 1:2], 1.0))
        E("gpsimd", lambda q: q.memset(js2[:, :], 0.0))
        E("gpsimd", lambda q: q.memset(ones_col[:, :], 1.0))
        E("gpsimd", lambda q: q.memset(eps_col[:, :], EPS_LN))
        E("gpsimd", lambda q: q.memset(kmax2[:, :], 0.0), inc="sg")
        sg_c0 = b.cnt["sg"]

        b.dma("sync", "ld", kvw[0:1, 6144:8192], bv)
        b.dma("sync", "ld", Wr_sb[:, :],
              bass.AP(Wr.tensor, 0, [[NE, 128], [128 * NE, DC], [1, NE]]))
        b.dma("sync", "ld", cofs_col[:, :], core_ofs.partition_broadcast(128))
        b.dma("sync", "ld", cols[0:1, 14:15], temp)
        ld_c0 = b.cnt["ld"]

        Wt("vector", "ld", ld_c0)
        E("vector", lambda q: q.reciprocal(cols[0:1, 15:16], cols[0:1, 14:15]), inc="sv")
        Wt("gpsimd", "sv")
        b.dma("gpsimd", "lg", sc_dram[0:1, 0:1], cols[0:1, 15:16])
        Wt("gpsimd", "lg")
        b.dma("gpsimd", "lg", invt_col[:, :], sc_dram[0:1, 0:1].partition_broadcast(128))

        Wt("scalar", "sg", sg_c0)
        Wt("gpsimd", "sg", sg_c0)
        E("gpsimd", lambda q: q.memset(cols[:, 13:14], 128.0), inc="sg")
        Wt("scalar", "sg")
        E("scalar", lambda q: q.activation(isq_col[:, :], cols[:, 13:14], AF.Sqrt),
          inc="ss")
        Wt("vector", "ss")
        E("vector", lambda q: q.reciprocal(isqn_col[:, :], isq_col[:, :]))
        E("vector", lambda q: q.tensor_scalar(isq_col[:, :], isqn_col[:, :], 1.0,
                                              None, ALU.mult))
        E("vector", lambda q: q.tensor_scalar(isqn_col[:, :], isqn_col[:, :], -1.0,
                                              None, ALU.mult), inc="sv")
        E("scalar", lambda q: q.activation(cols[:, 11:12], ones_col[:, 0:1], AF.Exp,
                                           scale=-1.0), inc="ss")
        Wt("vector", "ss")
        E("vector", lambda q: q.tensor_scalar(gc_col[:, :], cols[:, 11:12], -1.0,
                                              None, ALU.mult), inc="sv")
        Wt("vector", "sg", sg_c0)
        E("vector", lambda q: q.tensor_scalar(ident[:, :], iotaf[:, :], iotap[:, 0:1],
                                              None, ALU.is_equal))
        E("vector", lambda q: q.tensor_scalar(gi8m[:, :], gi8[:, :], -8.0,
                                              None, ALU.add), inc="sv")
        Wt("gpsimd", "sg", sg_c0)
        b.dma("gpsimd", "lg", wrow_dram[0:1, 0:2], sgn2[0:1, :])
        b.dma("gpsimd", "lg", sign_col[:, :],
              bass.AP(wrow_dram.tensor, 0, [[0, 64], [1, 2]]))
        lg_c0 = b.cnt["lg"]
        Wt("vector", "sg", sg_c0)
        Wt("vector", "lg", lg_c0)
        E("vector", lambda q: q.tensor_tensor(swapidx[:, :], iotap[:, :],
                                              sign_col[:, :], ALU.subtract))
        pad_v(4)
        E("vector", lambda q: q.tensor_scalar(swapm[:, :], iotaf[:, :],
                                              swapidx[:, 0:1], None, ALU.is_equal),
          inc="sv")
        for bi, bsrc in enumerate((bq, bk, bo)):
            Wt("sync", "st")
            b.dma("sync", "ld", scr2k[0:16, 0:128],
                  bass.AP(bsrc.tensor, 0, [[128, 16], [1, 128]]))
            Wt("tensor", "ld")
            Wt("tensor", "sv")
            E("tensor", lambda q, bi=bi: q.transpose(
                PS1[:, bi * 16:(bi + 1) * 16],
                scr2k[0:16, 0:128],
                ident[0:16, 0:16]), inc="st")
        Wt("vector", "st")
        E("vector", lambda q: q.tensor_copy(qkob[:, 0:48], PS1[:, 0:48]), inc="sv")
        Wt("gpsimd", "st")
        E("gpsimd", lambda q: q.memset(iotaf[0:1, :], 1.0), inc="sg")
        sg_ones = b.cnt["sg"]

        def load_tables(slab):
            Wt("sync", "sv")
            if slab is None:
                b.dma("sync", "ld", kvw[:, SIN_OFS:SIN_OFS + 1024], sin_o)
                b.dma("sync", "ld", kvw[:, COS_OFS:COS_OFS + 1024], cos_o)
            else:
                b.dma("sync", "ld", kvw[:, SIN_OFS:SIN_OFS + 1024],
                      sin_t[:, slab * TOK:(slab + 1) * TOK])
                b.dma("sync", "ld", kvw[:, COS_OFS:COS_OFS + 1024],
                      cos_t[:, slab * TOK:(slab + 1) * TOK])
            Wt("vector", "ld")
            Wt("vector", "lg", lg_c0)
            E("vector", lambda q: q.tensor_scalar(sin2s, sin2s, sign_col[:, 0:1],
                                                  None, ALU.mult))
            return b.cnt["sv"]

        def ln_slab(src_handle, dst_handle):
            Wt("vector", "ld")
            Wt("vector", "lg")
            Wt("scalar", "ld")
            Wt("scalar", "lg")
            Wt("scalar", "st")
            for tc in range(TC):
                E("vector", lambda q, tc=tc: q.reduce_sum(
                    cols[:, tc:tc + 1],
                    src_handle[:, tc * 2048:(tc + 1) * 2048], AX.X), inc="sv")
                E("scalar", lambda q, tc=tc: q.activation(
                    PS0[:, 0:2048], src_handle[:, tc * 2048:(tc + 1) * 2048],
                    AF.Square, accum_out=cols[:, 8 + tc:9 + tc]), inc="ss")
            Wt("vector", "ss")
            pad_v(4)
            E("vector", lambda q: q.tensor_scalar(gc_["a"][:, :], cols[:, 0:8],
                                                  1.0 / D, None, ALU.mult))
            E("vector", lambda q: q.tensor_scalar(gc_["b"][:, :], cols[:, 8:16],
                                                  1.0 / D, None, ALU.mult))
            pad_v(3)
            E("vector", lambda q: q.tensor_tensor(gc_["c"][:, :], gc_["a"][:, :],
                                                  gc_["a"][:, :], ALU.mult))
            pad_v(3)
            E("vector", lambda q: q.tensor_tensor(gc_["d"][:, :], gc_["b"][:, :],
                                                  gc_["c"][:, :], ALU.subtract),
              inc="sv")
            Wt("scalar", "sv")
            E("scalar", lambda q: q.activation(gc_["e"][:, :], gc_["d"][:, :],
                                               AF.Sqrt, bias=eps_col[:, 0:1]),
              inc="ss")
            Wt("vector", "ss")
            E("vector", lambda q: q.reciprocal(gc_["f"][:, :], gc_["e"][:, :]))
            pad_v(3)
            E("vector", lambda q: q.tensor_tensor(gc_["c"][:, :], gc_["a"][:, :],
                                                  gc_["f"][:, :], ALU.mult))
            pad_v(3)
            E("vector", lambda q: q.tensor_scalar(gc_["d"][:, :], gc_["c"][:, :],
                                                  -1.0, None, ALU.mult), inc="sv")
            pad_v(4)
            for tc in range(TC):
                E("vector", lambda q, tc=tc: q.tensor_scalar(
                    scr2k[:, :], src_handle[:, tc * 2048:(tc + 1) * 2048],
                    gc_["f"][:, tc:tc + 1], gc_["d"][:, tc:tc + 1],
                    ALU.mult, ALU.add), inc="sv")
                transpose_chunks(dst_handle, tc * 128, 1024,
                                 lambda c: scr2k[:, c * 128:(c + 1) * 128], DC)

        tpar = {"p": 0}

        def transpose_chunks(dst_handle, dst_base, dst_cstride, src_ap_fn, nch):
            for half in range((nch + 7) // 8):
                n_h = min(8, nch - half * 8)
                po = 1024 * tpar["p"]
                Wt("tensor", "sv")
                Wt("tensor", "ss")
                for j in range(n_h):
                    c = half * 8 + j
                    E("tensor", lambda q, c=c, j=j, po=po: q.transpose(
                        PS1[:, po + j * 128: po + (j + 1) * 128],
                        src_ap_fn(c), ident[:, :]),
                      inc=("st" if j == n_h - 1 else None))
                Wt("vector", "st")
                E("vector", lambda q, po=po, half=half, n_h=n_h: q.tensor_copy(
                    bass.AP(dst_handle, dst_base + half * 8 * dst_cstride,
                            [[16 * 1024, 128], [dst_cstride, n_h], [1, 128]]),
                    bass.AP(PS1, po, [[2048, 128], [128, n_h], [1, 128]])),
                  inc="sv")
                tpar["p"] ^= 1

        def wcol(Wap, co, w, ld_):
            return bass.AP(Wap.tensor, Wap.offset + co * w,
                           [[ld_, 128], [128 * ld_, DC], [1, w]])

        def proj_chunk(dst_ps, ph, wslot, brow_idx, do):
            for dc in range(DC):
                for hf in range(2):
                    E("tensor", lambda q, dc=dc, hf=hf, ph=ph, w=wslot: q.matmul(
                        dst_ps[:, ph + hf * 512: ph + (hf + 1) * 512],
                        kvw[:, w + dc * 128: w + (dc + 1) * 128],
                        big_a[:, dc * 1024 + hf * 512: dc * 1024 + (hf + 1) * 512],
                        start=(dc == 0), stop=(dc == DC - 1)),
                      inc=("st" if dc == DC - 1 and hf == 1 else None))

        def rope_chunk(ph, do, track_k2, dst_ap, bias_base):
            Wt("vector", "st")
            E("vector", lambda q, ph=ph, bc=bias_base + do: q.tensor_scalar(
                ktmp[:, :], PS0[:, ph:ph + 1024], qkob[:, bc:bc + 1], None,
                ALU.add), inc="sv")
            Wt("tensor", "ss")
            Wt("tensor", "sv")
            sph = 1024 * (do % 2)
            for hf in range(2):
                E("tensor", lambda q, hf=hf, sph=sph: q.matmul(
                    PS1[:, sph + hf * 512: sph + (hf + 1) * 512],
                    swapm[:, :], ktmp[:, hf * 512:(hf + 1) * 512],
                    start=True, stop=True), inc=("st" if hf == 1 else None))
            Wt("vector", "st")
            E("vector", lambda q: q.tensor_tensor(rtmp[:, :], ktmp[:, :], cos2,
                                                  ALU.mult))
            E("vector", lambda q, sph=sph, d=dst_ap: q.tensor_tensor(
                d, PS1[:, sph:sph + 1024], sin2s, ALU.mult))
            E("vector", lambda q, d=dst_ap: q.tensor_tensor(d, d, rtmp[:, :], ALU.add),
              inc="sv")
            if track_k2:
                Wt("scalar", "sv")
                E("scalar", lambda q, d=dst_ap: q.activation(rtmp[:, :], d, AF.Square),
                  inc="ss")
                Wt("tensor", "ss")
                for hf in range(2):
                    E("tensor", lambda q, hf=hf, sph=sph: q.matmul(
                        PS1[0:1, sph + hf * 512: sph + (hf + 1) * 512],
                        ones_col[:, 0:1], rtmp[:, hf * 512:(hf + 1) * 512],
                        start=True, stop=True), inc=("st" if hf == 1 else None))
                Wt("vector", "st")
                E("vector", lambda q, sph=sph: q.reduce_max(
                    cols[0:1, 13:14], PS1[0:1, sph:sph + 1024], AX.X))
                pad_v(4)
                E("vector", lambda q, do=do: q.tensor_max(
                    kmax2[0:1, do:do + 1], kmax2[0:1, do:do + 1], cols[0:1, 13:14]),
                  inc="sv")

        if PHASES >= 1:
            for sl in range(NSLAB):
                load_tables(sl)
                Wt("sync", "sv")
                Wt("sync", "lg")
                b.dma("sync", "ld",
                      bass.AP(big_b, 0, [[16 * 1024, 128], [2048, TC], [1, 2048]]),
                      bass.AP(x_full.tensor, sl * TOK * D,
                              [[D, 128], [128 * D, TC], [1, D]]))
                ln_slab(big_b, big_a)
                for do in range(DC):
                    wslot = (do % 2) * 2048
                    Wt("sync", "st")
                    b.dma("sync", "ld", kvw[:, wslot:wslot + 2048], wcol(Wk, do, 128, D))
                    Wt("tensor", "ld")
                    Wt("tensor", "ss")
                    Wt("tensor", "sv")
                    ph = 1024 * (do % 2)
                    proj_chunk(PS0, ph, wslot, 2, do)
                    rope_chunk(ph, do, True, big_b[:, do * 1024:(do + 1) * 1024], 16)
                Wt("gpsimd", "sv")
                Wt("gpsimd", "lg")
                E("gpsimd", lambda q, sl=sl: q.dma_start(
                    out=bass.AP(K_dram.tensor, sl * TOK,
                                [[S, 128], [128 * S, DC], [1, 1024]]),
                    in_=bass.AP(big_b, 0,
                                [[16 * 1024, 128], [1024, DC], [1, 1024]])),
                  inc="lg", add=16)
                lg_ksp = b.cnt["lg"]
                for dv8 in range(8):
                    Wt("sync", "st")
                    Wt("sync", "lg", lg_ksp)
                    b.dma("sync", "ld", kvw[:, 0:4096], wcol(Wv, dv8, 256, D))
                    Wt("tensor", "ld")
                    Wt("tensor", "lg", lg_ksp)
                    for tc in range(TC):
                        ps = PS0 if tc < 4 else PS1
                        pofs = (tc % 4) * 512
                        Wt("tensor", "ss")
                        for dc in range(DC):
                            E("tensor", lambda q, dc=dc, tc=tc, ps=ps, pofs=pofs: q.matmul(
                                ps[:, pofs:pofs + 256],
                                big_a[:, dc * 1024 + tc * 128: dc * 1024 + (tc + 1) * 128],
                                kvw[:, dc * 256:(dc + 1) * 256],
                                start=(dc == 0), stop=False))
                        E("tensor", lambda q, dv8=dv8, ps=ps, pofs=pofs: q.matmul(
                            ps[:, pofs:pofs + 256],
                            iotaf[0:1, 0:128],
                            kvw[0:1, 6144 + dv8 * 256: 6144 + (dv8 + 1) * 256],
                            start=False, stop=True), inc="st")
                        Wt("scalar", "st")
                        E("scalar", lambda q, tc=tc, dv8=dv8, ps=ps, pofs=pofs: q.copy(
                            big_b[:, tc * D + dv8 * 256: tc * D + (dv8 + 1) * 256],
                            ps[:, pofs:pofs + 256]), inc="ss")
                Wt("gpsimd", "ss")
                Wt("gpsimd", "lg")
                E("gpsimd", lambda q, sl=sl: q.dma_start(
                    out=bass.AP(V_dram.tensor, sl * TOK * D,
                                [[D, 128], [128 * D, TC], [1, D]]),
                    in_=bass.AP(big_b, 0,
                                [[16 * 1024, 128], [2048, TC], [1, 2048]])),
                  inc="lg", add=16)


        if PHASES >= 2:
            load_tables(None)
            Wt("sync", "sv")
            Wt("sync", "lg")
            b.dma("sync", "ld",
                  bass.AP(big_b, 0, [[16 * 1024, 128], [2048, TC], [1, 2048]]),
                  bass.AP(x_own.tensor, 0, [[D, 128], [128 * D, TC], [1, D]]))
            ln_slab(big_b, big_a)
            if DEBUG:
                Wt("gpsimd", "sv")
                Wt("gpsimd", "lg")
                E("gpsimd", lambda q: q.dma_start(out=X1_dram, in_=big_a[:, :]),
                  inc="lg", add=16)
            for do in range(DC):
                wslot = (do % 2) * 2048
                Wt("sync", "st")
                b.dma("sync", "ld", kvw[:, wslot:wslot + 2048], wcol(Wq, do, 128, D))
                Wt("tensor", "ld")
                Wt("tensor", "ss")
                Wt("tensor", "sv")
                Wt("tensor", "lg", b.cnt["lg"])
                ph = 1024 * (do % 2)
                proj_chunk(PS0, ph, wslot, 1, do)
                Wt("gpsimd", "sv")
                rope_chunk(ph, do, False, obuf[:, :], 0)
                Wt("gpsimd", "sv")
                Wt("gpsimd", "lg")
                E("gpsimd", lambda q, do=do: q.dma_start(
                    out=bass.AP(Q_dram.tensor, do * 128 * TOK, [[TOK, 128], [1, 1024]]),
                    in_=obuf[:, :]), inc="lg", add=16)
                Wt("vector", "lg")

        if PHASES >= 3:
            Wt("sync", "lg")
            for h in range(NH):
                Wt("sync", "st")
                Wt("sync", "ss")
                b.dma("sync", "ld", big_a[:, 0:S],
                      bass.AP(K_dram.tensor, h * 128 * S, [[S, 128], [1, S]]))
                b.dma("sync", "ld", big_a[:, S:2 * S],
                      bass.AP(V_dram.tensor, h * 128,
                              [[D, 128], [128 * D, 64], [1, 128]]))
                b.dma("sync", "ld", qbuf[:, :],
                      bass.AP(Q_dram.tensor, h * 128 * TOK, [[TOK, 128], [1, 1024]]))
                ld_head = b.cnt["ld"]
                Wt("scalar", "ld", ld_head)
                E("scalar", lambda q: q.activation(rtmp[:, :], qbuf[:, :], AF.Square),
                  inc="ss")
                Wt("tensor", "ss")
                Wt("tensor", "sv")
                for hf in range(2):
                    E("tensor", lambda q, hf=hf: q.matmul(
                        PS1[0:1, 1024 + hf * 512: 1024 + (hf + 1) * 512],
                        ones_col[:, 0:1], rtmp[:, hf * 512:(hf + 1) * 512],
                        start=True, stop=True), inc=("st" if hf == 1 else None))
                Wt("vector", "st")
                E("vector", lambda q: q.reduce_max(srows[0:1, 0:1],
                                                   PS1[0:1, 1024:2048], AX.X))
                pad_v(4)
                E("vector", lambda q, h=h: q.tensor_tensor(
                    srows[0:1, 1:2], srows[0:1, 0:1], kmax2[0:1, h:h + 1], ALU.mult),
                  inc="sv")
                Wt("scalar", "sv")
                E("scalar", lambda q: q.activation(srows[0:1, 2:3], srows[0:1, 1:2],
                                                   AF.Sqrt), inc="ss")
                Wt("vector", "ss")
                E("vector", lambda q: q.tensor_tensor(srows[0:1, 3:4], srows[0:1, 2:3],
                                                      isqn_col[0:1, 0:1], ALU.mult),
                  inc="sv")
                Wt("gpsimd", "sv")
                b.dma("gpsimd", "lg", sc_dram[0:1, 2:3], srows[0:1, 3:4])
                Wt("gpsimd", "lg")
                b.dma("gpsimd", "lg", km2rep[:, :],
                      sc_dram[0:1, 2:3].partition_broadcast(128))
                lg_negm = b.cnt["lg"]
                Wt("scalar", "lg", lg_negm)
                sv_negm = b.cnt["sv"]
                for qt in range(TC):
                    Wt("tensor", "ld", ld_head)
                    Wt("scalar", "sv")
                    for kc in range(16):
                        Wt("tensor", "ss")
                        E("tensor", lambda q, kc=kc, qt=qt: q.matmul(
                            PS0[:, (kc % 4) * 512:(kc % 4) * 512 + 512],
                            qbuf[:, qt * 128:(qt + 1) * 128],
                            big_a[:, kc * 512:(kc + 1) * 512],
                            start=True, stop=True), inc="st")
                        Wt("scalar", "st")
                        if kc == 0:
                            Wt("scalar", "sv", sv_negm)
                        E("scalar", lambda q, kc=kc, qt=qt: q.activation(
                            big_b[:, kc * 512:(kc + 1) * 512],
                            PS0[:, (kc % 4) * 512:(kc % 4) * 512 + 512], AF.Exp,
                            bias=km2rep[:, 0:1], scale=isq_col[:, 0:1],
                            accum_out=sc16[:, kc:kc + 1]), inc="ss")
                    Wt("vector", "ss")
                    Wt("vector", "st")
                    E("vector", lambda q, qt=qt: q.reduce_sum(
                        scols[:, qt:qt + 1], sc16[:, 0:16], AX.X), inc="sv")
                    for batch in range(16):
                        po = 512 * (batch % 2)
                        Wt("tensor", "sv")
                        Wt("tensor", "ss")
                        for j in range(4):
                            ch = batch * 4 + j
                            E("tensor", lambda q, ch=ch, j=j, po=po: q.transpose(
                                PS1[:, po + j * 128: po + (j + 1) * 128],
                                big_b[:, ch * 128:(ch + 1) * 128],
                                ident[:, :]), inc=("st" if j == 3 else None))
                        Wt("vector", "st")
                        E("vector", lambda q, batch=batch, po=po: q.tensor_copy(
                            scr2k[:, (batch % 2) * 512:(batch % 2) * 512 + 512],
                            PS1[:, po:po + 512]), inc="sv")
                        Wt("tensor", "sv")
                        for j in range(4):
                            ch = batch * 4 + j
                            E("tensor", lambda q, ch=ch, j=j, batch=batch: q.matmul(
                                PS1[:, 1536:1664],
                                big_a[:, S + ch * 128: S + (ch + 1) * 128],
                                scr2k[:, (batch % 2) * 512 + j * 128:
                                      (batch % 2) * 512 + (j + 1) * 128],
                                start=(ch == 0), stop=(ch == 63)),
                              inc=("st" if ch == 63 else None))
                    Wt("vector", "st")
                    E("vector", lambda q, qt=qt: q.tensor_copy(
                        obuf[:, qt * 128:(qt + 1) * 128], PS1[:, 1536:1664]), inc="sv")
                Wt("tensor", "sv")
                E("tensor", lambda q: q.transpose(PS1[0:8, 1664:1792], scols[:, 0:8],
                                                  ident[:, :]), inc="st")
                Wt("vector", "st")
                E("vector", lambda q: q.tensor_copy(srows[0:8, :], PS1[0:8, 1664:1792]))
                pad_v(4)
                E("vector", lambda q: q.reciprocal(srows2[0:8, :], srows[0:8, :]),
                  inc="sv")
                Wt("gpsimd", "sv")
                b.dma("gpsimd", "lg", bass.AP(row_dram.tensor, 0, [[128, 8], [1, 128]]),
                      srows2[0:8, :])
                Wt("gpsimd", "lg")
                b.dma("gpsimd", "lg", wrep[:, :], row_dram.partition_broadcast(128))
                Wt("vector", "lg")
                E("vector", lambda q: q.tensor_tensor(obuf[:, :], obuf[:, :],
                                                      wrep[:, :], ALU.mult), inc="sv")
                Wt("gpsimd", "sv")
                Wt("gpsimd", "lg")
                E("gpsimd", lambda q, h=h: q.dma_start(
                    out=bass.AP(O_dram.tensor, h * 128 * TOK, [[TOK, 128], [1, 1024]]),
                    in_=obuf[:, :]), inc="lg", add=16)
                Wt("vector", "lg")

        if PHASES >= 4:
            Wt("sync", "lg")
            Wt("sync", "st")
            Wt("sync", "ss")
            b.dma("sync", "ld", big_b[:, :],
                  bass.AP(O_dram.tensor, 0, [[TOK, 128], [128 * TOK, DC], [1, 1024]]))
            for do in range(DC):
                wslot = (do % 2) * 2048
                Wt("sync", "st")
                b.dma("sync", "ld", kvw[:, wslot:wslot + 2048], wcol(Wo, do, 128, D))
                Wt("tensor", "ld")
                Wt("tensor", "ss")
                ph = 1024 * (do % 2)
                for dc in range(DC):
                    for hf in range(2):
                        E("tensor", lambda q, dc=dc, hf=hf, ph=ph, w=wslot: q.matmul(
                            PS0[:, ph + hf * 512: ph + (hf + 1) * 512],
                            kvw[:, w + dc * 128: w + (dc + 1) * 128],
                            big_b[:, dc * 1024 + hf * 512: dc * 1024 + (hf + 1) * 512],
                            start=(dc == 0), stop=(dc == DC - 1)),
                          inc=("st" if dc == DC - 1 and hf == 1 else None))
                Wt("vector", "st")
                E("vector", lambda q, ph=ph, do=do: q.tensor_scalar(
                    ktmp[:, :], PS0[:, ph:ph + 1024], qkob[:, 32 + do:33 + do], None,
                    ALU.add), inc="sv")
                transpose_chunks(big_a, do * 128, 2048,
                                 lambda c: ktmp[:, c * 128:(c + 1) * 128], TC)
            for tc in range(TC):
                Wt("sync", "sv")
                b.dma("sync", "ld", kvw[:, 0:2048], x_own[tc * 128:(tc + 1) * 128, :])
                Wt("vector", "ld")
                E("vector", lambda q, tc=tc: q.tensor_tensor(
                    big_a[:, tc * 2048:(tc + 1) * 2048],
                    big_a[:, tc * 2048:(tc + 1) * 2048], kvw[:, 0:2048], ALU.add),
                  inc="sv")
            Wt("gpsimd", "sv")
            Wt("gpsimd", "lg")
            E("gpsimd", lambda q: q.dma_start(
                out=bass.AP(XP_dram.tensor, 0, [[D, 128], [128 * D, TC], [1, D]]),
                in_=bass.AP(big_a, 0, [[16 * 1024, 128], [2048, TC], [1, 2048]])),
              inc="lg", add=16)
            ln_slab(big_a, big_b)

        if PHASES >= 5:
            def g3(t):
                return bass.AP(t, 0, [[64, 128], [8, 8], [1, 8]])

            def gcE(t):
                return bass.AP(t, 0, [[8, 128], [1, 8], [0, 8]])

            def gcT(t):
                return bass.AP(t, 0, [[8, 128], [0, 8], [1, 8]])

            def gro(t):
                return bass.AP(t, 0, [[8, 128], [1, 8], [1, 1]])

            for tc in range(TC):
                po = 512 * (tc % 2)
                Wt("tensor", "sv")
                Wt("tensor", "ss")
                for dc in range(DC):
                    E("tensor", lambda q, dc=dc, tc=tc, po=po: q.matmul(
                        PS1[:, po:po + 8],
                        big_b[:, dc * 1024 + tc * 128: dc * 1024 + (tc + 1) * 128],
                        Wr_sb[:, dc * 8:(dc + 1) * 8],
                        start=(dc == 0), stop=(dc == DC - 1)),
                      inc=("st" if dc == DC - 1 else None))
                Wt("scalar", "st")
                Wt("scalar", "lg")
                E("scalar", lambda q, po=po, tc=tc: q.activation(
                    g["inh"][:, tc * 8:(tc + 1) * 8], PS1[:, po:po + 8], AF.Sigmoid,
                    scale=invt_col[:, 0:1]), inc="ss")
            Wt("vector", "ss")
            E("vector", lambda q: q.tensor_scalar(g["t1"][:, :], g["inh"][:, :],
                                                  GL, None, ALU.is_ge))
            E("vector", lambda q: q.tensor_scalar(g["t2"][:, :], g["inh"][:, :],
                                                  GU, None, ALU.is_le))
            pad_v(2)
            E("vector", lambda q: q.tensor_tensor(g["t2"][:, :], g["t1"][:, :],
                                                  g["t2"][:, :], ALU.mult), inc="sv")
            pad_s(2)
            E("scalar", lambda q: q.activation(g["dist"][:, :], g["inh"][:, :],
                                               AF.Abs, bias=gc_col[:, 0:1]))
            pad_s(4)
            Wt("scalar", "sv")
            E("scalar", lambda q: q.activation(g["w"][:, :], g["dist"][:, :],
                                               AF.Exp, scale=-10.0))
            E("scalar", lambda q: q.activation(g["fb"][:, :], g["dist"][:, :],
                                               AF.Exp, scale=-10.0 / 3.0), inc="ss")
            Wt("vector", "ss")
            pad_v(3)
            E("vector", lambda q: q.tensor_tensor(g["w"][:, :], g["w"][:, :],
                                                  g["t2"][:, :], ALU.mult))
            pad_v(3)
            E("vector", lambda q: q.tensor_reduce(gro(gc_["a"]), g3(g["w"]),
                                                  AX.X, ALU.add))
            E("vector", lambda q: q.reduce_max(gro(gc_["c"]), g3(g["fb"]), AX.X))
            pad_v(3)
            E("vector", lambda q: q.tensor_scalar(gc_["b"][:, :], gc_["a"][:, :],
                                                  1e-8, None, ALU.is_lt))
            E("vector", lambda q: q.tensor_tensor(g3(g["t1"]), g3(g["fb"]),
                                                  gcE(gc_["c"]), ALU.is_equal))
            pad_v(3)
            E("vector", lambda q: q.tensor_tensor(g3(g["t1"]), g3(g["t1"]),
                                                  gcT(gi8m), ALU.mult))
            pad_v(4)
            E("vector", lambda q: q.tensor_reduce(gro(gc_["d"]), g3(g["t1"]),
                                                  AX.X, ALU.min))
            pad_v(4)
            E("vector", lambda q: q.tensor_tensor(g3(g["eqi1"]), gcT(gi8m),
                                                  gcE(gc_["d"]), ALU.is_equal))
            pad_v(4)
            E("vector", lambda q: q.scalar_tensor_tensor(g["fb2"][:, :],
                                                         g["eqi1"][:, :], -1e30,
                                                         g["fb"][:, :], ALU.mult,
                                                         ALU.add))
            pad_v(4)
            E("vector", lambda q: q.reduce_max(gro(gc_["c"]), g3(g["fb2"]), AX.X))
            pad_v(4)
            E("vector", lambda q: q.tensor_tensor(g3(g["t1"]), g3(g["fb2"]),
                                                  gcE(gc_["c"]), ALU.is_equal))
            pad_v(3)
            E("vector", lambda q: q.tensor_tensor(g3(g["t1"]), g3(g["t1"]),
                                                  gcT(gi8m), ALU.mult))
            pad_v(4)
            E("vector", lambda q: q.tensor_reduce(gro(gc_["d"]), g3(g["t1"]),
                                                  AX.X, ALU.min))
            pad_v(4)
            E("vector", lambda q: q.tensor_tensor(g3(g["eqi2"]), gcT(gi8m),
                                                  gcE(gc_["d"]), ALU.is_equal))
            pad_v(4)
            E("vector", lambda q: q.tensor_tensor(g["t1"][:, :], g["eqi1"][:, :],
                                                  g["eqi2"][:, :], ALU.add))
            pad_v(4)
            E("vector", lambda q: q.tensor_tensor(g["t1"][:, :], g["fb"][:, :],
                                                  g["t1"][:, :], ALU.mult))
            pad_v(4)
            E("vector", lambda q: q.tensor_reduce(gro(gc_["e"]), g3(g["t1"]),
                                                  AX.X, ALU.add))
            pad_v(4)
            E("vector", lambda q: q.tensor_scalar(gc_["c"][:, :], gc_["e"][:, :],
                                                  1e-8, None, ALU.max))
            pad_v(4)
            E("vector", lambda q: q.reciprocal(gc_["d"][:, :], gc_["c"][:, :]))
            pad_v(4)
            E("vector", lambda q: q.tensor_tensor(g3(g["t1"]), g3(g["t1"]),
                                                  gcE(gc_["d"]), ALU.mult))
            pad_v(4)
            E("vector", lambda q: q.tensor_tensor(g["t2"][:, :], g["t1"][:, :],
                                                  g["w"][:, :], ALU.subtract))
            pad_v(3)
            E("vector", lambda q: q.tensor_tensor(g3(g["t2"]), g3(g["t2"]),
                                                  gcE(gc_["b"]), ALU.mult))
            pad_v(4)
            E("vector", lambda q: q.tensor_tensor(g["w"][:, :], g["w"][:, :],
                                                  g["t2"][:, :], ALU.add))
            pad_v(4)
            E("vector", lambda q: q.tensor_reduce(gro(gc_["a"]), g3(g["w"]),
                                                  AX.X, ALU.add))
            pad_v(4)
            E("vector", lambda q: q.tensor_scalar(gc_["c"][:, :], gc_["a"][:, :],
                                                  1e-8, None, ALU.max))
            pad_v(4)
            E("vector", lambda q: q.reciprocal(gc_["d"][:, :], gc_["c"][:, :]))
            pad_v(4)
            E("vector", lambda q: q.tensor_tensor(g3(g["w"]), g3(g["w"]),
                                                  gcE(gc_["d"]), ALU.mult), inc="sv")
            for tc in range(TC):
                Wt("tensor", "sv")
                E("tensor", lambda q, tc=tc: q.transpose(
                    PS1[0:8, 1536:1664], g["w"][:, tc * 8:(tc + 1) * 8],
                    ident[:, :]), inc="st")
                Wt("vector", "st")
                E("vector", lambda q, tc=tc: q.tensor_copy(
                    qbuf[0:8, tc * 128:(tc + 1) * 128], PS1[0:8, 1536:1664]), inc="sv")
            if DEBUG:
                Wt("gpsimd", "sv")
                Wt("gpsimd", "lg")
                E("gpsimd", lambda q: q.dma_start(out=GW_dram, in_=qbuf[0:8, :]),
                  inc="lg", add=16)
            for e in range(NE):
                Wt("gpsimd", "sv")
                Wt("gpsimd", "ss")
                b.dma("gpsimd", "lg", wrow_dram, qbuf[e:e + 1, :])
                Wt("gpsimd", "lg")
                b.dma("gpsimd", "lg", wrep[:, :], row_dram.partition_broadcast(128)
                      if False else wrow_dram.partition_broadcast(128))
                lg_w = b.cnt["lg"]
                for fc in range(DC):
                    Wt("sync", "st")
                    b.dma("sync", "ld", kvw[:, 0:2048],
                          wcol(bass.AP(Wg.tensor, e * D * FF, [[1, 1]]), fc, 128, FF))
                    b.dma("sync", "ld", kvw[:, 2048:4096],
                          wcol(bass.AP(Wu.tensor, e * D * FF, [[1, 1]]), fc, 128, FF))
                    Wt("tensor", "ld")
                    Wt("tensor", "ss")
                    Wt("tensor", "sv")
                    for dc in range(DC):
                        for hf in range(2):
                            E("tensor", lambda q, dc=dc, hf=hf: q.matmul(
                                PS0[:, hf * 512:(hf + 1) * 512],
                                kvw[:, dc * 128:(dc + 1) * 128],
                                big_b[:, dc * 1024 + hf * 512: dc * 1024 + (hf + 1) * 512],
                                start=(dc == 0), stop=(dc == DC - 1)))
                    for dc in range(DC):
                        for hf in range(2):
                            E("tensor", lambda q, dc=dc, hf=hf: q.matmul(
                                PS0[:, 1024 + hf * 512: 1024 + (hf + 1) * 512],
                                kvw[:, 2048 + dc * 128: 2048 + (dc + 1) * 128],
                                big_b[:, dc * 1024 + hf * 512: dc * 1024 + (hf + 1) * 512],
                                start=(dc == 0), stop=(dc == DC - 1)),
                              inc=("st" if dc == DC - 1 and hf == 1 else None))
                    Wt("scalar", "st")
                    E("scalar", lambda q: q.activation(ktmp[:, :], PS0[:, 0:1024],
                                                       AF.Silu), inc="ss")
                    Wt("vector", "ss")
                    Wt("vector", "lg", lg_w)
                    E("vector", lambda q: q.tensor_tensor(rtmp[:, :], ktmp[:, :],
                                                          PS0[:, 1024:2048], ALU.mult))
                    E("vector", lambda q, fc=fc: q.tensor_tensor(
                        big_a[:, fc * 1024:(fc + 1) * 1024], rtmp[:, :], wrep[:, :],
                        ALU.mult), inc="sv")
                for do in range(DC):
                    Wt("sync", "st")
                    b.dma("sync", "ld", kvw[:, 0:2048],
                          wcol(bass.AP(Wd.tensor, e * FF * D, [[1, 1]]), do, 128, D))
                    Wt("tensor", "ld")
                    Wt("tensor", "sv")
                    ph = 1024 * (do % 2)
                    Wt("tensor", "ss")
                    for fc in range(DC):
                        for hf in range(2):
                            E("tensor", lambda q, fc=fc, hf=hf, ph=ph: q.matmul(
                                PS1[:, ph + hf * 512: ph + (hf + 1) * 512],
                                kvw[:, fc * 128:(fc + 1) * 128],
                                big_a[:, fc * 1024 + hf * 512: fc * 1024 + (hf + 1) * 512],
                                start=(fc == 0), stop=(fc == DC - 1)),
                              inc=("st" if fc == DC - 1 and hf == 1 else None))
                    Wt("scalar", "st")
                    Wt("scalar", "lg")
                    bo_ = 1024 * (do % 2)
                    E("scalar", lambda q, ph=ph, bo_=bo_: q.copy(
                        scr2k[:, bo_:bo_ + 1024], PS1[:, ph:ph + 1024]), inc="ss")
                    Wt("gpsimd", "ss")
                    kwacc = {} if e == 0 else dict(accum_op=ALU.add)
                    Wt("gpsimd", "lg")
                    E("gpsimd", lambda q, do=do, bo_=bo_, kw=kwacc: q.dma_start(
                        out=bass.AP(FFN_dram.tensor, do * 128 * TOK,
                                    [[TOK, 128], [1, 1024]]),
                        in_=scr2k[:, bo_:bo_ + 1024], **kw), inc="lg", add=16)

        if PHASES >= 6:
            Wt("sync", "lg")
            for tc in range(TC):
                Wt("sync", "st")
                Wt("sync", "sv")
                b.dma("sync", "ld", kvw[:, 0:2048],
                      bass.AP(FFN_dram.tensor, tc * 128,
                              [[TOK, 128], [128 * TOK, DC], [1, 128]]))
                b.dma("sync", "ld", kvw[:, 2048:4096],
                      bass.AP(XP_dram.tensor, tc * 128 * D, [[D, 128], [1, D]]))
                Wt("tensor", "ld")
                Wt("tensor", "sv")
                ps = PS0 if tc % 2 == 0 else PS1
                for dc in range(DC):
                    E("tensor", lambda q, dc=dc, ps=ps: q.transpose(
                        ps[:, dc * 128:(dc + 1) * 128],
                        kvw[:, dc * 128:(dc + 1) * 128], ident[:, :]),
                      inc=("st" if dc == DC - 1 else None))
                Wt("vector", "st")
                Wt("vector", "out")
                ao = 2048 * (tc % 2)
                E("vector", lambda q, ps=ps, ao=ao: q.tensor_tensor(
                    big_a[:, ao:ao + 2048], kvw[:, 2048:4096], ps[:, 0:2048], ALU.add),
                  inc="sv")
                Wt("sync", "sv")
                b.dma("sync", "out", out_own[tc * 128:(tc + 1) * 128, :],
                      big_a[:, ao:ao + 2048])

        Wt("sync", "out")
        Wt("sync", "lg")
        Wt("gpsimd", "lg")
        Wt("gpsimd", "sg")

        with nc.Block() as block:
            for eng in ENGS:
                def mk_run(eng):
                    def run(q):
                        for fn in b.prog[eng]:
                            fn(q)
                    return run
                getattr(block, eng)(mk_run(eng))

    return nc


_cached = {}


def _rope_tables():
    import jax
    import jax.numpy as jnp
    cpu = jax.devices("cpu")[0]
    with jax.default_device(cpu):
        j = jnp.arange(HD // 2, dtype=jnp.float32)
        inv_freq = jnp.power(jnp.float32(10000.0), -2.0 * j / HD)
        pos = jnp.arange(S, dtype=jnp.float32)
        ang = pos[:, None] * inv_freq[None, :]
        sin, cos = jnp.sin(ang), jnp.cos(ang)
        sin2 = jnp.stack([sin, sin], axis=-1).reshape(S, HD)
        cos2 = jnp.stack([cos, cos], axis=-1).reshape(S, HD)
    s = np.ascontiguousarray(np.asarray(sin2).T)
    c = np.ascontiguousarray(np.asarray(cos2).T)
    return s, c


def kernel(L0_attn_hidden_state, L0_attn_Wq, L0_attn_bq, L0_attn_Wk, L0_attn_bk,
           L0_attn_Wv, L0_attn_bv, L0_attn_Wo, L0_attn_bo, L0_ffn_Wg, L0_ffn_Wu,
           L0_ffn_Wd, L0_ffn_Wr, L0_ffn_temperature):
    x = np.ascontiguousarray(np.asarray(L0_attn_hidden_state, dtype=np.float32)
                             .reshape(S, D))
    if "nc" not in _cached:
        _cached["nc"] = build_program()
    nc = _cached["nc"]
    common = {
        "x_full": x,
        "Wq": np.ascontiguousarray(L0_attn_Wq, dtype=np.float32),
        "Wk": np.ascontiguousarray(L0_attn_Wk, dtype=np.float32),
        "Wv": np.ascontiguousarray(L0_attn_Wv, dtype=np.float32),
        "Wo": np.ascontiguousarray(L0_attn_Wo, dtype=np.float32),
        "bq": np.ascontiguousarray(L0_attn_bq, dtype=np.float32).reshape(1, D),
        "bk": np.ascontiguousarray(L0_attn_bk, dtype=np.float32).reshape(1, D),
        "bv": np.ascontiguousarray(L0_attn_bv, dtype=np.float32).reshape(1, D),
        "bo": np.ascontiguousarray(L0_attn_bo, dtype=np.float32).reshape(1, D),
        "Wg": np.ascontiguousarray(L0_ffn_Wg, dtype=np.float32),
        "Wu": np.ascontiguousarray(L0_ffn_Wu, dtype=np.float32),
        "Wd": np.ascontiguousarray(L0_ffn_Wd, dtype=np.float32),
        "Wr": np.ascontiguousarray(L0_ffn_Wr, dtype=np.float32),
        "temp": np.ascontiguousarray(L0_ffn_temperature,
                                     dtype=np.float32).reshape(1, 1),
    }
    if "tab" not in _cached:
        _cached["tab"] = _rope_tables()
    sin_t, cos_t = _cached["tab"]
    common["sin_t"] = sin_t
    common["cos_t"] = cos_t
    in_maps = []
    for c in range(8):
        m = dict(common)
        m["x_own"] = np.ascontiguousarray(x[c * TOK:(c + 1) * TOK])
        m["core_ofs"] = np.asarray([[float(c * TOK)]], dtype=np.float32)
        m["sin_o"] = np.ascontiguousarray(sin_t[:, c * TOK:(c + 1) * TOK])
        m["cos_o"] = np.ascontiguousarray(cos_t[:, c * TOK:(c + 1) * TOK])
        in_maps.append(m)
    res = run_bass_kernel_spmd(nc, in_maps, core_ids=list(range(8)))
    out = np.concatenate([r["out_own"] for r in res.results], axis=0)
    return out.reshape(1, S, D)
